# version 04
# baseline (speedup 1.0000x reference)
import numpy as np
import concourse.bass as bass
import concourse.mybir as mybir
from concourse.bass_utils import run_bass_kernel_spmd
from contextlib import ExitStack

F32 = mybir.dt.float32
BF16 = mybir.dt.bfloat16
FP8 = mybir.dt.float8e4
I32 = mybir.dt.int32
AF = mybir.ActivationFunctionType
ALU = mybir.AluOpType
AX = mybir.AxisListType

NCORES = 8
T = 2048
D = 2048
QKV = 1536
HD = 512
FF = 5632
NTC = 16
NTAU = 4
NDC = 16
NJC = 44
NG = 11
HALO = 16
CS = 2064

ENGINES = ("sync", "scalar", "vector", "tensor", "gpsimd")

DEBUG_SIM = True
SKIP_HALO = False
DEBUG_DUMP = False
PHASE_LIMIT = 5


class Prog:

    def __init__(self):
        self.items = {e: [] for e in ENGINES}
        self.count = {}
        for e in ENGINES:
            self.count[e] = 0
        self.chans = []
        self.bound = {e: {} for e in ENGINES}

    def chan(self, name):
        if name not in self.count:
            self.count[name] = 0
            self.chans.append(name)
        return name

    def op(self, eng, fn, dma=None, inc=None):
        if dma is not None:
            sem = self.chan(dma)
            amt = 16
        else:
            sem = eng
            amt = 1 if inc is None else inc
            if DEBUG_SIM:
                amt = max(amt, 1)
                if self.count[eng] > 0:
                    self.items[eng].append(("wait", eng, self.count[eng]))
        self.items[eng].append(("op", fn, sem, amt))
        self.count[sem] += amt
        return (sem, self.count[sem]) if amt else None

    def wait(self, eng, tok):
        if tok is None:
            return
        sem, val = tok
        if sem == eng or self.bound[eng].get(sem, 0) >= val:
            return
        self.items[eng].append(("wait", sem, val))
        self.bound[eng][sem] = val

    def waits(self, eng, toks):
        for t in toks:
            self.wait(eng, t)

    def barrier(self):
        snap = dict(self.count)
        for e in ENGINES:
            for s, v in snap.items():
                if s != e and v > 0:
                    self.wait(e, (s, v))

    def token(self, sem):
        return (sem, self.count[sem]) if self.count.get(sem, 0) else None

    def emit(self, block, sems):
        def mk(ename):
            def emitter(eng):
                for it in self.items[ename]:
                    if it[0] == "wait":
                        eng.wait_ge(sems[it[1]], it[2])
                    else:
                        ins = it[1](eng)
                        if it[3]:
                            ins.then_inc(sems[it[2]], it[3])
            return emitter
        block.sync(mk("sync"))
        block.scalar(mk("scalar"))
        block.vector(mk("vector"))
        block.tensor(mk("tensor"))
        block.gpsimd(mk("gpsimd"))


def build_program():
    nc = bass.Bass("TRN2", target_bir_lowering=False, debug=False,
                   detect_race_conditions=DEBUG_SIM)
    h = {}

    def inp(name, shape, dt=F32):
        h[name] = nc.dram_tensor(name, shape, dt, kind="ExternalInput").ap()

    inp("x", [T, D]); inp("xh", [HALO, D])
    inp("cw", [3, D, QKV]); inp("cb2", [128, 12])
    for nm in ("wq2", "wk2", "wv2", "wo2"):
        inp(nm, [HD, HD])
    for nm in ("bq2", "bk2", "bv2", "bo2"):
        inp(nm, [128, 4])
    inp("pw", [HD, D]); inp("pbr", [1, D])
    inp("wg", [FF, D]); inp("wu", [FF, D]); inp("wd", [D, FF])
    inp("wgT", [D, FF]); inp("wuT", [D, FF]); inp("wdT", [FF, D])
    inp("didx", [FF], I32)
    inp("eyef", [128, 128]); inp("onesf", [1, 128]); inp("cmask", [128, 2, 256])
    h["out"] = nc.dram_tensor("out", [T, D], F32, kind="ExternalOutput").ap()
    if DEBUG_DUMP:
        h["dbg_s"] = nc.dram_tensor("dbg_s", [T, D], BF16, kind="ExternalOutput").ap()
        h["dbg_wg"] = nc.dram_tensor("dbg_wg", [D, FF], BF16, kind="ExternalOutput").ap()
        h["dbg_wd"] = nc.dram_tensor("dbg_wd", [FF, D], BF16, kind="ExternalOutput").ap()
        h["dbg_qx"] = nc.dram_tensor("dbg_qx", [128, NDC, T], BF16, kind="ExternalOutput").ap()

    xn_scr = nc.dram_tensor("xn_scr", [NDC, 128, CS], BF16, kind="Internal").ap()
    s_scr = nc.dram_tensor("s_scr", [T, D], BF16, kind="Internal").ap()
    wgq_scr = nc.dram_tensor("wgq_scr", [D, FF], BF16, kind="Internal").ap()
    wuq_scr = nc.dram_tensor("wuq_scr", [D, FF], BF16, kind="Internal").ap()
    wdq_scr = nc.dram_tensor("wdq_scr", [FF, D], BF16, kind="Internal").ap()

    p = Prog()
    es = ExitStack()
    with es:
        sems = {e: es.enter_context(nc.semaphore("sem_" + e)) for e in ENGINES}
        def finish():
            for cname in p.chans:
                sems[cname] = es.enter_context(nc.semaphore("sem_" + cname))
        sb = lambda name, shape, dt: es.enter_context(nc.sbuf_tensor(name, shape, dt))

        eyef = sb("c_eyef", [128, 128], F32)
        eyeb = sb("c_eyeb", [128, 128], BF16)
        onesf = sb("c_onesf", [1, 128], F32)
        cmask = sb("c_cmask", [128, 2, 256], F32)
        cb2 = sb("c_cb2", [128, 12], F32)
        bq8 = sb("c_bq8", [128, 4], F32)
        bk2 = sb("c_bk2", [128, 4], F32)
        bv2 = sb("c_bv2", [128, 4], F32)
        bo2 = sb("c_bo2", [128, 4], F32)
        wsg = sb("c_wsg", [128, NJC], F32)
        wsu = sb("c_wsu", [128, NJC], F32)
        pbr_sb = sb("pbr_sb", [1, D], F32)
        epsc = sb("c_eps", [128, 1], F32)

        tl = {}
        for nm, dst in (("eyef", eyef), ("onesf", onesf), ("cmask", cmask),
                        ("cb2", cb2), ("bk2", bk2), ("bv2", bv2), ("bo2", bo2),
                        ("pbr", pbr_sb)):
            tl[nm] = p.op("sync", lambda e, d=dst, a=h[nm]: e.dma_start(out=d[:], in_=a[:]), dma="ld_" + nm)
        tl["bq2"] = p.op("sync", lambda e: e.dma_start(out=bq8[:], in_=h["bq2"][:]), dma="ld_bq2")
        p.wait("vector", tl["bq2"])
        p.op("vector", lambda e: e.tensor_scalar_mul(out=bq8[:], in0=bq8[:], scalar1=0.125))
        t_bq8 = p.token("vector")
        p.wait("scalar", tl["eyef"])
        t_eyeb = p.op("scalar", lambda e: e.copy(out=eyeb[:], in_=eyef[:]))
        t_eps = p.op("vector", lambda e: e.memset(epsc[:], 1e-5))

        with ExitStack() as ew:
          if PHASE_LIMIT >= 1:
              sbw = lambda n, s, d: ew.enter_context(nc.sbuf_tensor(n, s, d))
              psW = [ew.enter_context(nc.psum_tensor(f"psW{i}", [128, 512], F32)) for i in range(2)]
              idx_s = sbw("idx_s", [128, NJC], I32)
              qrow_g = sbw("qrow_g", [1, FF], F32)
              qrow_u = sbw("qrow_u", [1, FF], F32)
              qrow_d = sbw("qrow_d", [1, D], F32)
              wrow_d = sbw("wrow_d", [1, D], F32)
              wnat = [sbw(f"wnat{i}", [128, FF], F32) for i in range(2)]
              rcol = sbw("rcol", [128, 4], F32)
              wTs = [sbw(f"wTs{i}", [128, 512], F32) for i in range(2)]
              q8t = sbw("q8t", [128, 512], FP8)
              qbf = [sbw(f"qbf{i}", [128, 512], BF16) for i in range(2)]
              qsb = sbw("qsb", [128, 512], F32)
              qsdb = [sbw(f"qsdb{i}", [128, 512], F32) for i in range(4)]
              wsdb = [sbw(f"wsdb{i}", [128, 512], F32) for i in range(4)]
              wdqt = [sbw(f"wdqt{i}", [128, D], BF16) for i in range(2)]

              t_idx = p.op("sync", lambda e: e.dma_start(
                  out=idx_s[:], in_=h["didx"].rearrange("(c p) -> p c", p=128),
                  allow_slow_non_contiguous=True), dma="ld_idx")

              prev_wnat = [None, None]

              def w_maxes(src, n_chunks, row_len, qrow, wscol, wrow):
                  prev = prev_wnat
                  t_rcol_free = None
                  for c in range(n_chunks):
                      buf = wnat[c % 2]
                      p.wait("sync", prev[c % 2])
                      tld = p.op("sync", lambda e, b=buf, s=src, c=c, L=row_len: e.dma_start(
                          out=b[:, 0:L], in_=s[c * 128:(c + 1) * 128, :]), dma=f"wn{c % 2}")
                      p.waits("vector", [tld, t_rcol_free])
                      p.op("vector", lambda e, b=buf, L=row_len: e.reduce_max(
                          out=rcol[:, 0:1], in_=b[:, 0:L], axis=AX.X, apply_absolute_value=True))
                      prev[c % 2] = p.token("vector")
                      p.op("vector", lambda e: e.reciprocal(out=rcol[:, 1:2], in_=rcol[:, 0:1]))
                      p.op("vector", lambda e: e.tensor_scalar_mul(out=rcol[:, 2:3], in0=rcol[:, 1:2], scalar1=224.0))
                      if wscol is not None:
                          p.op("vector", lambda e, c=c: e.tensor_scalar_mul(
                              out=wscol[:, c:c + 1], in0=rcol[:, 0:1], scalar1=1.0 / 224.0))
                      else:
                          p.op("vector", lambda e: e.tensor_scalar_mul(
                              out=rcol[:, 3:4], in0=rcol[:, 0:1], scalar1=1.0 / 224.0))
                      tv = p.token("vector")
                      p.waits("tensor", [tv, tl["eyef"]])
                      tt = p.op("tensor", lambda e: e.transpose(
                          out=psW[0][0:1, 0:128], in_=rcol[:, 2:3], identity=eyef[:]))
                      t_rcol_free = tt
                      p.wait("vector", tt)
                      p.op("vector", lambda e, q=qrow, c=c: e.tensor_copy(
                          out=q[:, c * 128:(c + 1) * 128], in_=psW[0][0:1, 0:128]))
                      tpe = p.token("vector")
                      p.wait("tensor", tpe)
                      if wscol is None:
                          tt2 = p.op("tensor", lambda e: e.transpose(
                              out=psW[0][0:1, 0:128], in_=rcol[:, 3:4], identity=eyef[:]))
                          t_rcol_free = tt2
                          p.wait("vector", tt2)
                          p.op("vector", lambda e, w=wrow, c=c: e.tensor_copy(
                              out=w[:, c * 128:(c + 1) * 128], in_=psW[0][0:1, 0:128]))
                          tpe = p.token("vector")
                          p.wait("tensor", tpe)
                  return p.token("vector")

              t_mg = w_maxes(h["wg"], NJC, D, qrow_g, wsg, None)
              t_mu = w_maxes(h["wu"], NJC, D, qrow_u, wsu, None)
              t_md = w_maxes(h["wd"], NDC, FF, qrow_d, None, wrow_d)

              wq_state = {"prev8": [None, None], "prev_st": [None, None], "t_q8_free": None}

              def w_quant(srcT, qrow, dst_scr, qdep):
                  prev8 = wq_state["prev8"]
                  prev_st = wq_state["prev_st"]
                  t_q8_free = wq_state["t_q8_free"]
                  for G in range(NG):
                      p.waits("tensor", [qdep, tl["onesf"]])
                      tb = p.op("tensor", lambda e, G=G, q=qrow: e.matmul(
                          out=psW[1][:], lhsT=onesf[:], rhs=q[:, G * 512:(G + 1) * 512],
                          start=True, stop=True))
                      p.wait("vector", tb)
                      tq = p.op("vector", lambda e: e.tensor_copy(out=qsb[:], in_=psW[1][:]))
                      p.wait("tensor", tq)
                      for d in range(NDC):
                          buf = wTs[d % 2]
                          p.wait("sync", prev8[d % 2])
                          tld = p.op("sync", lambda e, b=buf, s=srcT, d=d, G=G: e.dma_start(
                              out=b[:], in_=s[d * 128:(d + 1) * 128, G * 512:(G + 1) * 512]), dma=f"wt{d % 2}")
                          p.waits("vector", [tld, t_q8_free])
                          t8 = p.op("vector", lambda e, b=buf: e.tensor_tensor(
                              out=q8t[:], in0=b[:], in1=qsb[:], op=ALU.mult))
                          prev8[d % 2] = t8
                          ob = qbf[d % 2]
                          p.waits("scalar", [t8, prev_st[d % 2]])
                          tc = p.op("scalar", lambda e, o=ob: e.copy(out=o[:], in_=q8t[:]))
                          t_q8_free = tc
                          wq_state["t_q8_free"] = tc
                          p.wait("sync", tc)
                          ts = p.op("sync", lambda e, o=ob, ds=dst_scr, d=d, G=G: e.dma_start(
                              out=ds[d * 128:(d + 1) * 128, G * 512:(G + 1) * 512], in_=o[:]), dma=f"wqs{d % 2}")
                          prev_st[d % 2] = ts
                  return p.token("sync")

              w_quant(h["wgT"], qrow_g, wgq_scr, t_mg)
              w_quant(h["wuT"], qrow_u, wuq_scr, t_mu)

              t_b4 = None
              for hg in range(4):
                  for src_row, dstl in ((qrow_d, qsdb), (wrow_d, wsdb)):
                      p.waits("tensor", [t_md, tl["onesf"], t_b4])
                      tb = p.op("tensor", lambda e, hg=hg, r=src_row: e.matmul(
                          out=psW[1][:], lhsT=onesf[:], rhs=r[:, hg * 512:(hg + 1) * 512],
                          start=True, stop=True))
                      p.wait("vector", tb)
                      t_b4 = p.op("vector", lambda e, hg=hg, dl=dstl: e.tensor_copy(out=dl[hg][:], in_=psW[1][:]))
                      p.wait("tensor", t_b4)

              prev_ld = prev_wnat
              prev_sc = [None, None]
              for c in range(NJC):
                  buf = wnat[c % 2]
                  p.wait("sync", prev_ld[c % 2])
                  tld = p.op("sync", lambda e, b=buf, c=c: e.dma_start(
                      out=b[:, 0:D], in_=h["wdT"][c * 128:(c + 1) * 128, :]), dma=f"wn{c % 2}")
                  ob = wdqt[c % 2]
                  p.waits("vector", [tld, t_b4, prev_sc[c % 2]])
                  for hg in range(4):
                      p.op("vector", lambda e, b=buf, hg=hg: e.tensor_tensor(
                          out=q8t[:], in0=b[:, hg * 512:(hg + 1) * 512], in1=qsdb[hg][:], op=ALU.mult))
                      p.op("vector", lambda e, o=ob, hg=hg: e.tensor_tensor(
                          out=o[:, hg * 512:(hg + 1) * 512], in0=q8t[:], in1=wsdb[hg][:], op=ALU.mult))
                  tf = p.token("vector")
                  prev_ld[c % 2] = tf
                  p.waits("gpsimd", [tf, t_idx, prev_sc[c % 2]])
                  tsc = p.op("gpsimd", lambda e, o=ob, c=c: e.indirect_dma_start(
                      out=wdq_scr[:, :],
                      out_offset=bass.IndirectOffsetOnAxis(ap=idx_s[:, c:c + 1], axis=0),
                      in_=o[:], in_offset=None), dma=f"sca{c % 2}")
                  prev_sc[c % 2] = tsc

        p.barrier()

        qx_fm = sb("qx_fm", [128, NDC, T], BF16)

        with ExitStack() as e0:
          if PHASE_LIMIT >= 2:
              sb0 = lambda n, s, d: e0.enter_context(nc.sbuf_tensor(n, s, d))
              psb = [e0.enter_context(nc.psum_tensor(f"ps0b{i}", [128, 512], BF16)) for i in range(2)]
              xt = [sb0(f"xt{i}", [128, D], F32) for i in range(2)]
              xnf = sb0("xnf", [128, D], F32)
              junk = sb0("junk", [128, D], BF16)
              xnb = sb0("xnb", [128, D], BF16)
              q8x = sb0("q8x", [128, D], FP8)
              qxb = sb0("qxb", [128, D], BF16)
              cols = sb0("cols", [128, 8], F32)
              stg = sb0("stg", [128, NDC, 128], BF16)

              guard_xt = [[], []]
              g_xnf = []
              g_q8x = []
              g_qxb = []
              g_xnb = []
              g_stg = None
              bank_tok = [None, None]

              for tc in range(NTC if SKIP_HALO else NTC + 1):
                  halo = (tc == NTC)
                  P = HALO if halo else 128
                  buf = xt[tc % 2]
                  src = h["xh"][:] if halo else h["x"][tc * 128:(tc + 1) * 128, :]
                  p.waits("sync", guard_xt[tc % 2])
                  tld = p.op("sync", lambda e, b=buf, s=src, P=P: e.dma_start(out=b[0:P, :], in_=s), dma=f"xt{tc % 2}")
                  p.waits("vector", [tld] + g_xnf)
                  p.op("vector", lambda e, b=buf, P=P: e.reduce_sum(out=cols[0:P, 0:1], in_=b[0:P, :], axis=AX.X))
                  p.wait("scalar", tld)
                  tsq = p.op("scalar", lambda e, b=buf, P=P: e.activation(
                      out=junk[0:P, :], in_=b[0:P, :], func=AF.Square, accum_out=cols[0:P, 1:2]))
                  p.wait("vector", tsq)
                  p.op("vector", lambda e, P=P: e.tensor_scalar_mul(out=cols[0:P, 2:3], in0=cols[0:P, 0:1], scalar1=1.0 / D))
                  p.op("vector", lambda e, P=P: e.tensor_scalar_mul(out=cols[0:P, 3:4], in0=cols[0:P, 1:2], scalar1=1.0 / D))
                  p.op("vector", lambda e, P=P: e.tensor_tensor(out=cols[0:P, 4:5], in0=cols[0:P, 2:3], in1=cols[0:P, 2:3], op=ALU.mult))
                  tv = p.op("vector", lambda e, P=P: e.tensor_tensor(out=cols[0:P, 4:5], in0=cols[0:P, 3:4], in1=cols[0:P, 4:5], op=ALU.subtract))
                  p.waits("scalar", [tv, t_eps])
                  tsd = p.op("scalar", lambda e, P=P: e.activation(out=cols[0:P, 5:6], in_=cols[0:P, 4:5], func=AF.Sqrt, bias=epsc[0:P, 0:1]))
                  p.wait("vector", tsd)
                  p.op("vector", lambda e, P=P: e.reciprocal(out=cols[0:P, 6:7], in_=cols[0:P, 5:6]))
                  txn = p.op("vector", lambda e, b=buf, P=P: e.tensor_scalar(
                      out=xnf[0:P, :], in0=b[0:P, :], scalar1=cols[0:P, 2:3], scalar2=cols[0:P, 6:7],
                      op0=ALU.subtract, op1=ALU.mult))
                  guard_xt[tc % 2] = [txn, tsq]
                  p.wait("scalar", txn)
                  txb = p.op("scalar", lambda e, P=P: e.copy(out=xnb[0:P, :], in_=xnf[0:P, :]))
                  g_xnf = [txb]

                  if not halo:
                      p.op("vector", lambda e: e.reduce_max(out=cols[:, 0:1], in_=xnf[:], axis=AX.X, apply_absolute_value=True))
                      p.op("vector", lambda e: e.reciprocal(out=cols[:, 1:2], in_=cols[:, 0:1]))
                      tqs = p.op("vector", lambda e: e.tensor_scalar_mul(out=cols[:, 2:3], in0=cols[:, 1:2], scalar1=224.0))
                      tsx = p.op("vector", lambda e: e.tensor_scalar_mul(out=cols[:, 3:4], in0=cols[:, 0:1], scalar1=1.0 / 224.0))
                      p.waits("scalar", [tqs] + g_q8x)
                      t8 = p.op("scalar", lambda e: e.activation(out=q8x[:], in_=xnf[:], func=AF.Copy, scale=cols[:, 2:3]))
                      g_xnf = [t8]
                      p.waits("vector", [t8, tsx] + g_qxb)
                      tqx = p.op("vector", lambda e: e.tensor_scalar_mul(out=qxb[:], in0=q8x[:], scalar1=cols[:, 3:4]))
                      g_q8x = [tqx]

                  ev = []
                  for d in range(NDC):
                      bank = psb[d % 2]
                      p.waits("tensor", [txb, t_eyeb, bank_tok[d % 2], g_stg])
                      tt = p.op("tensor", lambda e, d=d, b=bank, P=P: e.transpose(
                          out=b[:, 0:P], in_=xnb[0:P, d * 128:(d + 1) * 128], identity=eyeb[0:P, 0:P]))
                      if d == NDC - 1:
                          g_xnb = [tt]
                      p.wait("scalar", tt)
                      te = p.op("scalar", lambda e, d=d, b=bank, P=P: e.copy(out=stg[:, d, 0:P], in_=b[:, 0:P]))
                      ev.append(te)
                      bank_tok[d % 2] = te
                  for d in range(NDC):
                      p.wait("sync", ev[d])
                      cst = 0 if halo else HALO + tc * 128
                      p.op("sync", lambda e, d=d, c=cst, P=P: e.dma_start(
                          out=xn_scr[d, :, c:c + P], in_=stg[:, d, 0:P]), dma="stg")
                  g_stg = p.token("stg")

                  if not halo:
                      ev2 = []
                      for d in range(NDC):
                          bank = psb[d % 2]
                          p.waits("tensor", [tqx, bank_tok[d % 2]])
                          tt = p.op("tensor", lambda e, d=d, b=bank: e.transpose(
                              out=b[:, 128:256], in_=qxb[:, d * 128:(d + 1) * 128], identity=eyeb[:]))
                          if d == NDC - 1:
                              g_qxb = [tt]
                          p.wait("vector", tt)
                          te = p.op("vector", lambda e, d=d, tc=tc, b=bank: e.tensor_copy(
                              out=qx_fm[:, d, tc * 128:(tc + 1) * 128], in_=b[:, 128:256]))
                          ev2.append(te)
                          bank_tok[d % 2] = te

        p.barrier()

        qkv_fm = sb("qkv_fm", [128, 12, T], BF16)
        with ExitStack() as e1:
          if PHASE_LIMIT >= 3:
              sb1 = lambda n, s, d: e1.enter_context(nc.sbuf_tensor(n, s, d))
              psA = [e1.enter_context(nc.psum_tensor(f"psA{i}", [128, 512], F32)) for i in range(8)]
              xslab = [sb1(f"xslab{i}", [128, CS], BF16) for i in range(2)]
              cwf = [sb1(f"cwf{i}", [128, 256], F32) for i in range(2)]
              cwb = [sb1(f"cwb{i}", [128, 256], BF16) for i in range(2)]

              prev_ev = None
              prev_cw = [None, None]
              prev_cwf = [None, None]
              prev_x = [None, None]
              for ccg in range(6):
                  step = 0
                  last_mm = None
                  for d in range(NDC):
                      bx = xslab[d % 2]
                      p.wait("sync", prev_x[d % 2])
                      tlx = p.op("sync", lambda e, b=bx, d=d: e.dma_start(out=b[:], in_=xn_scr[d, :, :]), dma=f"xs{d % 2}")
                      for w in range(3):
                          bcf = cwf[step % 2]
                          bcb = cwb[step % 2]
                          p.wait("sync", prev_cwf[step % 2])
                          tlw = p.op("sync", lambda e, b=bcf, w=w, d=d, g=ccg: e.dma_start(
                              out=b[:], in_=h["cw"][w, d * 128:(d + 1) * 128, g * 256:(g + 1) * 256]), dma=f"cw{step % 2}")
                          p.waits("scalar", [tlw, prev_cw[step % 2]])
                          tcw = p.op("scalar", lambda e, bf=bcf, bb=bcb: e.copy(out=bb[:], in_=bf[:]))
                          prev_cwf[step % 2] = tcw
                          p.waits("tensor", [tcw, tlx, prev_ev])
                          first = (d == 0 and w == 0)
                          lastg = (d == NDC - 1 and w == 2)
                          for cc2 in range(2):
                              for tau in range(NTAU):
                                  bk = psA[cc2 * 4 + tau]
                                  isl = (cc2 == 1 and tau == NTAU - 1)
                                  tok = p.op("tensor", lambda e, bk=bk, bb=bcb, bx=bx, cc2=cc2, tau=tau, w=w, f=first, L=lastg:
                                             e.matmul(out=bk[:], lhsT=bb[:, cc2 * 128:(cc2 + 1) * 128],
                                                      rhs=bx[:, HALO + tau * 512 + (w - 2): HALO + tau * 512 + (w - 2) + 512],
                                                      start=f, stop=L),
                                             inc=(1 if isl else 0))
                                  if isl:
                                      last_mm = tok
                          prev_cw[step % 2] = last_mm
                          step += 1
                      prev_x[d % 2] = last_mm
                  p.waits("scalar", [last_mm, tl["cb2"]])
                  for cc2 in range(2):
                      for tau in range(NTAU):
                          bk = psA[cc2 * 4 + tau]
                          cc = ccg * 2 + cc2
                          p.op("scalar", lambda e, bk=bk, cc=cc, tau=tau: e.activation(
                              out=qkv_fm[:, cc, tau * 512:(tau + 1) * 512], in_=bk[:],
                              func=AF.Identity, bias=cb2[:, cc:cc + 1]))
                  prev_ev = p.token("scalar")

        p.barrier()

        with ExitStack() as e2:
          if PHASE_LIMIT >= 4:
              sb2 = lambda n, s, d: e2.enter_context(nc.sbuf_tensor(n, s, d))
              psf = [e2.enter_context(nc.psum_tensor(f"psf{i}", [128, 512], F32)) for i in range(6)]
              psT = [e2.enter_context(nc.psum_tensor(f"psT{i}", [128, 512], BF16)) for i in range(2)]
              wqb = sb2("wqb", [128, 4, 512], BF16)
              wkb = sb2("wkb", [128, 4, 512], BF16)
              wvb = sb2("wvb", [128, 4, 512], BF16)
              wob = sb2("wob", [128, 4, 512], BF16)
              pwb = sb2("pwb", [128, 4, D], BF16)
              pbb = sb2("pbb", [128, 4, 512], F32)
              wtmp = sb2("wtmp", [128, D], F32)
              qh = sb2("qh", [128, 4, 512], BF16)
              kh = sb2("kh", [128, 4, 512], BF16)
              vh = sb2("vh", [128, 4, 512], BF16)
              probs = sb2("probs", [128, 256], BF16)
              probsn = sb2("probsn", [128, 256], BF16)
              pT = sb2("pT", [128, 3, 128], BF16)
              ctxf = sb2("ctxf", [128, 4, 256], BF16)
              mha = sb2("mha", [128, 4, 256], BF16)
              sstg = sb2("sstg", [128, 4, 512], BF16)
              scols = sb2("scols", [128, 2], F32)

              t_wcast = None
              for nm, dst in (("wq2", wqb), ("wk2", wkb), ("wv2", wvb), ("wo2", wob)):
                  for c in range(4):
                      p.wait("sync", t_wcast)
                      tld = p.op("sync", lambda e, a=h[nm], c=c: e.dma_start(
                          out=wtmp[:, 0:512], in_=a[c * 128:(c + 1) * 128, :]), dma="wtm")
                      p.waits("scalar", [tld, t_wcast])
                      t_wcast = p.op("scalar", lambda e, d=dst, c=c: e.copy(out=d[:, c, :], in_=wtmp[:, 0:512]))
              for c in range(4):
                  p.wait("sync", t_wcast)
                  tld = p.op("sync", lambda e, c=c: e.dma_start(
                      out=wtmp[:], in_=h["pw"][c * 128:(c + 1) * 128, :]), dma="wtm")
                  p.waits("scalar", [tld, t_wcast])
                  t_wcast = p.op("scalar", lambda e, c=c: e.copy(out=pwb[:, c, :], in_=wtmp[:]))

              t_pb = None
              for dg in range(4):
                  p.waits("tensor", [tl["onesf"], tl["pbr"], t_pb])
                  tb = p.op("tensor", lambda e, dg=dg: e.matmul(
                      out=psf[5][:], lhsT=onesf[:], rhs=pbr_sb[:, dg * 512:(dg + 1) * 512],
                      start=True, stop=True))
                  p.wait("vector", tb)
                  t_pb = p.op("vector", lambda e, dg=dg: e.tensor_copy(out=pbb[:, dg, :], in_=psf[5][:]))
                  p.wait("tensor", t_pb)

              bank_free = {i: None for i in range(6)}
              bankT_free = {i: None for i in range(2)}
              g_qhkh = None
              g_vh = None
              g_probs = None
              g_probsn = None
              g_pT = None
              g_ctx = None
              g_mha = None
              g_sstg = None

              for tau in range(NTAU):
                  for (mat, bias, outb, scl, qr) in ((wqb, bq8, qh, 0.125, 0), (wkb, bk2, kh, 1.0, 1)):
                      for c in range(4):
                          bk = psf[c % 2]
                          p.waits("tensor", [t_wcast, bank_free[c % 2]])
                          for kc in range(4):
                              p.op("tensor", lambda e, bk=bk, m=mat, c=c, kc=kc, tau=tau, qr=qr:
                                   e.matmul(out=bk[:], lhsT=m[:, kc, c * 128:(c + 1) * 128],
                                            rhs=qkv_fm[:, qr * 4 + kc, tau * 512:(tau + 1) * 512],
                                            start=(kc == 0), stop=(kc == 3)),
                                   inc=(1 if kc == 3 else 0))
                          tmm = p.token("tensor")
                          p.waits("scalar", [tmm, t_bq8, g_qhkh])
                          tev = p.op("scalar", lambda e, bk=bk, o=outb, c=c, b=bias, s=scl: e.activation(
                              out=o[:, c, :], in_=bk[:], func=AF.Identity, bias=b[:, c:c + 1], scale=s))
                          bank_free[c % 2] = tev
                  for tc2 in range(4):
                      bk = psf[2 + tc2 % 2]
                      p.waits("tensor", [t_wcast, bank_free[2 + tc2 % 2]])
                      for kc in range(4):
                          p.op("tensor", lambda e, bk=bk, tc2=tc2, kc=kc, tau=tau:
                               e.matmul(out=bk[:], lhsT=qkv_fm[:, 8 + kc, tau * 512 + tc2 * 128: tau * 512 + (tc2 + 1) * 128],
                                        rhs=wvb[:, kc, :], start=(kc == 0), stop=(kc == 3)),
                               inc=(1 if kc == 3 else 0))
                      tmm = p.token("tensor")
                      p.waits("scalar", [tmm, g_vh])
                      tev = p.op("scalar", lambda e, bk=bk, tc2=tc2: e.copy(out=vh[:, tc2, :], in_=bk[:]))
                      bank_free[2 + tc2 % 2] = tev

                  for b2 in range(2):
                      blk = tau * 2 + b2
                      for hh in range(8):
                          hp, eo = hh // 2, hh % 2
                          hd0 = 64 * eo
                          for qc in range(2):
                              bk = psf[qc]
                              p.waits("tensor", [bank_free[qc], tl["cmask"]])
                              tmm = p.op("tensor", lambda e, bk=bk, hp=hp, hd0=hd0, qc=qc, b2=b2:
                                         e.matmul(out=bk[:, 0:256],
                                                  lhsT=qh[hd0:hd0 + 64, hp, b2 * 256 + qc * 128: b2 * 256 + (qc + 1) * 128],
                                                  rhs=kh[hd0:hd0 + 64, hp, b2 * 256:(b2 + 1) * 256],
                                                  start=True, stop=True))
                              g_qhkh = tmm
                              p.wait("vector", tmm)
                              p.op("vector", lambda e, bk=bk, qc=qc: e.tensor_tensor(
                                  out=bk[:, 0:256], in0=bk[:, 0:256], in1=cmask[:, qc, :], op=ALU.add))
                              tmx = p.op("vector", lambda e, bk=bk: e.reduce_max(
                                  out=scols[:, 0:1], in_=bk[:, 0:256], axis=AX.X, negate=True))
                              p.waits("scalar", [tmx] + ([g_probs] if g_probs else []))
                              tex = p.op("scalar", lambda e, bk=bk: e.activation(
                                  out=probs[:], in_=bk[:, 0:256], func=AF.Exp,
                                  bias=scols[:, 0:1], accum_out=scols[:, 1:2]))
                              bank_free[qc] = tex
                              p.waits("vector", [tex, g_probsn])
                              p.op("vector", lambda e: e.reciprocal(out=scols[:, 1:2], in_=scols[:, 1:2]))
                              tno = p.op("vector", lambda e: e.tensor_scalar_mul(
                                  out=probsn[:], in0=probs[:], scalar1=scols[:, 1:2]))
                              g_probs = tno
                              for sc in range(qc + 1):
                                  bT = psT[sc % 2]
                                  p.waits("tensor", [tno, bankT_free[sc % 2], g_pT])
                                  ttr = p.op("tensor", lambda e, bT=bT, sc=sc: e.transpose(
                                      out=bT[:, 0:128], in_=probsn[:, sc * 128:(sc + 1) * 128], identity=eyeb[:]))
                                  g_probsn = ttr
                                  p.wait("vector", ttr)
                                  tevT = p.op("vector", lambda e, bT=bT, qc=qc, sc=sc: e.tensor_copy(
                                      out=pT[:, qc + sc, :], in_=bT[:, 0:128]))
                                  bankT_free[sc % 2] = tevT
                          tpT = p.token("vector")
                          cbi = 2 + hh % 2
                          cb = psf[cbi]
                          p.waits("tensor", [tpT, bank_free[cbi]])
                          p.op("tensor", lambda e, cb=cb, b2=b2, hh=hh:
                               e.matmul(out=cb[0:64, 0:128],
                                        lhsT=vh[:, b2 * 2, hh * 64:hh * 64 + 64],
                                        rhs=pT[:, 0, :], start=True, stop=True), inc=0)
                          p.op("tensor", lambda e, cb=cb, b2=b2, hh=hh:
                               e.matmul(out=cb[0:64, 128:256],
                                        lhsT=vh[:, b2 * 2, hh * 64:hh * 64 + 64],
                                        rhs=pT[:, 1, :], start=True, stop=False), inc=0)
                          tcx = p.op("tensor", lambda e, cb=cb, b2=b2, hh=hh:
                                     e.matmul(out=cb[0:64, 128:256],
                                              lhsT=vh[:, b2 * 2 + 1, hh * 64:hh * 64 + 64],
                                              rhs=pT[:, 2, :], start=False, stop=True), inc=1)
                          g_pT = tcx
                          g_vh = tcx
                          p.waits("scalar", [tcx, g_ctx])
                          tevc = p.op("scalar", lambda e, cb=cb, hp=hp, eo=eo: e.activation(
                              out=ctxf[64 * eo:64 * eo + 64, hp, :], in_=cb[0:64, 0:256],
                              func=AF.Identity, bias=bv2[64 * eo:64 * eo + 64, hp:hp + 1]))
                          bank_free[cbi] = tevc
                      t_ctx = p.token("scalar")
                      for oc in range(4):
                          p.waits("tensor", [t_ctx, bank_free[4]])
                          for kc in range(4):
                              p.op("tensor", lambda e, oc=oc, kc=kc:
                                   e.matmul(out=psf[4][:, 0:256], lhsT=wob[:, kc, oc * 128:(oc + 1) * 128],
                                            rhs=ctxf[:, kc, :], start=(kc == 0), stop=(kc == 3)),
                                   inc=(1 if kc == 3 else 0))
                          tmm = p.token("tensor")
                          g_ctx = tmm
                          p.waits("scalar", [tmm, g_mha])
                          tev = p.op("scalar", lambda e, oc=oc: e.activation(
                              out=mha[:, oc, :], in_=psf[4][:, 0:256], func=AF.Identity, bias=bo2[:, oc:oc + 1]))
                          bank_free[4] = tev
                      t_mha = p.token("scalar")
                      for tc2 in range(2):
                          for dg in range(4):
                              bki = 4 + dg % 2
                              p.waits("tensor", [t_mha, bank_free[bki], t_pb])
                              for oc in range(4):
                                  p.op("tensor", lambda e, bki=bki, tc2=tc2, dg=dg, oc=oc:
                                       e.matmul(out=psf[bki][:], lhsT=mha[:, oc, tc2 * 128:(tc2 + 1) * 128],
                                                rhs=pwb[:, oc, dg * 512:(dg + 1) * 512],
                                                start=(oc == 0), stop=(oc == 3)),
                                       inc=(1 if oc == 3 else 0))
                              tmm = p.token("tensor")
                              g_mha = tmm
                              p.waits("vector", [tmm, g_sstg])
                              tev = p.op("vector", lambda e, bki=bki, dg=dg: e.tensor_tensor(
                                  out=sstg[:, dg, :], in0=psf[bki][:], in1=pbb[:, dg, :], op=ALU.add))
                              bank_free[bki] = tev
                          tevl = p.token("vector")
                          row0 = blk * 256 + tc2 * 128
                          p.wait("gpsimd", tevl)
                          g_sstg = p.op("gpsimd", lambda e, r=row0: e.dma_start(
                              out=s_scr[r:r + 128, :], in_=sstg[:].rearrange("p g n -> p (g n)")), dma="sst")

        p.barrier()

        with ExitStack() as e3:
          if PHASE_LIMIT >= 5:
              sb3 = lambda n, s, d: e3.enter_context(nc.sbuf_tensor(n, s, d))
              psB = [e3.enter_context(nc.psum_tensor(f"psB{i}", [128, 512], F32)) for i in range(8)]
              t_buf = sb3("t_buf", [128, NJC, 512], BF16)
              wslab = [sb3(f"wslab{i}", [128, 512], BF16) for i in range(4)]
              wdslab = [sb3(f"wdslab{i}", [128, D], BF16) for i in range(2)]
              sgb = sb3("sgb", [128, 512], BF16)
              uub = sb3("uub", [128, 512], BF16)
              q8b = sb3("q8b", [128, 512], FP8)
              tmx = sb3("tmx", [128, 512], F32)
              trow = sb3("trow", [1, 512], F32)
              qrow = sb3("qrow", [1, 512], F32)
              srow = sb3("srow", [1, 512], F32)
              qsbt = sb3("qsbt", [128, 512], F32)
              stbt = sb3("stbt", [128, 512], F32)
              sslab = sb3("sslab", [128, 512], BF16)
              xslab2 = [sb3(f"xslab2{i}", [128, 512], F32) for i in range(2)]
              omb = sb3("omb", [128, 512], F32)
              gyb = sb3("gyb", [128, 512], BF16)

              g_tbuf = None
              prev_wd = [None, None]
              g_sgb = None
              g_sslab = None
              g_x2 = [None, None]
              g_omb = None
              g_gyb = None

              bank_free = {i: None for i in range(8)}
              prev_w = [None] * 4
              for tau in range(NTAU):
                  for G in range(NG):
                      for d in range(NDC):
                          for mi, scr in enumerate((wgq_scr, wuq_scr)):
                              wb = wslab[(d * 2 + mi) % 4]
                              p.wait("sync", prev_w[(d * 2 + mi) % 4])
                              tld = p.op("sync", lambda e, b=wb, s=scr, d=d, G=G: e.dma_start(
                                  out=b[:], in_=s[d * 128:(d + 1) * 128, G * 512:(G + 1) * 512]), dma=f"ws{(d * 2 + mi) % 4}")
                              p.waits("tensor", [tld] + ([bank_free[i] for i in range(8)] + [g_tbuf] if d == 0 else []))
                              for cc in range(4):
                                  bk = psB[mi * 4 + cc]
                                  p.op("tensor", lambda e, bk=bk, b=wb, cc=cc, d=d, tau=tau:
                                       e.matmul(out=bk[:], lhsT=b[:, cc * 128:(cc + 1) * 128],
                                                rhs=qx_fm[:, d, tau * 512:(tau + 1) * 512],
                                                start=(d == 0), stop=(d == NDC - 1)),
                                       inc=(1 if cc == 3 else 0))
                              prev_w[(d * 2 + mi) % 4] = p.token("tensor")
                      t_gu = p.token("tensor")
                      for cc in range(4):
                          c = G * 4 + cc
                          p.waits("scalar", [t_gu, g_sgb])
                          tg = p.op("scalar", lambda e, cc=cc, c=c: e.activation(
                              out=sgb[:], in_=psB[cc][:], func=AF.Silu, scale=wsg[:, c:c + 1]))
                          tu = p.op("scalar", lambda e, cc=cc, c=c: e.activation(
                              out=uub[:], in_=psB[4 + cc][:], func=AF.Copy, scale=wsu[:, c:c + 1]))
                          p.waits("vector", [tu, g_tbuf])
                          tt2 = p.op("vector", lambda e, c=c: e.tensor_tensor(
                              out=t_buf[:, c, :], in0=sgb[:], in1=uub[:], op=ALU.mult))
                          g_sgb = tt2
                          bank_free[cc] = tt2
                          bank_free[4 + cc] = tt2
                  g_tbuf = None

                  p.op("vector", lambda e: e.tensor_reduce(
                      out=tmx[:], in_=t_buf[:].rearrange("p c s -> p s c"),
                      axis=AX.X, op=ALU.max, apply_absolute_value=True))
                  t_amax = p.token("vector")
                  p.wait("gpsimd", t_amax)
                  t_crow = p.op("gpsimd", lambda e: e.tensor_reduce(
                      out=trow[:], in_=tmx[:], axis=AX.C, op=ALU.max))
                  p.wait("vector", t_crow)
                  p.op("vector", lambda e: e.reciprocal(out=qrow[:], in_=trow[:]))
                  p.op("vector", lambda e: e.tensor_scalar_mul(out=qrow[:], in0=qrow[:], scalar1=224.0))
                  p.op("vector", lambda e: e.tensor_scalar_mul(out=srow[:], in0=trow[:], scalar1=1.0 / 224.0))
                  t_rows = p.token("vector")
                  for (row, dst) in ((qrow, qsbt), (srow, stbt)):
                      p.waits("tensor", [t_rows, tl["onesf"], bank_free[0]])
                      tb = p.op("tensor", lambda e, r=row: e.matmul(
                          out=psB[0][:], lhsT=onesf[:], rhs=r[:], start=True, stop=True))
                      p.wait("vector", tb)
                      tcp = p.op("vector", lambda e, ds=dst: e.tensor_copy(out=ds[:], in_=psB[0][:]))
                      bank_free[0] = tcp
                      p.wait("tensor", tcp)
                  for c in range(NJC):
                      p.op("vector", lambda e, c=c: e.tensor_tensor(
                          out=q8b[:], in0=t_buf[:, c, :], in1=qsbt[:], op=ALU.mult))
                      p.op("vector", lambda e, c=c: e.tensor_tensor(
                          out=t_buf[:, c, :], in0=q8b[:], in1=stbt[:], op=ALU.mult))
                  t_qt = p.token("vector")

                  for tcp2 in range(2):
                      for j in range(NJC):
                          wb = wdslab[j % 2]
                          p.wait("sync", prev_wd[j % 2])
                          tld = p.op("sync", lambda e, b=wb, j=j: e.dma_start(
                              out=b[:], in_=wdq_scr[j * 128:(j + 1) * 128, :]), dma=f"wdl{j % 2}")
                          p.waits("tensor", [tld, t_qt] + ([bank_free[i] for i in range(8)] if j == 0 else []))
                          for tc2 in range(2):
                              for hg in range(4):
                                  bk = psB[tc2 * 4 + hg]
                                  p.op("tensor", lambda e, bk=bk, b=wb, j=j, tcp2=tcp2, tc2=tc2, hg=hg:
                                       e.matmul(out=bk[:],
                                                lhsT=t_buf[:, j, (tcp2 * 2 + tc2) * 128:(tcp2 * 2 + tc2 + 1) * 128],
                                                rhs=b[:, hg * 512:(hg + 1) * 512],
                                                start=(j == 0), stop=(j == NJC - 1)),
                                       inc=(1 if tc2 == 1 and hg == 3 else 0))
                          prev_wd[j % 2] = p.token("tensor")
                      t_dn = p.token("tensor")
                      g_tbuf = t_dn
                      for tc2 in range(2):
                          row0 = (tau * 4 + tcp2 * 2 + tc2) * 128
                          for hg in range(4):
                              bk = psB[tc2 * 4 + hg]
                              p.waits("scalar", [g_x2[hg % 2]])
                              tlx = p.op("scalar", lambda e, b=xslab2[hg % 2], r=row0, hg=hg: e.dma_start(
                                  out=b[:], in_=h["x"][r:r + 128, hg * 512:(hg + 1) * 512]), dma=f"x2{hg % 2}")
                              p.waits("scalar", [g_sslab])
                              tls = p.op("scalar", lambda e, r=row0, hg=hg: e.dma_start(
                                  out=sslab[:], in_=s_scr[r:r + 128, hg * 512:(hg + 1) * 512]), dma="ssl")
                              p.waits("scalar", [t_dn, g_gyb])
                              tg = p.op("scalar", lambda e, bk=bk: e.activation(
                                  out=gyb[:], in_=bk[:], func=AF.Silu))
                              bank_free[tc2 * 4 + hg] = tg
                              p.waits("vector", [tg, tls, tlx, g_omb])
                              p.op("vector", lambda e: e.tensor_tensor(out=omb[:], in0=sslab[:], in1=gyb[:], op=ALU.mult))
                              g_sslab = p.token("vector")
                              g_gyb = g_sslab
                              tcb = p.op("vector", lambda e, b=xslab2[hg % 2]: e.tensor_tensor(
                                  out=omb[:], in0=omb[:], in1=b[:], op=ALU.add))
                              g_x2[hg % 2] = tcb
                              p.wait("gpsimd", tcb)
                              g_omb = p.op("gpsimd", lambda e, r=row0, hg=hg: e.dma_start(
                                  out=h["out"][r:r + 128, hg * 512:(hg + 1) * 512], in_=omb[:]), dma="out")


        if DEBUG_DUMP:
            p.barrier()
            if PHASE_LIMIT >= 1:
                p.op("sync", lambda e: e.dma_start(out=h["dbg_wg"], in_=wgq_scr[:, :]), dma="out")
                p.op("sync", lambda e: e.dma_start(out=h["dbg_wd"], in_=wdq_scr[:, :]), dma="out")
            if PHASE_LIMIT >= 2:
                p.op("sync", lambda e: e.dma_start(out=h["dbg_qx"], in_=qx_fm[:]), dma="out")
            if PHASE_LIMIT >= 4:
                p.op("sync", lambda e: e.dma_start(out=h["dbg_s"], in_=s_scr[:, :]), dma="out")
            p.items["sync"].append(("wait", "out", p.count["out"]))
        if "out" in p.count:
            p.items["gpsimd"].append(("wait", "out", p.count["out"]))
        else:
            for cname in list(p.chans):
                p.items["gpsimd"].append(("wait", cname, p.count[cname]))
        finish()
        with nc.Block() as block:
            p.emit(block, sems)
    return nc


_NC_CACHE = None


def _get_program():
    global _NC_CACHE
    if _NC_CACHE is None:
        _NC_CACHE = build_program()
    return _NC_CACHE


def kernel(**inputs):
    f32 = np.float32
    x = np.asarray(inputs["L0_attn_x"], f32)
    cw = np.ascontiguousarray(np.asarray(inputs["L0_attn_conv_w"], f32))
    cb2 = np.ascontiguousarray(np.asarray(inputs["L0_attn_conv_b"], f32).reshape(12, 128).T)
    wq2 = np.ascontiguousarray(np.asarray(inputs["L0_attn_wq"], f32).reshape(512, 512))
    wk2 = np.ascontiguousarray(np.asarray(inputs["L0_attn_wk"], f32).reshape(512, 512))
    wv2 = np.ascontiguousarray(np.asarray(inputs["L0_attn_wv"], f32).reshape(512, 512))
    wo2 = np.ascontiguousarray(np.asarray(inputs["L0_attn_wo"], f32).reshape(512, 512))
    bq2 = np.ascontiguousarray(np.asarray(inputs["L0_attn_bq"], f32).reshape(4, 128).T)
    bk2 = np.ascontiguousarray(np.asarray(inputs["L0_attn_bk"], f32).reshape(4, 128).T)
    bv2 = np.ascontiguousarray(np.asarray(inputs["L0_attn_bv"], f32).reshape(4, 128).T)
    bo2 = np.ascontiguousarray(np.asarray(inputs["L0_attn_bo"], f32).reshape(4, 128).T)
    pw = np.ascontiguousarray(np.asarray(inputs["L0_attn_proj_w"], f32))
    pbr = np.ascontiguousarray(np.asarray(inputs["L0_attn_proj_b"], f32).reshape(1, D))
    wg = np.ascontiguousarray(np.asarray(inputs["L0_ffn_w_gate"], f32))
    wu = np.ascontiguousarray(np.asarray(inputs["L0_ffn_w_up"], f32))
    wd = np.ascontiguousarray(np.asarray(inputs["L0_ffn_w_down"], f32))
    wgT = np.ascontiguousarray(wg.T)
    wuT = np.ascontiguousarray(wu.T)
    wdT = np.ascontiguousarray(wd.T)
    didx = np.ascontiguousarray(np.asarray(inputs["L0_ffn_down_reorder_index"], np.int32))
    eyef = np.eye(128, dtype=f32)
    onesf = np.ones((1, 128), f32)
    jj = np.arange(256)[None, :]
    pp = np.arange(128)[:, None]
    cmask = np.ascontiguousarray(np.stack([
        np.where(jj <= pp, 0.0, -1e9).astype(f32),
        np.where(jj <= 128 + pp, 0.0, -1e9).astype(f32)], axis=1))

    shared = dict(cw=cw, cb2=cb2, wq2=wq2, wk2=wk2, wv2=wv2, wo2=wo2,
                  bq2=bq2, bk2=bk2, bv2=bv2, bo2=bo2, pw=pw, pbr=pbr,
                  wg=wg, wu=wu, wd=wd, wgT=wgT, wuT=wuT, wdT=wdT,
                  didx=didx, eyef=eyef, onesf=onesf, cmask=cmask)
    in_maps = []
    for c in range(NCORES):
        seq, half = c // 2, c % 2
        t0 = half * T
        m = dict(shared)
        m["x"] = np.ascontiguousarray(x[seq, t0:t0 + T])
        m["xh"] = (np.zeros((HALO, D), f32) if half == 0
                   else np.ascontiguousarray(x[seq, t0 - HALO:t0]))
        in_maps.append(m)

    nc = _get_program()
    res = run_bass_kernel_spmd(nc, in_maps, core_ids=list(range(NCORES)))
    out = np.empty((4, 4096, D), f32)
    for c in range(NCORES):
        seq, half = c // 2, c % 2
        out[seq, half * T:(half + 1) * T] = np.asarray(res.results[c]["out"])
    return out


if __name__ == "__main__":
    build_program()
    print("built ok")


# version 05
# speedup vs baseline: 1.5781x; 1.5781x over previous
import numpy as np
import concourse.bass as bass
import concourse.mybir as mybir
from concourse.bass_utils import run_bass_kernel_spmd
from contextlib import ExitStack

F32 = mybir.dt.float32
BF16 = mybir.dt.bfloat16
FP8 = mybir.dt.float8e4
I32 = mybir.dt.int32
AF = mybir.ActivationFunctionType
ALU = mybir.AluOpType
AX = mybir.AxisListType

NCORES = 8
T = 2048
D = 2048
QKV = 1536
HD = 512
FF = 5632
NTC = 16
NTAU = 4
NDC = 16
NJC = 44
NG = 11
HALO = 16
CS = 2064

ENGINES = ("sync", "scalar", "vector", "tensor", "gpsimd")

DEBUG_SIM = True
SELF_WAIT_ENGINES = {"vector"}
FENCE_MODE = "all"
SKIP_HALO = False
DEBUG_DUMP = False
PHASE_LIMIT = 5


class Prog:

    def __init__(self):
        self.items = {e: [] for e in ENGINES}
        self.count = {}
        for e in ENGINES:
            self.count[e] = 0
        self.chans = []
        self.bound = {e: {} for e in ENGINES}

    def chan(self, name):
        if name not in self.count:
            self.count[name] = 0
            self.chans.append(name)
        return name

    def op(self, eng, fn, dma=None, inc=None):
        if dma is not None:
            sem = self.chan(dma)
            amt = 16
        else:
            sem = eng
            amt = 1 if inc is None else inc
            if DEBUG_SIM and FENCE_MODE == "all" and (SELF_WAIT_ENGINES is None or eng in SELF_WAIT_ENGINES):
                amt = max(amt, 1)
                if self.count[eng] > 0:
                    self.items[eng].append(("wait", eng, self.count[eng]))
        self.items[eng].append(("op", fn, sem, amt))
        self.count[sem] += amt
        return (sem, self.count[sem]) if amt else None

    def fence(self, eng):
        if self.count.get(eng, 0) > 0:
            self.items[eng].append(("wait", eng, self.count[eng]))

    def wait(self, eng, tok):
        if tok is None:
            return
        sem, val = tok
        if sem == eng or self.bound[eng].get(sem, 0) >= val:
            return
        self.items[eng].append(("wait", sem, val))
        self.bound[eng][sem] = val

    def waits(self, eng, toks):
        for t in toks:
            self.wait(eng, t)

    def barrier(self):
        snap = dict(self.count)
        for e in ENGINES:
            for s, v in snap.items():
                if s != e and v > 0:
                    self.wait(e, (s, v))

    def token(self, sem):
        return (sem, self.count[sem]) if self.count.get(sem, 0) else None

    def emit(self, block, sems):
        def mk(ename):
            def emitter(eng):
                for it in self.items[ename]:
                    if it[0] == "wait":
                        eng.wait_ge(sems[it[1]], it[2])
                    else:
                        ins = it[1](eng)
                        if it[3]:
                            ins.then_inc(sems[it[2]], it[3])
            return emitter
        block.sync(mk("sync"))
        block.scalar(mk("scalar"))
        block.vector(mk("vector"))
        block.tensor(mk("tensor"))
        block.gpsimd(mk("gpsimd"))


def build_program():
    nc = bass.Bass("TRN2", target_bir_lowering=False, debug=False,
                   detect_race_conditions=DEBUG_SIM)
    h = {}

    def inp(name, shape, dt=F32):
        h[name] = nc.dram_tensor(name, shape, dt, kind="ExternalInput").ap()

    inp("x", [T, D]); inp("xh", [HALO, D])
    inp("cw", [3, D, QKV]); inp("cb2", [128, 12])
    for nm in ("wq2", "wk2", "wv2", "wo2"):
        inp(nm, [HD, HD])
    for nm in ("bq2", "bk2", "bv2", "bo2"):
        inp(nm, [128, 4])
    inp("pw", [HD, D]); inp("pbr", [1, D])
    inp("wg", [FF, D]); inp("wu", [FF, D]); inp("wd", [D, FF])
    inp("wgT", [D, FF]); inp("wuT", [D, FF]); inp("wdT", [FF, D])
    inp("didx", [FF], I32)
    inp("eyef", [128, 128]); inp("onesf", [1, 128]); inp("cmask", [128, 2, 256])
    h["out"] = nc.dram_tensor("out", [T, D], F32, kind="ExternalOutput").ap()
    if DEBUG_DUMP:
        h["dbg_s"] = nc.dram_tensor("dbg_s", [T, D], BF16, kind="ExternalOutput").ap()
        h["dbg_wg"] = nc.dram_tensor("dbg_wg", [D, FF], BF16, kind="ExternalOutput").ap()
        h["dbg_wd"] = nc.dram_tensor("dbg_wd", [FF, D], BF16, kind="ExternalOutput").ap()
        h["dbg_qx"] = nc.dram_tensor("dbg_qx", [128, NDC, T], BF16, kind="ExternalOutput").ap()

    xn_scr = nc.dram_tensor("xn_scr", [NDC, 128, CS], BF16, kind="Internal").ap()
    s_scr = nc.dram_tensor("s_scr", [T, D], BF16, kind="Internal").ap()
    wgq_scr = nc.dram_tensor("wgq_scr", [D, FF], BF16, kind="Internal").ap()
    wuq_scr = nc.dram_tensor("wuq_scr", [D, FF], BF16, kind="Internal").ap()
    wdq_scr = nc.dram_tensor("wdq_scr", [FF, D], BF16, kind="Internal").ap()

    p = Prog()
    es = ExitStack()
    with es:
        sems = {e: es.enter_context(nc.semaphore("sem_" + e)) for e in ENGINES}
        def finish():
            for cname in p.chans:
                sems[cname] = es.enter_context(nc.semaphore("sem_" + cname))
        sb = lambda name, shape, dt: es.enter_context(nc.sbuf_tensor(name, shape, dt))

        eyef = sb("c_eyef", [128, 128], F32)
        eyeb = sb("c_eyeb", [128, 128], BF16)
        onesf = sb("c_onesf", [1, 128], F32)
        cmask = sb("c_cmask", [128, 2, 256], F32)
        cb2 = sb("c_cb2", [128, 12], F32)
        bq8 = sb("c_bq8", [128, 4], F32)
        bk2 = sb("c_bk2", [128, 4], F32)
        bv2 = sb("c_bv2", [128, 4], F32)
        bo2 = sb("c_bo2", [128, 4], F32)
        wsg = sb("c_wsg", [128, NJC], F32)
        wsu = sb("c_wsu", [128, NJC], F32)
        pbr_sb = sb("pbr_sb", [1, D], F32)
        epsc = sb("c_eps", [128, 1], F32)

        tl = {}
        for nm, dst in (("eyef", eyef), ("onesf", onesf), ("cmask", cmask),
                        ("cb2", cb2), ("bk2", bk2), ("bv2", bv2), ("bo2", bo2),
                        ("pbr", pbr_sb)):
            tl[nm] = p.op("sync", lambda e, d=dst, a=h[nm]: e.dma_start(out=d[:], in_=a[:]), dma="ld_" + nm)
        tl["bq2"] = p.op("sync", lambda e: e.dma_start(out=bq8[:], in_=h["bq2"][:]), dma="ld_bq2")
        p.wait("vector", tl["bq2"])
        p.op("vector", lambda e: e.tensor_scalar_mul(out=bq8[:], in0=bq8[:], scalar1=0.125))
        t_bq8 = p.token("vector")
        p.wait("scalar", tl["eyef"])
        t_eyeb = p.op("scalar", lambda e: e.copy(out=eyeb[:], in_=eyef[:]))
        t_eps = p.op("vector", lambda e: e.memset(epsc[:], 1e-5))

        with ExitStack() as ew:
          if PHASE_LIMIT >= 1:
              sbw = lambda n, s, d: ew.enter_context(nc.sbuf_tensor(n, s, d))
              psW = [ew.enter_context(nc.psum_tensor(f"psW{i}", [128, 512], F32)) for i in range(2)]
              idx_s = sbw("idx_s", [128, NJC], I32)
              qrow_g = sbw("qrow_g", [1, FF], F32)
              qrow_u = sbw("qrow_u", [1, FF], F32)
              qrow_d = sbw("qrow_d", [1, D], F32)
              wrow_d = sbw("wrow_d", [1, D], F32)
              wnat = [sbw(f"wnat{i}", [128, FF], F32) for i in range(2)]
              rcol = sbw("rcol", [128, 4], F32)
              wTs = [sbw(f"wTs{i}", [128, 512], F32) for i in range(2)]
              q8t = sbw("q8t", [128, 512], FP8)
              qbf = [sbw(f"qbf{i}", [128, 512], BF16) for i in range(2)]
              qsb = sbw("qsb", [128, 512], F32)
              qsdb = [sbw(f"qsdb{i}", [128, 512], F32) for i in range(4)]
              wsdb = [sbw(f"wsdb{i}", [128, 512], F32) for i in range(4)]
              wdqt = [sbw(f"wdqt{i}", [128, D], BF16) for i in range(2)]

              t_idx = p.op("sync", lambda e: e.dma_start(
                  out=idx_s[:], in_=h["didx"].rearrange("(c p) -> p c", p=128),
                  allow_slow_non_contiguous=True), dma="ld_idx")

              prev_wnat = [None, None]

              def w_maxes(src, n_chunks, row_len, qrow, wscol, wrow):
                  prev = prev_wnat
                  t_rcol_free = None
                  for c in range(n_chunks):
                      buf = wnat[c % 2]
                      p.wait("sync", prev[c % 2])
                      tld = p.op("sync", lambda e, b=buf, s=src, c=c, L=row_len: e.dma_start(
                          out=b[:, 0:L], in_=s[c * 128:(c + 1) * 128, :]), dma=f"wn{c % 2}")
                      p.waits("vector", [tld, t_rcol_free])
                      p.op("vector", lambda e, b=buf, L=row_len: e.reduce_max(
                          out=rcol[:, 0:1], in_=b[:, 0:L], axis=AX.X, apply_absolute_value=True))
                      if FENCE_MODE == "recip": p.fence("vector")
                      prev[c % 2] = p.token("vector")
                      p.op("vector", lambda e: e.reciprocal(out=rcol[:, 1:2], in_=rcol[:, 0:1]))
                      if FENCE_MODE == "recip": p.fence("vector")
                      p.op("vector", lambda e: e.tensor_scalar_mul(out=rcol[:, 2:3], in0=rcol[:, 1:2], scalar1=224.0))
                      if wscol is not None:
                          p.op("vector", lambda e, c=c: e.tensor_scalar_mul(
                              out=wscol[:, c:c + 1], in0=rcol[:, 0:1], scalar1=1.0 / 224.0))
                      else:
                          p.op("vector", lambda e: e.tensor_scalar_mul(
                              out=rcol[:, 3:4], in0=rcol[:, 0:1], scalar1=1.0 / 224.0))
                      tv = p.token("vector")
                      p.waits("tensor", [tv, tl["eyef"]])
                      tt = p.op("tensor", lambda e: e.transpose(
                          out=psW[0][0:1, 0:128], in_=rcol[:, 2:3], identity=eyef[:]))
                      t_rcol_free = tt
                      p.wait("vector", tt)
                      p.op("vector", lambda e, q=qrow, c=c: e.tensor_copy(
                          out=q[:, c * 128:(c + 1) * 128], in_=psW[0][0:1, 0:128]))
                      tpe = p.token("vector")
                      p.wait("tensor", tpe)
                      if wscol is None:
                          tt2 = p.op("tensor", lambda e: e.transpose(
                              out=psW[0][0:1, 0:128], in_=rcol[:, 3:4], identity=eyef[:]))
                          t_rcol_free = tt2
                          p.wait("vector", tt2)
                          p.op("vector", lambda e, w=wrow, c=c: e.tensor_copy(
                              out=w[:, c * 128:(c + 1) * 128], in_=psW[0][0:1, 0:128]))
                          tpe = p.token("vector")
                          p.wait("tensor", tpe)
                  return p.token("vector")

              t_mg = w_maxes(h["wg"], NJC, D, qrow_g, wsg, None)
              t_mu = w_maxes(h["wu"], NJC, D, qrow_u, wsu, None)
              t_md = w_maxes(h["wd"], NDC, FF, qrow_d, None, wrow_d)

              wq_state = {"prev8": [None, None], "prev_st": [None, None], "t_q8_free": None}

              def w_quant(srcT, qrow, dst_scr, qdep):
                  prev8 = wq_state["prev8"]
                  prev_st = wq_state["prev_st"]
                  t_q8_free = wq_state["t_q8_free"]
                  for G in range(NG):
                      p.waits("tensor", [qdep, tl["onesf"]])
                      tb = p.op("tensor", lambda e, G=G, q=qrow: e.matmul(
                          out=psW[1][:], lhsT=onesf[:], rhs=q[:, G * 512:(G + 1) * 512],
                          start=True, stop=True))
                      p.wait("vector", tb)
                      tq = p.op("vector", lambda e: e.tensor_copy(out=qsb[:], in_=psW[1][:]))
                      p.wait("tensor", tq)
                      for d in range(NDC):
                          buf = wTs[d % 2]
                          p.wait("sync", prev8[d % 2])
                          tld = p.op("sync", lambda e, b=buf, s=srcT, d=d, G=G: e.dma_start(
                              out=b[:], in_=s[d * 128:(d + 1) * 128, G * 512:(G + 1) * 512]), dma=f"wt{d % 2}")
                          p.waits("vector", [tld, t_q8_free])
                          t8 = p.op("vector", lambda e, b=buf: e.tensor_tensor(
                              out=q8t[:], in0=b[:], in1=qsb[:], op=ALU.mult))
                          prev8[d % 2] = t8
                          ob = qbf[d % 2]
                          p.waits("scalar", [t8, prev_st[d % 2]])
                          tc = p.op("scalar", lambda e, o=ob: e.copy(out=o[:], in_=q8t[:]))
                          t_q8_free = tc
                          wq_state["t_q8_free"] = tc
                          p.wait("sync", tc)
                          ts = p.op("sync", lambda e, o=ob, ds=dst_scr, d=d, G=G: e.dma_start(
                              out=ds[d * 128:(d + 1) * 128, G * 512:(G + 1) * 512], in_=o[:]), dma=f"wqs{d % 2}")
                          prev_st[d % 2] = ts
                  return p.token("sync")

              w_quant(h["wgT"], qrow_g, wgq_scr, t_mg)
              w_quant(h["wuT"], qrow_u, wuq_scr, t_mu)

              t_b4 = None
              for hg in range(4):
                  for src_row, dstl in ((qrow_d, qsdb), (wrow_d, wsdb)):
                      p.waits("tensor", [t_md, tl["onesf"], t_b4])
                      tb = p.op("tensor", lambda e, hg=hg, r=src_row: e.matmul(
                          out=psW[1][:], lhsT=onesf[:], rhs=r[:, hg * 512:(hg + 1) * 512],
                          start=True, stop=True))
                      p.wait("vector", tb)
                      t_b4 = p.op("vector", lambda e, hg=hg, dl=dstl: e.tensor_copy(out=dl[hg][:], in_=psW[1][:]))
                      p.wait("tensor", t_b4)

              prev_ld = prev_wnat
              prev_sc = [None, None]
              for c in range(NJC):
                  buf = wnat[c % 2]
                  p.wait("sync", prev_ld[c % 2])
                  tld = p.op("sync", lambda e, b=buf, c=c: e.dma_start(
                      out=b[:, 0:D], in_=h["wdT"][c * 128:(c + 1) * 128, :]), dma=f"wn{c % 2}")
                  ob = wdqt[c % 2]
                  p.waits("vector", [tld, t_b4, prev_sc[c % 2]])
                  for hg in range(4):
                      p.op("vector", lambda e, b=buf, hg=hg: e.tensor_tensor(
                          out=q8t[:], in0=b[:, hg * 512:(hg + 1) * 512], in1=qsdb[hg][:], op=ALU.mult))
                      p.op("vector", lambda e, o=ob, hg=hg: e.tensor_tensor(
                          out=o[:, hg * 512:(hg + 1) * 512], in0=q8t[:], in1=wsdb[hg][:], op=ALU.mult))
                  tf = p.token("vector")
                  prev_ld[c % 2] = tf
                  p.waits("gpsimd", [tf, t_idx, prev_sc[c % 2]])
                  tsc = p.op("gpsimd", lambda e, o=ob, c=c: e.indirect_dma_start(
                      out=wdq_scr[:, :],
                      out_offset=bass.IndirectOffsetOnAxis(ap=idx_s[:, c:c + 1], axis=0),
                      in_=o[:], in_offset=None), dma=f"sca{c % 2}")
                  prev_sc[c % 2] = tsc

        p.barrier()

        qx_fm = sb("qx_fm", [128, NDC, T], BF16)

        with ExitStack() as e0:
          if PHASE_LIMIT >= 2:
              sb0 = lambda n, s, d: e0.enter_context(nc.sbuf_tensor(n, s, d))
              psb = [e0.enter_context(nc.psum_tensor(f"ps0b{i}", [128, 512], BF16)) for i in range(2)]
              xt = [sb0(f"xt{i}", [128, D], F32) for i in range(2)]
              xnf = sb0("xnf", [128, D], F32)
              junk = sb0("junk", [128, D], BF16)
              xnb = sb0("xnb", [128, D], BF16)
              q8x = sb0("q8x", [128, D], FP8)
              qxb = sb0("qxb", [128, D], BF16)
              cols = sb0("cols", [128, 8], F32)
              stg = sb0("stg", [128, NDC, 128], BF16)

              guard_xt = [[], []]
              g_xnf = []
              g_q8x = []
              g_qxb = []
              g_xnb = []
              g_stg = None
              bank_tok = [None, None]

              for tc in range(NTC if SKIP_HALO else NTC + 1):
                  halo = (tc == NTC)
                  P = HALO if halo else 128
                  buf = xt[tc % 2]
                  src = h["xh"][:] if halo else h["x"][tc * 128:(tc + 1) * 128, :]
                  p.waits("sync", guard_xt[tc % 2])
                  tld = p.op("sync", lambda e, b=buf, s=src, P=P: e.dma_start(out=b[0:P, :], in_=s), dma=f"xt{tc % 2}")
                  p.waits("vector", [tld] + g_xnf)
                  p.op("vector", lambda e, b=buf, P=P: e.reduce_sum(out=cols[0:P, 0:1], in_=b[0:P, :], axis=AX.X))
                  if FENCE_MODE == "recip": p.fence("vector")
                  p.wait("scalar", tld)
                  tsq = p.op("scalar", lambda e, b=buf, P=P: e.activation(
                      out=junk[0:P, :], in_=b[0:P, :], func=AF.Square, accum_out=cols[0:P, 1:2]))
                  p.wait("vector", tsq)
                  p.op("vector", lambda e, P=P: e.tensor_scalar_mul(out=cols[0:P, 2:3], in0=cols[0:P, 0:1], scalar1=1.0 / D))
                  p.op("vector", lambda e, P=P: e.tensor_scalar_mul(out=cols[0:P, 3:4], in0=cols[0:P, 1:2], scalar1=1.0 / D))
                  p.op("vector", lambda e, P=P: e.tensor_tensor(out=cols[0:P, 4:5], in0=cols[0:P, 2:3], in1=cols[0:P, 2:3], op=ALU.mult))
                  tv = p.op("vector", lambda e, P=P: e.tensor_tensor(out=cols[0:P, 4:5], in0=cols[0:P, 3:4], in1=cols[0:P, 4:5], op=ALU.subtract))
                  p.waits("scalar", [tv, t_eps])
                  tsd = p.op("scalar", lambda e, P=P: e.activation(out=cols[0:P, 5:6], in_=cols[0:P, 4:5], func=AF.Sqrt, bias=epsc[0:P, 0:1]))
                  p.wait("vector", tsd)
                  p.op("vector", lambda e, P=P: e.reciprocal(out=cols[0:P, 6:7], in_=cols[0:P, 5:6]))
                  if FENCE_MODE == "recip": p.fence("vector")
                  txn = p.op("vector", lambda e, b=buf, P=P: e.tensor_scalar(
                      out=xnf[0:P, :], in0=b[0:P, :], scalar1=cols[0:P, 2:3], scalar2=cols[0:P, 6:7],
                      op0=ALU.subtract, op1=ALU.mult))
                  guard_xt[tc % 2] = [txn, tsq]
                  p.wait("scalar", txn)
                  txb = p.op("scalar", lambda e, P=P: e.copy(out=xnb[0:P, :], in_=xnf[0:P, :]))
                  g_xnf = [txb]

                  if not halo:
                      p.op("vector", lambda e: e.reduce_max(out=cols[:, 0:1], in_=xnf[:], axis=AX.X, apply_absolute_value=True))
                      if FENCE_MODE == "recip": p.fence("vector")
                      p.op("vector", lambda e: e.reciprocal(out=cols[:, 1:2], in_=cols[:, 0:1]))
                      if FENCE_MODE == "recip": p.fence("vector")
                      tqs = p.op("vector", lambda e: e.tensor_scalar_mul(out=cols[:, 2:3], in0=cols[:, 1:2], scalar1=224.0))
                      tsx = p.op("vector", lambda e: e.tensor_scalar_mul(out=cols[:, 3:4], in0=cols[:, 0:1], scalar1=1.0 / 224.0))
                      p.waits("scalar", [tqs] + g_q8x)
                      t8 = p.op("scalar", lambda e: e.activation(out=q8x[:], in_=xnf[:], func=AF.Copy, scale=cols[:, 2:3]))
                      g_xnf = [t8]
                      p.waits("vector", [t8, tsx] + g_qxb)
                      tqx = p.op("vector", lambda e: e.tensor_scalar_mul(out=qxb[:], in0=q8x[:], scalar1=cols[:, 3:4]))
                      g_q8x = [tqx]

                  ev = []
                  for d in range(NDC):
                      bank = psb[d % 2]
                      p.waits("tensor", [txb, t_eyeb, bank_tok[d % 2], g_stg])
                      tt = p.op("tensor", lambda e, d=d, b=bank, P=P: e.transpose(
                          out=b[:, 0:P], in_=xnb[0:P, d * 128:(d + 1) * 128], identity=eyeb[0:P, 0:P]))
                      if d == NDC - 1:
                          g_xnb = [tt]
                      p.wait("scalar", tt)
                      te = p.op("scalar", lambda e, d=d, b=bank, P=P: e.copy(out=stg[:, d, 0:P], in_=b[:, 0:P]))
                      ev.append(te)
                      bank_tok[d % 2] = te
                  for d in range(NDC):
                      p.wait("sync", ev[d])
                      cst = 0 if halo else HALO + tc * 128
                      p.op("sync", lambda e, d=d, c=cst, P=P: e.dma_start(
                          out=xn_scr[d, :, c:c + P], in_=stg[:, d, 0:P]), dma="stg")
                  g_stg = p.token("stg")

                  if not halo:
                      ev2 = []
                      for d in range(NDC):
                          bank = psb[d % 2]
                          p.waits("tensor", [tqx, bank_tok[d % 2]])
                          tt = p.op("tensor", lambda e, d=d, b=bank: e.transpose(
                              out=b[:, 128:256], in_=qxb[:, d * 128:(d + 1) * 128], identity=eyeb[:]))
                          if d == NDC - 1:
                              g_qxb = [tt]
                          p.wait("vector", tt)
                          te = p.op("vector", lambda e, d=d, tc=tc, b=bank: e.tensor_copy(
                              out=qx_fm[:, d, tc * 128:(tc + 1) * 128], in_=b[:, 128:256]))
                          ev2.append(te)
                          bank_tok[d % 2] = te

        p.barrier()

        qkv_fm = sb("qkv_fm", [128, 12, T], BF16)
        with ExitStack() as e1:
          if PHASE_LIMIT >= 3:
              sb1 = lambda n, s, d: e1.enter_context(nc.sbuf_tensor(n, s, d))
              psA = [e1.enter_context(nc.psum_tensor(f"psA{i}", [128, 512], F32)) for i in range(8)]
              xslab = [sb1(f"xslab{i}", [128, CS], BF16) for i in range(2)]
              cwf = [sb1(f"cwf{i}", [128, 256], F32) for i in range(2)]
              cwb = [sb1(f"cwb{i}", [128, 256], BF16) for i in range(2)]

              prev_ev = None
              prev_cw = [None, None]
              prev_cwf = [None, None]
              prev_x = [None, None]
              for ccg in range(6):
                  step = 0
                  last_mm = None
                  for d in range(NDC):
                      bx = xslab[d % 2]
                      p.wait("sync", prev_x[d % 2])
                      tlx = p.op("sync", lambda e, b=bx, d=d: e.dma_start(out=b[:], in_=xn_scr[d, :, :]), dma=f"xs{d % 2}")
                      for w in range(3):
                          bcf = cwf[step % 2]
                          bcb = cwb[step % 2]
                          p.wait("sync", prev_cwf[step % 2])
                          tlw = p.op("sync", lambda e, b=bcf, w=w, d=d, g=ccg: e.dma_start(
                              out=b[:], in_=h["cw"][w, d * 128:(d + 1) * 128, g * 256:(g + 1) * 256]), dma=f"cw{step % 2}")
                          p.waits("scalar", [tlw, prev_cw[step % 2]])
                          tcw = p.op("scalar", lambda e, bf=bcf, bb=bcb: e.copy(out=bb[:], in_=bf[:]))
                          prev_cwf[step % 2] = tcw
                          p.waits("tensor", [tcw, tlx, prev_ev])
                          first = (d == 0 and w == 0)
                          lastg = (d == NDC - 1 and w == 2)
                          for cc2 in range(2):
                              for tau in range(NTAU):
                                  bk = psA[cc2 * 4 + tau]
                                  isl = (cc2 == 1 and tau == NTAU - 1)
                                  tok = p.op("tensor", lambda e, bk=bk, bb=bcb, bx=bx, cc2=cc2, tau=tau, w=w, f=first, L=lastg:
                                             e.matmul(out=bk[:], lhsT=bb[:, cc2 * 128:(cc2 + 1) * 128],
                                                      rhs=bx[:, HALO + tau * 512 + (w - 2): HALO + tau * 512 + (w - 2) + 512],
                                                      start=f, stop=L),
                                             inc=(1 if isl else 0))
                                  if isl:
                                      last_mm = tok
                          prev_cw[step % 2] = last_mm
                          step += 1
                      prev_x[d % 2] = last_mm
                  p.waits("scalar", [last_mm, tl["cb2"]])
                  for cc2 in range(2):
                      for tau in range(NTAU):
                          bk = psA[cc2 * 4 + tau]
                          cc = ccg * 2 + cc2
                          p.op("scalar", lambda e, bk=bk, cc=cc, tau=tau: e.activation(
                              out=qkv_fm[:, cc, tau * 512:(tau + 1) * 512], in_=bk[:],
                              func=AF.Identity, bias=cb2[:, cc:cc + 1]))
                  prev_ev = p.token("scalar")

        p.barrier()

        with ExitStack() as e2:
          if PHASE_LIMIT >= 4:
              sb2 = lambda n, s, d: e2.enter_context(nc.sbuf_tensor(n, s, d))
              psf = [e2.enter_context(nc.psum_tensor(f"psf{i}", [128, 512], F32)) for i in range(6)]
              psT = [e2.enter_context(nc.psum_tensor(f"psT{i}", [128, 512], BF16)) for i in range(2)]
              wqb = sb2("wqb", [128, 4, 512], BF16)
              wkb = sb2("wkb", [128, 4, 512], BF16)
              wvb = sb2("wvb", [128, 4, 512], BF16)
              wob = sb2("wob", [128, 4, 512], BF16)
              pwb = sb2("pwb", [128, 4, D], BF16)
              pbb = sb2("pbb", [128, 4, 512], F32)
              wtmp = sb2("wtmp", [128, D], F32)
              qh = sb2("qh", [128, 4, 512], BF16)
              kh = sb2("kh", [128, 4, 512], BF16)
              vh = sb2("vh", [128, 4, 512], BF16)
              probs = sb2("probs", [128, 256], BF16)
              probsn = sb2("probsn", [128, 256], BF16)
              pT = sb2("pT", [128, 3, 128], BF16)
              ctxf = sb2("ctxf", [128, 4, 256], BF16)
              mha = sb2("mha", [128, 4, 256], BF16)
              sstg = sb2("sstg", [128, 4, 512], BF16)
              scols = sb2("scols", [128, 2], F32)

              t_wcast = None
              for nm, dst in (("wq2", wqb), ("wk2", wkb), ("wv2", wvb), ("wo2", wob)):
                  for c in range(4):
                      p.wait("sync", t_wcast)
                      tld = p.op("sync", lambda e, a=h[nm], c=c: e.dma_start(
                          out=wtmp[:, 0:512], in_=a[c * 128:(c + 1) * 128, :]), dma="wtm")
                      p.waits("scalar", [tld, t_wcast])
                      t_wcast = p.op("scalar", lambda e, d=dst, c=c: e.copy(out=d[:, c, :], in_=wtmp[:, 0:512]))
              for c in range(4):
                  p.wait("sync", t_wcast)
                  tld = p.op("sync", lambda e, c=c: e.dma_start(
                      out=wtmp[:], in_=h["pw"][c * 128:(c + 1) * 128, :]), dma="wtm")
                  p.waits("scalar", [tld, t_wcast])
                  t_wcast = p.op("scalar", lambda e, c=c: e.copy(out=pwb[:, c, :], in_=wtmp[:]))

              t_pb = None
              for dg in range(4):
                  p.waits("tensor", [tl["onesf"], tl["pbr"], t_pb])
                  tb = p.op("tensor", lambda e, dg=dg: e.matmul(
                      out=psf[5][:], lhsT=onesf[:], rhs=pbr_sb[:, dg * 512:(dg + 1) * 512],
                      start=True, stop=True))
                  p.wait("vector", tb)
                  t_pb = p.op("vector", lambda e, dg=dg: e.tensor_copy(out=pbb[:, dg, :], in_=psf[5][:]))
                  p.wait("tensor", t_pb)

              bank_free = {i: None for i in range(6)}
              bankT_free = {i: None for i in range(2)}
              g_qhkh = None
              g_vh = None
              g_probs = None
              g_probsn = None
              g_pT = None
              g_ctx = None
              g_mha = None
              g_sstg = None

              for tau in range(NTAU):
                  for (mat, bias, outb, scl, qr) in ((wqb, bq8, qh, 0.125, 0), (wkb, bk2, kh, 1.0, 1)):
                      for c in range(4):
                          bk = psf[c % 2]
                          p.waits("tensor", [t_wcast, bank_free[c % 2]])
                          for kc in range(4):
                              p.op("tensor", lambda e, bk=bk, m=mat, c=c, kc=kc, tau=tau, qr=qr:
                                   e.matmul(out=bk[:], lhsT=m[:, kc, c * 128:(c + 1) * 128],
                                            rhs=qkv_fm[:, qr * 4 + kc, tau * 512:(tau + 1) * 512],
                                            start=(kc == 0), stop=(kc == 3)),
                                   inc=(1 if kc == 3 else 0))
                          tmm = p.token("tensor")
                          p.waits("scalar", [tmm, t_bq8, g_qhkh])
                          tev = p.op("scalar", lambda e, bk=bk, o=outb, c=c, b=bias, s=scl: e.activation(
                              out=o[:, c, :], in_=bk[:], func=AF.Identity, bias=b[:, c:c + 1], scale=s))
                          bank_free[c % 2] = tev
                  for tc2 in range(4):
                      bk = psf[2 + tc2 % 2]
                      p.waits("tensor", [t_wcast, bank_free[2 + tc2 % 2]])
                      for kc in range(4):
                          p.op("tensor", lambda e, bk=bk, tc2=tc2, kc=kc, tau=tau:
                               e.matmul(out=bk[:], lhsT=qkv_fm[:, 8 + kc, tau * 512 + tc2 * 128: tau * 512 + (tc2 + 1) * 128],
                                        rhs=wvb[:, kc, :], start=(kc == 0), stop=(kc == 3)),
                               inc=(1 if kc == 3 else 0))
                      tmm = p.token("tensor")
                      p.waits("scalar", [tmm, g_vh])
                      tev = p.op("scalar", lambda e, bk=bk, tc2=tc2: e.copy(out=vh[:, tc2, :], in_=bk[:]))
                      bank_free[2 + tc2 % 2] = tev

                  for b2 in range(2):
                      blk = tau * 2 + b2
                      for hh in range(8):
                          hp, eo = hh // 2, hh % 2
                          hd0 = 64 * eo
                          for qc in range(2):
                              bk = psf[qc]
                              p.waits("tensor", [bank_free[qc], tl["cmask"]])
                              tmm = p.op("tensor", lambda e, bk=bk, hp=hp, hd0=hd0, qc=qc, b2=b2:
                                         e.matmul(out=bk[:, 0:256],
                                                  lhsT=qh[hd0:hd0 + 64, hp, b2 * 256 + qc * 128: b2 * 256 + (qc + 1) * 128],
                                                  rhs=kh[hd0:hd0 + 64, hp, b2 * 256:(b2 + 1) * 256],
                                                  start=True, stop=True))
                              g_qhkh = tmm
                              p.wait("vector", tmm)
                              p.op("vector", lambda e, bk=bk, qc=qc: e.tensor_tensor(
                                  out=bk[:, 0:256], in0=bk[:, 0:256], in1=cmask[:, qc, :], op=ALU.add))
                              tmx = p.op("vector", lambda e, bk=bk: e.reduce_max(
                                  out=scols[:, 0:1], in_=bk[:, 0:256], axis=AX.X, negate=True))
                              if FENCE_MODE == "recip": p.fence("vector")
                              p.waits("scalar", [tmx] + ([g_probs] if g_probs else []))
                              tex = p.op("scalar", lambda e, bk=bk: e.activation(
                                  out=probs[:], in_=bk[:, 0:256], func=AF.Exp,
                                  bias=scols[:, 0:1], accum_out=scols[:, 1:2]))
                              bank_free[qc] = tex
                              p.waits("vector", [tex, g_probsn])
                              p.op("vector", lambda e: e.reciprocal(out=scols[:, 1:2], in_=scols[:, 1:2]))
                              if FENCE_MODE == "recip": p.fence("vector")
                              tno = p.op("vector", lambda e: e.tensor_scalar_mul(
                                  out=probsn[:], in0=probs[:], scalar1=scols[:, 1:2]))
                              g_probs = tno
                              for sc in range(qc + 1):
                                  bT = psT[sc % 2]
                                  p.waits("tensor", [tno, bankT_free[sc % 2], g_pT])
                                  ttr = p.op("tensor", lambda e, bT=bT, sc=sc: e.transpose(
                                      out=bT[:, 0:128], in_=probsn[:, sc * 128:(sc + 1) * 128], identity=eyeb[:]))
                                  g_probsn = ttr
                                  p.wait("vector", ttr)
                                  tevT = p.op("vector", lambda e, bT=bT, qc=qc, sc=sc: e.tensor_copy(
                                      out=pT[:, qc + sc, :], in_=bT[:, 0:128]))
                                  bankT_free[sc % 2] = tevT
                          tpT = p.token("vector")
                          cbi = 2 + hh % 2
                          cb = psf[cbi]
                          p.waits("tensor", [tpT, bank_free[cbi]])
                          p.op("tensor", lambda e, cb=cb, b2=b2, hh=hh:
                               e.matmul(out=cb[0:64, 0:128],
                                        lhsT=vh[:, b2 * 2, hh * 64:hh * 64 + 64],
                                        rhs=pT[:, 0, :], start=True, stop=True), inc=0)
                          p.op("tensor", lambda e, cb=cb, b2=b2, hh=hh:
                               e.matmul(out=cb[0:64, 128:256],
                                        lhsT=vh[:, b2 * 2, hh * 64:hh * 64 + 64],
                                        rhs=pT[:, 1, :], start=True, stop=False), inc=0)
                          tcx = p.op("tensor", lambda e, cb=cb, b2=b2, hh=hh:
                                     e.matmul(out=cb[0:64, 128:256],
                                              lhsT=vh[:, b2 * 2 + 1, hh * 64:hh * 64 + 64],
                                              rhs=pT[:, 2, :], start=False, stop=True), inc=1)
                          g_pT = tcx
                          g_vh = tcx
                          p.waits("scalar", [tcx, g_ctx])
                          tevc = p.op("scalar", lambda e, cb=cb, hp=hp, eo=eo: e.activation(
                              out=ctxf[64 * eo:64 * eo + 64, hp, :], in_=cb[0:64, 0:256],
                              func=AF.Identity, bias=bv2[64 * eo:64 * eo + 64, hp:hp + 1]))
                          bank_free[cbi] = tevc
                      t_ctx = p.token("scalar")
                      for oc in range(4):
                          p.waits("tensor", [t_ctx, bank_free[4]])
                          for kc in range(4):
                              p.op("tensor", lambda e, oc=oc, kc=kc:
                                   e.matmul(out=psf[4][:, 0:256], lhsT=wob[:, kc, oc * 128:(oc + 1) * 128],
                                            rhs=ctxf[:, kc, :], start=(kc == 0), stop=(kc == 3)),
                                   inc=(1 if kc == 3 else 0))
                          tmm = p.token("tensor")
                          g_ctx = tmm
                          p.waits("scalar", [tmm, g_mha])
                          tev = p.op("scalar", lambda e, oc=oc: e.activation(
                              out=mha[:, oc, :], in_=psf[4][:, 0:256], func=AF.Identity, bias=bo2[:, oc:oc + 1]))
                          bank_free[4] = tev
                      t_mha = p.token("scalar")
                      for tc2 in range(2):
                          for dg in range(4):
                              bki = 4 + dg % 2
                              p.waits("tensor", [t_mha, bank_free[bki], t_pb])
                              for oc in range(4):
                                  p.op("tensor", lambda e, bki=bki, tc2=tc2, dg=dg, oc=oc:
                                       e.matmul(out=psf[bki][:], lhsT=mha[:, oc, tc2 * 128:(tc2 + 1) * 128],
                                                rhs=pwb[:, oc, dg * 512:(dg + 1) * 512],
                                                start=(oc == 0), stop=(oc == 3)),
                                       inc=(1 if oc == 3 else 0))
                              tmm = p.token("tensor")
                              g_mha = tmm
                              p.waits("vector", [tmm, g_sstg])
                              tev = p.op("vector", lambda e, bki=bki, dg=dg: e.tensor_tensor(
                                  out=sstg[:, dg, :], in0=psf[bki][:], in1=pbb[:, dg, :], op=ALU.add))
                              bank_free[bki] = tev
                          tevl = p.token("vector")
                          row0 = blk * 256 + tc2 * 128
                          p.wait("gpsimd", tevl)
                          g_sstg = p.op("gpsimd", lambda e, r=row0: e.dma_start(
                              out=s_scr[r:r + 128, :], in_=sstg[:].rearrange("p g n -> p (g n)")), dma="sst")

        p.barrier()

        with ExitStack() as e3:
          if PHASE_LIMIT >= 5:
              sb3 = lambda n, s, d: e3.enter_context(nc.sbuf_tensor(n, s, d))
              psB = [e3.enter_context(nc.psum_tensor(f"psB{i}", [128, 512], F32)) for i in range(8)]
              t_buf = sb3("t_buf", [128, NJC, 512], BF16)
              wslab = [sb3(f"wslab{i}", [128, 512], BF16) for i in range(4)]
              wdslab = [sb3(f"wdslab{i}", [128, D], BF16) for i in range(2)]
              sgb = sb3("sgb", [128, 512], BF16)
              uub = sb3("uub", [128, 512], BF16)
              q8b = sb3("q8b", [128, 512], FP8)
              tmx = sb3("tmx", [128, 512], F32)
              trow = sb3("trow", [1, 512], F32)
              qrow = sb3("qrow", [1, 512], F32)
              srow = sb3("srow", [1, 512], F32)
              qsbt = sb3("qsbt", [128, 512], F32)
              stbt = sb3("stbt", [128, 512], F32)
              sslab = sb3("sslab", [128, 512], BF16)
              xslab2 = [sb3(f"xslab2{i}", [128, 512], F32) for i in range(2)]
              omb = sb3("omb", [128, 512], F32)
              gyb = sb3("gyb", [128, 512], BF16)

              g_tbuf = None
              prev_wd = [None, None]
              g_sgb = None
              g_sslab = None
              g_x2 = [None, None]
              g_omb = None
              g_gyb = None

              bank_free = {i: None for i in range(8)}
              prev_w = [None] * 4
              for tau in range(NTAU):
                  for G in range(NG):
                      for d in range(NDC):
                          for mi, scr in enumerate((wgq_scr, wuq_scr)):
                              wb = wslab[(d * 2 + mi) % 4]
                              p.wait("sync", prev_w[(d * 2 + mi) % 4])
                              tld = p.op("sync", lambda e, b=wb, s=scr, d=d, G=G: e.dma_start(
                                  out=b[:], in_=s[d * 128:(d + 1) * 128, G * 512:(G + 1) * 512]), dma=f"ws{(d * 2 + mi) % 4}")
                              p.waits("tensor", [tld] + ([bank_free[i] for i in range(8)] + [g_tbuf] if d == 0 else []))
                              for cc in range(4):
                                  bk = psB[mi * 4 + cc]
                                  p.op("tensor", lambda e, bk=bk, b=wb, cc=cc, d=d, tau=tau:
                                       e.matmul(out=bk[:], lhsT=b[:, cc * 128:(cc + 1) * 128],
                                                rhs=qx_fm[:, d, tau * 512:(tau + 1) * 512],
                                                start=(d == 0), stop=(d == NDC - 1)),
                                       inc=(1 if cc == 3 else 0))
                              prev_w[(d * 2 + mi) % 4] = p.token("tensor")
                      t_gu = p.token("tensor")
                      for cc in range(4):
                          c = G * 4 + cc
                          p.waits("scalar", [t_gu, g_sgb])
                          tg = p.op("scalar", lambda e, cc=cc, c=c: e.activation(
                              out=sgb[:], in_=psB[cc][:], func=AF.Silu, scale=wsg[:, c:c + 1]))
                          tu = p.op("scalar", lambda e, cc=cc, c=c: e.activation(
                              out=uub[:], in_=psB[4 + cc][:], func=AF.Copy, scale=wsu[:, c:c + 1]))
                          p.waits("vector", [tu, g_tbuf])
                          tt2 = p.op("vector", lambda e, c=c: e.tensor_tensor(
                              out=t_buf[:, c, :], in0=sgb[:], in1=uub[:], op=ALU.mult))
                          g_sgb = tt2
                          bank_free[cc] = tt2
                          bank_free[4 + cc] = tt2
                  g_tbuf = None

                  p.op("vector", lambda e: e.tensor_reduce(
                      out=tmx[:], in_=t_buf[:].rearrange("p c s -> p s c"),
                      axis=AX.X, op=ALU.max, apply_absolute_value=True))
                  if FENCE_MODE == "recip": p.fence("vector")
                  t_amax = p.token("vector")
                  p.wait("gpsimd", t_amax)
                  t_crow = p.op("gpsimd", lambda e: e.tensor_reduce(
                      out=trow[:], in_=tmx[:], axis=AX.C, op=ALU.max))
                  p.wait("vector", t_crow)
                  p.op("vector", lambda e: e.reciprocal(out=qrow[:], in_=trow[:]))
                  if FENCE_MODE == "recip": p.fence("vector")
                  p.op("vector", lambda e: e.tensor_scalar_mul(out=qrow[:], in0=qrow[:], scalar1=224.0))
                  p.op("vector", lambda e: e.tensor_scalar_mul(out=srow[:], in0=trow[:], scalar1=1.0 / 224.0))
                  t_rows = p.token("vector")
                  for (row, dst) in ((qrow, qsbt), (srow, stbt)):
                      p.waits("tensor", [t_rows, tl["onesf"], bank_free[0]])
                      tb = p.op("tensor", lambda e, r=row: e.matmul(
                          out=psB[0][:], lhsT=onesf[:], rhs=r[:], start=True, stop=True))
                      p.wait("vector", tb)
                      tcp = p.op("vector", lambda e, ds=dst: e.tensor_copy(out=ds[:], in_=psB[0][:]))
                      bank_free[0] = tcp
                      p.wait("tensor", tcp)
                  for c in range(NJC):
                      p.op("vector", lambda e, c=c: e.tensor_tensor(
                          out=q8b[:], in0=t_buf[:, c, :], in1=qsbt[:], op=ALU.mult))
                      p.op("vector", lambda e, c=c: e.tensor_tensor(
                          out=t_buf[:, c, :], in0=q8b[:], in1=stbt[:], op=ALU.mult))
                  t_qt = p.token("vector")

                  for tcp2 in range(2):
                      for j in range(NJC):
                          wb = wdslab[j % 2]
                          p.wait("sync", prev_wd[j % 2])
                          tld = p.op("sync", lambda e, b=wb, j=j: e.dma_start(
                              out=b[:], in_=wdq_scr[j * 128:(j + 1) * 128, :]), dma=f"wdl{j % 2}")
                          p.waits("tensor", [tld, t_qt] + ([bank_free[i] for i in range(8)] if j == 0 else []))
                          for tc2 in range(2):
                              for hg in range(4):
                                  bk = psB[tc2 * 4 + hg]
                                  p.op("tensor", lambda e, bk=bk, b=wb, j=j, tcp2=tcp2, tc2=tc2, hg=hg:
                                       e.matmul(out=bk[:],
                                                lhsT=t_buf[:, j, (tcp2 * 2 + tc2) * 128:(tcp2 * 2 + tc2 + 1) * 128],
                                                rhs=b[:, hg * 512:(hg + 1) * 512],
                                                start=(j == 0), stop=(j == NJC - 1)),
                                       inc=(1 if tc2 == 1 and hg == 3 else 0))
                          prev_wd[j % 2] = p.token("tensor")
                      t_dn = p.token("tensor")
                      g_tbuf = t_dn
                      for tc2 in range(2):
                          row0 = (tau * 4 + tcp2 * 2 + tc2) * 128
                          for hg in range(4):
                              bk = psB[tc2 * 4 + hg]
                              p.waits("scalar", [g_x2[hg % 2]])
                              tlx = p.op("scalar", lambda e, b=xslab2[hg % 2], r=row0, hg=hg: e.dma_start(
                                  out=b[:], in_=h["x"][r:r + 128, hg * 512:(hg + 1) * 512]), dma=f"x2{hg % 2}")
                              p.waits("scalar", [g_sslab])
                              tls = p.op("scalar", lambda e, r=row0, hg=hg: e.dma_start(
                                  out=sslab[:], in_=s_scr[r:r + 128, hg * 512:(hg + 1) * 512]), dma="ssl")
                              p.waits("scalar", [t_dn, g_gyb])
                              tg = p.op("scalar", lambda e, bk=bk: e.activation(
                                  out=gyb[:], in_=bk[:], func=AF.Silu))
                              bank_free[tc2 * 4 + hg] = tg
                              p.waits("vector", [tg, tls, tlx, g_omb])
                              p.op("vector", lambda e: e.tensor_tensor(out=omb[:], in0=sslab[:], in1=gyb[:], op=ALU.mult))
                              g_sslab = p.token("vector")
                              g_gyb = g_sslab
                              tcb = p.op("vector", lambda e, b=xslab2[hg % 2]: e.tensor_tensor(
                                  out=omb[:], in0=omb[:], in1=b[:], op=ALU.add))
                              g_x2[hg % 2] = tcb
                              p.wait("gpsimd", tcb)
                              g_omb = p.op("gpsimd", lambda e, r=row0, hg=hg: e.dma_start(
                                  out=h["out"][r:r + 128, hg * 512:(hg + 1) * 512], in_=omb[:]), dma="out")


        if DEBUG_DUMP:
            p.barrier()
            if PHASE_LIMIT >= 1:
                p.op("sync", lambda e: e.dma_start(out=h["dbg_wg"], in_=wgq_scr[:, :]), dma="out")
                p.op("sync", lambda e: e.dma_start(out=h["dbg_wd"], in_=wdq_scr[:, :]), dma="out")
            if PHASE_LIMIT >= 2:
                p.op("sync", lambda e: e.dma_start(out=h["dbg_qx"], in_=qx_fm[:]), dma="out")
            if PHASE_LIMIT >= 4:
                p.op("sync", lambda e: e.dma_start(out=h["dbg_s"], in_=s_scr[:, :]), dma="out")
            p.items["sync"].append(("wait", "out", p.count["out"]))
        if "out" in p.count:
            p.items["gpsimd"].append(("wait", "out", p.count["out"]))
        else:
            for cname in list(p.chans):
                p.items["gpsimd"].append(("wait", cname, p.count[cname]))
        finish()
        with nc.Block() as block:
            p.emit(block, sems)
    return nc


_NC_CACHE = None


def _get_program():
    global _NC_CACHE
    if _NC_CACHE is None:
        _NC_CACHE = build_program()
    return _NC_CACHE


def kernel(**inputs):
    f32 = np.float32
    x = np.asarray(inputs["L0_attn_x"], f32)
    cw = np.ascontiguousarray(np.asarray(inputs["L0_attn_conv_w"], f32))
    cb2 = np.ascontiguousarray(np.asarray(inputs["L0_attn_conv_b"], f32).reshape(12, 128).T)
    wq2 = np.ascontiguousarray(np.asarray(inputs["L0_attn_wq"], f32).reshape(512, 512))
    wk2 = np.ascontiguousarray(np.asarray(inputs["L0_attn_wk"], f32).reshape(512, 512))
    wv2 = np.ascontiguousarray(np.asarray(inputs["L0_attn_wv"], f32).reshape(512, 512))
    wo2 = np.ascontiguousarray(np.asarray(inputs["L0_attn_wo"], f32).reshape(512, 512))
    bq2 = np.ascontiguousarray(np.asarray(inputs["L0_attn_bq"], f32).reshape(4, 128).T)
    bk2 = np.ascontiguousarray(np.asarray(inputs["L0_attn_bk"], f32).reshape(4, 128).T)
    bv2 = np.ascontiguousarray(np.asarray(inputs["L0_attn_bv"], f32).reshape(4, 128).T)
    bo2 = np.ascontiguousarray(np.asarray(inputs["L0_attn_bo"], f32).reshape(4, 128).T)
    pw = np.ascontiguousarray(np.asarray(inputs["L0_attn_proj_w"], f32))
    pbr = np.ascontiguousarray(np.asarray(inputs["L0_attn_proj_b"], f32).reshape(1, D))
    wg = np.ascontiguousarray(np.asarray(inputs["L0_ffn_w_gate"], f32))
    wu = np.ascontiguousarray(np.asarray(inputs["L0_ffn_w_up"], f32))
    wd = np.ascontiguousarray(np.asarray(inputs["L0_ffn_w_down"], f32))
    wgT = np.ascontiguousarray(wg.T)
    wuT = np.ascontiguousarray(wu.T)
    wdT = np.ascontiguousarray(wd.T)
    didx = np.ascontiguousarray(np.asarray(inputs["L0_ffn_down_reorder_index"], np.int32))
    eyef = np.eye(128, dtype=f32)
    onesf = np.ones((1, 128), f32)
    jj = np.arange(256)[None, :]
    pp = np.arange(128)[:, None]
    cmask = np.ascontiguousarray(np.stack([
        np.where(jj <= pp, 0.0, -1e9).astype(f32),
        np.where(jj <= 128 + pp, 0.0, -1e9).astype(f32)], axis=1))

    shared = dict(cw=cw, cb2=cb2, wq2=wq2, wk2=wk2, wv2=wv2, wo2=wo2,
                  bq2=bq2, bk2=bk2, bv2=bv2, bo2=bo2, pw=pw, pbr=pbr,
                  wg=wg, wu=wu, wd=wd, wgT=wgT, wuT=wuT, wdT=wdT,
                  didx=didx, eyef=eyef, onesf=onesf, cmask=cmask)
    in_maps = []
    for c in range(NCORES):
        seq, half = c // 2, c % 2
        t0 = half * T
        m = dict(shared)
        m["x"] = np.ascontiguousarray(x[seq, t0:t0 + T])
        m["xh"] = (np.zeros((HALO, D), f32) if half == 0
                   else np.ascontiguousarray(x[seq, t0 - HALO:t0]))
        in_maps.append(m)

    nc = _get_program()
    res = run_bass_kernel_spmd(nc, in_maps, core_ids=list(range(NCORES)))
    out = np.empty((4, 4096, D), f32)
    for c in range(NCORES):
        seq, half = c // 2, c % 2
        out[seq, half * T:(half + 1) * T] = np.asarray(res.results[c]["out"])
    return out


if __name__ == "__main__":
    build_program()
    print("built ok")


# version 06
# speedup vs baseline: 1.6032x; 1.0159x over previous
import numpy as np
import concourse.bass as bass
import concourse.mybir as mybir
from concourse.bass_utils import run_bass_kernel_spmd
from contextlib import ExitStack

F32 = mybir.dt.float32
BF16 = mybir.dt.bfloat16
FP8 = mybir.dt.float8e4
I32 = mybir.dt.int32
AF = mybir.ActivationFunctionType
ALU = mybir.AluOpType
AX = mybir.AxisListType

NCORES = 8
T = 2048
D = 2048
QKV = 1536
HD = 512
FF = 5632
NTC = 16
NTAU = 4
NDC = 16
NJC = 44
NG = 11
HALO = 16
CS = 2064

ENGINES = ("sync", "scalar", "vector", "tensor", "gpsimd")

DEBUG_SIM = False
SELF_WAIT_ENGINES = None
FENCE_MODE = "small"
SKIP_HALO = False
DEBUG_DUMP = False
PHASE_LIMIT = 5


class Prog:

    def __init__(self):
        self.items = {e: [] for e in ENGINES}
        self.count = {}
        for e in ENGINES:
            self.count[e] = 0
        self.chans = []
        self.bound = {e: {} for e in ENGINES}

    def chan(self, name):
        if name not in self.count:
            self.count[name] = 0
            self.chans.append(name)
        return name

    def op(self, eng, fn, dma=None, inc=None):
        if dma is not None:
            sem = self.chan(dma)
            amt = 16
        else:
            sem = eng
            amt = 1 if inc is None else inc
            if DEBUG_SIM and FENCE_MODE == "all" and (SELF_WAIT_ENGINES is None or eng in SELF_WAIT_ENGINES):
                amt = max(amt, 1)
                if self.count[eng] > 0:
                    self.items[eng].append(("wait", eng, self.count[eng]))
        self.items[eng].append(("op", fn, sem, amt))
        self.count[sem] += amt
        return (sem, self.count[sem]) if amt else None

    def fence(self, eng):
        if self.count.get(eng, 0) > 0:
            self.items[eng].append(("wait", eng, self.count[eng]))

    def wait(self, eng, tok):
        if tok is None:
            return
        sem, val = tok
        if sem == eng or self.bound[eng].get(sem, 0) >= val:
            return
        self.items[eng].append(("wait", sem, val))
        self.bound[eng][sem] = val

    def waits(self, eng, toks):
        for t in toks:
            self.wait(eng, t)

    def barrier(self):
        snap = dict(self.count)
        for e in ENGINES:
            for s, v in snap.items():
                if s != e and v > 0:
                    self.wait(e, (s, v))

    def token(self, sem):
        return (sem, self.count[sem]) if self.count.get(sem, 0) else None

    def emit(self, block, sems):
        def mk(ename):
            def emitter(eng):
                for it in self.items[ename]:
                    if it[0] == "wait":
                        eng.wait_ge(sems[it[1]], it[2])
                    else:
                        ins = it[1](eng)
                        if it[3]:
                            ins.then_inc(sems[it[2]], it[3])
            return emitter
        block.sync(mk("sync"))
        block.scalar(mk("scalar"))
        block.vector(mk("vector"))
        block.tensor(mk("tensor"))
        block.gpsimd(mk("gpsimd"))


def build_program():
    nc = bass.Bass("TRN2", target_bir_lowering=False, debug=False,
                   detect_race_conditions=DEBUG_SIM)
    h = {}

    def inp(name, shape, dt=F32):
        h[name] = nc.dram_tensor(name, shape, dt, kind="ExternalInput").ap()

    inp("x", [T, D]); inp("xh", [HALO, D])
    inp("cw", [3, D, QKV]); inp("cb2", [128, 12])
    for nm in ("wq2", "wk2", "wv2", "wo2"):
        inp(nm, [HD, HD])
    for nm in ("bq2", "bk2", "bv2", "bo2"):
        inp(nm, [128, 4])
    inp("pw", [HD, D]); inp("pbr", [1, D])
    inp("wg", [FF, D]); inp("wu", [FF, D]); inp("wd", [D, FF])
    inp("wgT", [D, FF]); inp("wuT", [D, FF]); inp("wdT", [FF, D])
    inp("didx", [FF], I32)
    inp("eyef", [128, 128]); inp("onesf", [1, 128]); inp("cmask", [128, 2, 256])
    h["out"] = nc.dram_tensor("out", [T, D], F32, kind="ExternalOutput").ap()
    if DEBUG_DUMP:
        h["dbg_s"] = nc.dram_tensor("dbg_s", [T, D], BF16, kind="ExternalOutput").ap()
        h["dbg_wg"] = nc.dram_tensor("dbg_wg", [D, FF], BF16, kind="ExternalOutput").ap()
        h["dbg_wd"] = nc.dram_tensor("dbg_wd", [FF, D], BF16, kind="ExternalOutput").ap()
        h["dbg_qx"] = nc.dram_tensor("dbg_qx", [128, NDC, T], BF16, kind="ExternalOutput").ap()

    xn_scr = nc.dram_tensor("xn_scr", [NDC, 128, CS], BF16, kind="Internal").ap()
    s_scr = nc.dram_tensor("s_scr", [T, D], BF16, kind="Internal").ap()
    wgq_scr = nc.dram_tensor("wgq_scr", [D, FF], BF16, kind="Internal").ap()
    wuq_scr = nc.dram_tensor("wuq_scr", [D, FF], BF16, kind="Internal").ap()
    wdq_scr = nc.dram_tensor("wdq_scr", [FF, D], BF16, kind="Internal").ap()

    p = Prog()
    es = ExitStack()
    with es:
        sems = {e: es.enter_context(nc.semaphore("sem_" + e)) for e in ENGINES}
        def finish():
            for cname in p.chans:
                sems[cname] = es.enter_context(nc.semaphore("sem_" + cname))
        sb = lambda name, shape, dt: es.enter_context(nc.sbuf_tensor(name, shape, dt))

        eyef = sb("c_eyef", [128, 128], F32)
        eyeb = sb("c_eyeb", [128, 128], BF16)
        onesf = sb("c_onesf", [1, 128], F32)
        cmask = sb("c_cmask", [128, 2, 256], F32)
        cb2 = sb("c_cb2", [128, 12], F32)
        bq8 = sb("c_bq8", [128, 4], F32)
        bk2 = sb("c_bk2", [128, 4], F32)
        bv2 = sb("c_bv2", [128, 4], F32)
        bo2 = sb("c_bo2", [128, 4], F32)
        wsg = sb("c_wsg", [128, NJC], F32)
        wsu = sb("c_wsu", [128, NJC], F32)
        pbr_sb = sb("pbr_sb", [1, D], F32)
        epsc = sb("c_eps", [128, 1], F32)

        tl = {}
        for nm, dst in (("eyef", eyef), ("onesf", onesf), ("cmask", cmask),
                        ("cb2", cb2), ("bk2", bk2), ("bv2", bv2), ("bo2", bo2),
                        ("pbr", pbr_sb)):
            tl[nm] = p.op("sync", lambda e, d=dst, a=h[nm]: e.dma_start(out=d[:], in_=a[:]), dma="ld_" + nm)
        tl["bq2"] = p.op("sync", lambda e: e.dma_start(out=bq8[:], in_=h["bq2"][:]), dma="ld_bq2")
        p.wait("vector", tl["bq2"])
        p.op("vector", lambda e: e.tensor_scalar_mul(out=bq8[:], in0=bq8[:], scalar1=0.125))
        if FENCE_MODE in ("recip", "small"): p.fence("vector")
        t_bq8 = p.token("vector")
        p.wait("scalar", tl["eyef"])
        t_eyeb = p.op("scalar", lambda e: e.copy(out=eyeb[:], in_=eyef[:]))
        t_eps = p.op("vector", lambda e: e.memset(epsc[:], 1e-5))

        with ExitStack() as ew:
          if PHASE_LIMIT >= 1:
              sbw = lambda n, s, d: ew.enter_context(nc.sbuf_tensor(n, s, d))
              psW = [ew.enter_context(nc.psum_tensor(f"psW{i}", [128, 512], F32)) for i in range(2)]
              idx_s = sbw("idx_s", [128, NJC], I32)
              qrow_g = sbw("qrow_g", [1, FF], F32)
              qrow_u = sbw("qrow_u", [1, FF], F32)
              qrow_d = sbw("qrow_d", [1, D], F32)
              wrow_d = sbw("wrow_d", [1, D], F32)
              wnat = [sbw(f"wnat{i}", [128, FF], F32) for i in range(2)]
              rcol = sbw("rcol", [128, 4], F32)
              wTs = [sbw(f"wTs{i}", [128, 512], F32) for i in range(2)]
              q8t = sbw("q8t", [128, 512], FP8)
              qbf = [sbw(f"qbf{i}", [128, 512], BF16) for i in range(2)]
              qsb = sbw("qsb", [128, 512], F32)
              qsdb = [sbw(f"qsdb{i}", [128, 512], F32) for i in range(4)]
              wsdb = [sbw(f"wsdb{i}", [128, 512], F32) for i in range(4)]
              wdqt = [sbw(f"wdqt{i}", [128, D], BF16) for i in range(2)]

              t_idx = p.op("sync", lambda e: e.dma_start(
                  out=idx_s[:], in_=h["didx"].rearrange("(c p) -> p c", p=128),
                  allow_slow_non_contiguous=True), dma="ld_idx")

              prev_wnat = [None, None]

              def w_maxes(src, n_chunks, row_len, qrow, wscol, wrow):
                  prev = prev_wnat
                  t_rcol_free = None
                  for c in range(n_chunks):
                      buf = wnat[c % 2]
                      p.wait("sync", prev[c % 2])
                      tld = p.op("sync", lambda e, b=buf, s=src, c=c, L=row_len: e.dma_start(
                          out=b[:, 0:L], in_=s[c * 128:(c + 1) * 128, :]), dma=f"wn{c % 2}")
                      p.waits("vector", [tld, t_rcol_free])
                      p.op("vector", lambda e, b=buf, L=row_len: e.reduce_max(
                          out=rcol[:, 0:1], in_=b[:, 0:L], axis=AX.X, apply_absolute_value=True))
                      if FENCE_MODE in ("recip", "small"): p.fence("vector")
                      prev[c % 2] = p.token("vector")
                      p.op("vector", lambda e: e.reciprocal(out=rcol[:, 1:2], in_=rcol[:, 0:1]))
                      if FENCE_MODE in ("recip", "small"): p.fence("vector")
                      p.op("vector", lambda e: e.tensor_scalar_mul(out=rcol[:, 2:3], in0=rcol[:, 1:2], scalar1=224.0))
                      if FENCE_MODE in ("recip", "small"): p.fence("vector")
                      if wscol is not None:
                          p.op("vector", lambda e, c=c: e.tensor_scalar_mul(
                              out=wscol[:, c:c + 1], in0=rcol[:, 0:1], scalar1=1.0 / 224.0))
                          if FENCE_MODE in ("recip", "small"): p.fence("vector")
                      else:
                          p.op("vector", lambda e: e.tensor_scalar_mul(
                              out=rcol[:, 3:4], in0=rcol[:, 0:1], scalar1=1.0 / 224.0))
                          if FENCE_MODE in ("recip", "small"): p.fence("vector")
                      tv = p.token("vector")
                      p.waits("tensor", [tv, tl["eyef"]])
                      tt = p.op("tensor", lambda e: e.transpose(
                          out=psW[0][0:1, 0:128], in_=rcol[:, 2:3], identity=eyef[:]))
                      t_rcol_free = tt
                      p.wait("vector", tt)
                      p.op("vector", lambda e, q=qrow, c=c: e.tensor_copy(
                          out=q[:, c * 128:(c + 1) * 128], in_=psW[0][0:1, 0:128]))
                      if FENCE_MODE in ("recip", "small"): p.fence("vector")
                      tpe = p.token("vector")
                      p.wait("tensor", tpe)
                      if wscol is None:
                          tt2 = p.op("tensor", lambda e: e.transpose(
                              out=psW[0][0:1, 0:128], in_=rcol[:, 3:4], identity=eyef[:]))
                          t_rcol_free = tt2
                          p.wait("vector", tt2)
                          p.op("vector", lambda e, w=wrow, c=c: e.tensor_copy(
                              out=w[:, c * 128:(c + 1) * 128], in_=psW[0][0:1, 0:128]))
                          if FENCE_MODE in ("recip", "small"): p.fence("vector")
                          tpe = p.token("vector")
                          p.wait("tensor", tpe)
                  return p.token("vector")

              t_mg = w_maxes(h["wg"], NJC, D, qrow_g, wsg, None)
              t_mu = w_maxes(h["wu"], NJC, D, qrow_u, wsu, None)
              t_md = w_maxes(h["wd"], NDC, FF, qrow_d, None, wrow_d)

              wq_state = {"prev8": [None, None], "prev_st": [None, None], "t_q8_free": None}

              def w_quant(srcT, qrow, dst_scr, qdep):
                  prev8 = wq_state["prev8"]
                  prev_st = wq_state["prev_st"]
                  t_q8_free = wq_state["t_q8_free"]
                  for G in range(NG):
                      p.waits("tensor", [qdep, tl["onesf"]])
                      tb = p.op("tensor", lambda e, G=G, q=qrow: e.matmul(
                          out=psW[1][:], lhsT=onesf[:], rhs=q[:, G * 512:(G + 1) * 512],
                          start=True, stop=True))
                      p.wait("vector", tb)
                      tq = p.op("vector", lambda e: e.tensor_copy(out=qsb[:], in_=psW[1][:]))
                      if FENCE_MODE in ("recip", "small"): p.fence("vector")
                      p.wait("tensor", tq)
                      for d in range(NDC):
                          buf = wTs[d % 2]
                          p.wait("sync", prev8[d % 2])
                          tld = p.op("sync", lambda e, b=buf, s=srcT, d=d, G=G: e.dma_start(
                              out=b[:], in_=s[d * 128:(d + 1) * 128, G * 512:(G + 1) * 512]), dma=f"wt{d % 2}")
                          p.waits("vector", [tld, t_q8_free])
                          t8 = p.op("vector", lambda e, b=buf: e.tensor_tensor(
                              out=q8t[:], in0=b[:], in1=qsb[:], op=ALU.mult))
                          prev8[d % 2] = t8
                          ob = qbf[d % 2]
                          p.waits("scalar", [t8, prev_st[d % 2]])
                          tc = p.op("scalar", lambda e, o=ob: e.copy(out=o[:], in_=q8t[:]))
                          t_q8_free = tc
                          wq_state["t_q8_free"] = tc
                          p.wait("sync", tc)
                          ts = p.op("sync", lambda e, o=ob, ds=dst_scr, d=d, G=G: e.dma_start(
                              out=ds[d * 128:(d + 1) * 128, G * 512:(G + 1) * 512], in_=o[:]), dma=f"wqs{d % 2}")
                          prev_st[d % 2] = ts
                  return p.token("sync")

              w_quant(h["wgT"], qrow_g, wgq_scr, t_mg)
              w_quant(h["wuT"], qrow_u, wuq_scr, t_mu)

              t_b4 = None
              for hg in range(4):
                  for src_row, dstl in ((qrow_d, qsdb), (wrow_d, wsdb)):
                      p.waits("tensor", [t_md, tl["onesf"], t_b4])
                      tb = p.op("tensor", lambda e, hg=hg, r=src_row: e.matmul(
                          out=psW[1][:], lhsT=onesf[:], rhs=r[:, hg * 512:(hg + 1) * 512],
                          start=True, stop=True))
                      p.wait("vector", tb)
                      t_b4 = p.op("vector", lambda e, hg=hg, dl=dstl: e.tensor_copy(out=dl[hg][:], in_=psW[1][:]))
                      if FENCE_MODE in ("recip", "small"): p.fence("vector")
                      p.wait("tensor", t_b4)

              prev_ld = prev_wnat
              prev_sc = [None, None]
              for c in range(NJC):
                  buf = wnat[c % 2]
                  p.wait("sync", prev_ld[c % 2])
                  tld = p.op("sync", lambda e, b=buf, c=c: e.dma_start(
                      out=b[:, 0:D], in_=h["wdT"][c * 128:(c + 1) * 128, :]), dma=f"wn{c % 2}")
                  ob = wdqt[c % 2]
                  p.waits("vector", [tld, t_b4, prev_sc[c % 2]])
                  for hg in range(4):
                      p.op("vector", lambda e, b=buf, hg=hg: e.tensor_tensor(
                          out=q8t[:], in0=b[:, hg * 512:(hg + 1) * 512], in1=qsdb[hg][:], op=ALU.mult))
                      p.op("vector", lambda e, o=ob, hg=hg: e.tensor_tensor(
                          out=o[:, hg * 512:(hg + 1) * 512], in0=q8t[:], in1=wsdb[hg][:], op=ALU.mult))
                  tf = p.token("vector")
                  prev_ld[c % 2] = tf
                  p.waits("gpsimd", [tf, t_idx, prev_sc[c % 2]])
                  tsc = p.op("gpsimd", lambda e, o=ob, c=c: e.indirect_dma_start(
                      out=wdq_scr[:, :],
                      out_offset=bass.IndirectOffsetOnAxis(ap=idx_s[:, c:c + 1], axis=0),
                      in_=o[:], in_offset=None), dma=f"sca{c % 2}")
                  prev_sc[c % 2] = tsc

        p.barrier()

        qx_fm = sb("qx_fm", [128, NDC, T], BF16)

        with ExitStack() as e0:
          if PHASE_LIMIT >= 2:
              sb0 = lambda n, s, d: e0.enter_context(nc.sbuf_tensor(n, s, d))
              psb = [e0.enter_context(nc.psum_tensor(f"ps0b{i}", [128, 512], BF16)) for i in range(2)]
              xt = [sb0(f"xt{i}", [128, D], F32) for i in range(2)]
              xnf = sb0("xnf", [128, D], F32)
              junk = sb0("junk", [128, D], BF16)
              xnb = sb0("xnb", [128, D], BF16)
              q8x = sb0("q8x", [128, D], FP8)
              qxb = sb0("qxb", [128, D], BF16)
              cols = sb0("cols", [128, 8], F32)
              stg = sb0("stg", [128, NDC, 128], BF16)

              guard_xt = [[], []]
              g_xnf = []
              g_q8x = []
              g_qxb = []
              g_xnb = []
              g_stg = None
              bank_tok = [None, None]

              for tc in range(NTC if SKIP_HALO else NTC + 1):
                  halo = (tc == NTC)
                  P = HALO if halo else 128
                  buf = xt[tc % 2]
                  src = h["xh"][:] if halo else h["x"][tc * 128:(tc + 1) * 128, :]
                  p.waits("sync", guard_xt[tc % 2])
                  tld = p.op("sync", lambda e, b=buf, s=src, P=P: e.dma_start(out=b[0:P, :], in_=s), dma=f"xt{tc % 2}")
                  p.waits("vector", [tld] + g_xnf)
                  p.op("vector", lambda e, b=buf, P=P: e.reduce_sum(out=cols[0:P, 0:1], in_=b[0:P, :], axis=AX.X))
                  if FENCE_MODE in ("recip", "small"): p.fence("vector")
                  p.wait("scalar", tld)
                  tsq = p.op("scalar", lambda e, b=buf, P=P: e.activation(
                      out=junk[0:P, :], in_=b[0:P, :], func=AF.Square, accum_out=cols[0:P, 1:2]))
                  p.wait("vector", tsq)
                  p.op("vector", lambda e, P=P: e.tensor_scalar_mul(out=cols[0:P, 2:3], in0=cols[0:P, 0:1], scalar1=1.0 / D))
                  if FENCE_MODE in ("recip", "small"): p.fence("vector")
                  p.op("vector", lambda e, P=P: e.tensor_scalar_mul(out=cols[0:P, 3:4], in0=cols[0:P, 1:2], scalar1=1.0 / D))
                  if FENCE_MODE in ("recip", "small"): p.fence("vector")
                  p.op("vector", lambda e, P=P: e.tensor_tensor(out=cols[0:P, 4:5], in0=cols[0:P, 2:3], in1=cols[0:P, 2:3], op=ALU.mult))
                  if FENCE_MODE in ("recip", "small"): p.fence("vector")
                  tv = p.op("vector", lambda e, P=P: e.tensor_tensor(out=cols[0:P, 4:5], in0=cols[0:P, 3:4], in1=cols[0:P, 4:5], op=ALU.subtract))
                  if FENCE_MODE in ("recip", "small"): p.fence("vector")
                  p.waits("scalar", [tv, t_eps])
                  tsd = p.op("scalar", lambda e, P=P: e.activation(out=cols[0:P, 5:6], in_=cols[0:P, 4:5], func=AF.Sqrt, bias=epsc[0:P, 0:1]))
                  p.wait("vector", tsd)
                  p.op("vector", lambda e, P=P: e.reciprocal(out=cols[0:P, 6:7], in_=cols[0:P, 5:6]))
                  if FENCE_MODE in ("recip", "small"): p.fence("vector")
                  txn = p.op("vector", lambda e, b=buf, P=P: e.tensor_scalar(
                      out=xnf[0:P, :], in0=b[0:P, :], scalar1=cols[0:P, 2:3], scalar2=cols[0:P, 6:7],
                      op0=ALU.subtract, op1=ALU.mult))
                  guard_xt[tc % 2] = [txn, tsq]
                  p.wait("scalar", txn)
                  txb = p.op("scalar", lambda e, P=P: e.copy(out=xnb[0:P, :], in_=xnf[0:P, :]))
                  g_xnf = [txb]

                  if not halo:
                      p.op("vector", lambda e: e.reduce_max(out=cols[:, 0:1], in_=xnf[:], axis=AX.X, apply_absolute_value=True))
                      if FENCE_MODE in ("recip", "small"): p.fence("vector")
                      p.op("vector", lambda e: e.reciprocal(out=cols[:, 1:2], in_=cols[:, 0:1]))
                      if FENCE_MODE in ("recip", "small"): p.fence("vector")
                      tqs = p.op("vector", lambda e: e.tensor_scalar_mul(out=cols[:, 2:3], in0=cols[:, 1:2], scalar1=224.0))
                      if FENCE_MODE in ("recip", "small"): p.fence("vector")
                      tsx = p.op("vector", lambda e: e.tensor_scalar_mul(out=cols[:, 3:4], in0=cols[:, 0:1], scalar1=1.0 / 224.0))
                      if FENCE_MODE in ("recip", "small"): p.fence("vector")
                      p.waits("scalar", [tqs] + g_q8x)
                      t8 = p.op("scalar", lambda e: e.activation(out=q8x[:], in_=xnf[:], func=AF.Copy, scale=cols[:, 2:3]))
                      g_xnf = [t8]
                      p.waits("vector", [t8, tsx] + g_qxb)
                      tqx = p.op("vector", lambda e: e.tensor_scalar_mul(out=qxb[:], in0=q8x[:], scalar1=cols[:, 3:4]))
                      g_q8x = [tqx]

                  ev = []
                  for d in range(NDC):
                      bank = psb[d % 2]
                      p.waits("tensor", [txb, t_eyeb, bank_tok[d % 2], g_stg])
                      tt = p.op("tensor", lambda e, d=d, b=bank, P=P: e.transpose(
                          out=b[:, 0:P], in_=xnb[0:P, d * 128:(d + 1) * 128], identity=eyeb[0:P, 0:P]))
                      if d == NDC - 1:
                          g_xnb = [tt]
                      p.wait("scalar", tt)
                      te = p.op("scalar", lambda e, d=d, b=bank, P=P: e.copy(out=stg[:, d, 0:P], in_=b[:, 0:P]))
                      ev.append(te)
                      bank_tok[d % 2] = te
                  for d in range(NDC):
                      p.wait("sync", ev[d])
                      cst = 0 if halo else HALO + tc * 128
                      p.op("sync", lambda e, d=d, c=cst, P=P: e.dma_start(
                          out=xn_scr[d, :, c:c + P], in_=stg[:, d, 0:P]), dma="stg")
                  g_stg = p.token("stg")

                  if not halo:
                      ev2 = []
                      for d in range(NDC):
                          bank = psb[d % 2]
                          p.waits("tensor", [tqx, bank_tok[d % 2]])
                          tt = p.op("tensor", lambda e, d=d, b=bank: e.transpose(
                              out=b[:, 128:256], in_=qxb[:, d * 128:(d + 1) * 128], identity=eyeb[:]))
                          if d == NDC - 1:
                              g_qxb = [tt]
                          p.wait("vector", tt)
                          te = p.op("vector", lambda e, d=d, tc=tc, b=bank: e.tensor_copy(
                              out=qx_fm[:, d, tc * 128:(tc + 1) * 128], in_=b[:, 128:256]))
                          ev2.append(te)
                          bank_tok[d % 2] = te

        p.barrier()

        qkv_fm = sb("qkv_fm", [128, 12, T], BF16)
        with ExitStack() as e1:
          if PHASE_LIMIT >= 3:
              sb1 = lambda n, s, d: e1.enter_context(nc.sbuf_tensor(n, s, d))
              psA = [e1.enter_context(nc.psum_tensor(f"psA{i}", [128, 512], F32)) for i in range(8)]
              xslab = [sb1(f"xslab{i}", [128, CS], BF16) for i in range(2)]
              cwf = [sb1(f"cwf{i}", [128, 256], F32) for i in range(2)]
              cwb = [sb1(f"cwb{i}", [128, 256], BF16) for i in range(2)]

              prev_ev = None
              prev_cw = [None, None]
              prev_cwf = [None, None]
              prev_x = [None, None]
              for ccg in range(6):
                  step = 0
                  last_mm = None
                  for d in range(NDC):
                      bx = xslab[d % 2]
                      p.wait("sync", prev_x[d % 2])
                      tlx = p.op("sync", lambda e, b=bx, d=d: e.dma_start(out=b[:], in_=xn_scr[d, :, :]), dma=f"xs{d % 2}")
                      for w in range(3):
                          bcf = cwf[step % 2]
                          bcb = cwb[step % 2]
                          p.wait("sync", prev_cwf[step % 2])
                          tlw = p.op("sync", lambda e, b=bcf, w=w, d=d, g=ccg: e.dma_start(
                              out=b[:], in_=h["cw"][w, d * 128:(d + 1) * 128, g * 256:(g + 1) * 256]), dma=f"cw{step % 2}")
                          p.waits("scalar", [tlw, prev_cw[step % 2]])
                          tcw = p.op("scalar", lambda e, bf=bcf, bb=bcb: e.copy(out=bb[:], in_=bf[:]))
                          prev_cwf[step % 2] = tcw
                          p.waits("tensor", [tcw, tlx, prev_ev])
                          first = (d == 0 and w == 0)
                          lastg = (d == NDC - 1 and w == 2)
                          for cc2 in range(2):
                              for tau in range(NTAU):
                                  bk = psA[cc2 * 4 + tau]
                                  isl = (cc2 == 1 and tau == NTAU - 1)
                                  tok = p.op("tensor", lambda e, bk=bk, bb=bcb, bx=bx, cc2=cc2, tau=tau, w=w, f=first, L=lastg:
                                             e.matmul(out=bk[:], lhsT=bb[:, cc2 * 128:(cc2 + 1) * 128],
                                                      rhs=bx[:, HALO + tau * 512 + (w - 2): HALO + tau * 512 + (w - 2) + 512],
                                                      start=f, stop=L),
                                             inc=(1 if isl else 0))
                                  if isl:
                                      last_mm = tok
                          prev_cw[step % 2] = last_mm
                          step += 1
                      prev_x[d % 2] = last_mm
                  p.waits("scalar", [last_mm, tl["cb2"]])
                  for cc2 in range(2):
                      for tau in range(NTAU):
                          bk = psA[cc2 * 4 + tau]
                          cc = ccg * 2 + cc2
                          p.op("scalar", lambda e, bk=bk, cc=cc, tau=tau: e.activation(
                              out=qkv_fm[:, cc, tau * 512:(tau + 1) * 512], in_=bk[:],
                              func=AF.Identity, bias=cb2[:, cc:cc + 1]))
                  prev_ev = p.token("scalar")

        p.barrier()

        with ExitStack() as e2:
          if PHASE_LIMIT >= 4:
              sb2 = lambda n, s, d: e2.enter_context(nc.sbuf_tensor(n, s, d))
              psf = [e2.enter_context(nc.psum_tensor(f"psf{i}", [128, 512], F32)) for i in range(6)]
              psT = [e2.enter_context(nc.psum_tensor(f"psT{i}", [128, 512], BF16)) for i in range(2)]
              wqb = sb2("wqb", [128, 4, 512], BF16)
              wkb = sb2("wkb", [128, 4, 512], BF16)
              wvb = sb2("wvb", [128, 4, 512], BF16)
              wob = sb2("wob", [128, 4, 512], BF16)
              pwb = sb2("pwb", [128, 4, D], BF16)
              pbb = sb2("pbb", [128, 4, 512], F32)
              wtmp = sb2("wtmp", [128, D], F32)
              qh = sb2("qh", [128, 4, 512], BF16)
              kh = sb2("kh", [128, 4, 512], BF16)
              vh = sb2("vh", [128, 4, 512], BF16)
              probs = sb2("probs", [128, 256], BF16)
              probsn = sb2("probsn", [128, 256], BF16)
              pT = sb2("pT", [128, 3, 128], BF16)
              ctxf = sb2("ctxf", [128, 4, 256], BF16)
              mha = sb2("mha", [128, 4, 256], BF16)
              sstg = sb2("sstg", [128, 4, 512], BF16)
              scols = sb2("scols", [128, 2], F32)

              t_wcast = None
              for nm, dst in (("wq2", wqb), ("wk2", wkb), ("wv2", wvb), ("wo2", wob)):
                  for c in range(4):
                      p.wait("sync", t_wcast)
                      tld = p.op("sync", lambda e, a=h[nm], c=c: e.dma_start(
                          out=wtmp[:, 0:512], in_=a[c * 128:(c + 1) * 128, :]), dma="wtm")
                      p.waits("scalar", [tld, t_wcast])
                      t_wcast = p.op("scalar", lambda e, d=dst, c=c: e.copy(out=d[:, c, :], in_=wtmp[:, 0:512]))
              for c in range(4):
                  p.wait("sync", t_wcast)
                  tld = p.op("sync", lambda e, c=c: e.dma_start(
                      out=wtmp[:], in_=h["pw"][c * 128:(c + 1) * 128, :]), dma="wtm")
                  p.waits("scalar", [tld, t_wcast])
                  t_wcast = p.op("scalar", lambda e, c=c: e.copy(out=pwb[:, c, :], in_=wtmp[:]))

              t_pb = None
              for dg in range(4):
                  p.waits("tensor", [tl["onesf"], tl["pbr"], t_pb])
                  tb = p.op("tensor", lambda e, dg=dg: e.matmul(
                      out=psf[5][:], lhsT=onesf[:], rhs=pbr_sb[:, dg * 512:(dg + 1) * 512],
                      start=True, stop=True))
                  p.wait("vector", tb)
                  t_pb = p.op("vector", lambda e, dg=dg: e.tensor_copy(out=pbb[:, dg, :], in_=psf[5][:]))
                  if FENCE_MODE in ("recip", "small"): p.fence("vector")
                  p.wait("tensor", t_pb)

              bank_free = {i: None for i in range(6)}
              bankT_free = {i: None for i in range(2)}
              g_qhkh = None
              g_vh = None
              g_probs = None
              g_probsn = None
              g_pT = None
              g_ctx = None
              g_mha = None
              g_sstg = None

              for tau in range(NTAU):
                  for (mat, bias, outb, scl, qr) in ((wqb, bq8, qh, 0.125, 0), (wkb, bk2, kh, 1.0, 1)):
                      for c in range(4):
                          bk = psf[c % 2]
                          p.waits("tensor", [t_wcast, bank_free[c % 2]])
                          for kc in range(4):
                              p.op("tensor", lambda e, bk=bk, m=mat, c=c, kc=kc, tau=tau, qr=qr:
                                   e.matmul(out=bk[:], lhsT=m[:, kc, c * 128:(c + 1) * 128],
                                            rhs=qkv_fm[:, qr * 4 + kc, tau * 512:(tau + 1) * 512],
                                            start=(kc == 0), stop=(kc == 3)),
                                   inc=(1 if kc == 3 else 0))
                          tmm = p.token("tensor")
                          p.waits("scalar", [tmm, t_bq8, g_qhkh])
                          tev = p.op("scalar", lambda e, bk=bk, o=outb, c=c, b=bias, s=scl: e.activation(
                              out=o[:, c, :], in_=bk[:], func=AF.Identity, bias=b[:, c:c + 1], scale=s))
                          bank_free[c % 2] = tev
                  for tc2 in range(4):
                      bk = psf[2 + tc2 % 2]
                      p.waits("tensor", [t_wcast, bank_free[2 + tc2 % 2]])
                      for kc in range(4):
                          p.op("tensor", lambda e, bk=bk, tc2=tc2, kc=kc, tau=tau:
                               e.matmul(out=bk[:], lhsT=qkv_fm[:, 8 + kc, tau * 512 + tc2 * 128: tau * 512 + (tc2 + 1) * 128],
                                        rhs=wvb[:, kc, :], start=(kc == 0), stop=(kc == 3)),
                               inc=(1 if kc == 3 else 0))
                      tmm = p.token("tensor")
                      p.waits("scalar", [tmm, g_vh])
                      tev = p.op("scalar", lambda e, bk=bk, tc2=tc2: e.copy(out=vh[:, tc2, :], in_=bk[:]))
                      bank_free[2 + tc2 % 2] = tev

                  for b2 in range(2):
                      blk = tau * 2 + b2
                      for hh in range(8):
                          hp, eo = hh // 2, hh % 2
                          hd0 = 64 * eo
                          for qc in range(2):
                              bk = psf[qc]
                              p.waits("tensor", [bank_free[qc], tl["cmask"]])
                              tmm = p.op("tensor", lambda e, bk=bk, hp=hp, hd0=hd0, qc=qc, b2=b2:
                                         e.matmul(out=bk[:, 0:256],
                                                  lhsT=qh[hd0:hd0 + 64, hp, b2 * 256 + qc * 128: b2 * 256 + (qc + 1) * 128],
                                                  rhs=kh[hd0:hd0 + 64, hp, b2 * 256:(b2 + 1) * 256],
                                                  start=True, stop=True))
                              g_qhkh = tmm
                              p.wait("vector", tmm)
                              p.op("vector", lambda e, bk=bk, qc=qc: e.tensor_tensor(
                                  out=bk[:, 0:256], in0=bk[:, 0:256], in1=cmask[:, qc, :], op=ALU.add))
                              tmx = p.op("vector", lambda e, bk=bk: e.reduce_max(
                                  out=scols[:, 0:1], in_=bk[:, 0:256], axis=AX.X, negate=True))
                              if FENCE_MODE in ("recip", "small"): p.fence("vector")
                              p.waits("scalar", [tmx] + ([g_probs] if g_probs else []))
                              tex = p.op("scalar", lambda e, bk=bk: e.activation(
                                  out=probs[:], in_=bk[:, 0:256], func=AF.Exp,
                                  bias=scols[:, 0:1], accum_out=scols[:, 1:2]))
                              bank_free[qc] = tex
                              p.waits("vector", [tex, g_probsn])
                              p.op("vector", lambda e: e.reciprocal(out=scols[:, 1:2], in_=scols[:, 1:2]))
                              if FENCE_MODE in ("recip", "small"): p.fence("vector")
                              tno = p.op("vector", lambda e: e.tensor_scalar_mul(
                                  out=probsn[:], in0=probs[:], scalar1=scols[:, 1:2]))
                              g_probs = tno
                              for sc in range(qc + 1):
                                  bT = psT[sc % 2]
                                  p.waits("tensor", [tno, bankT_free[sc % 2], g_pT])
                                  ttr = p.op("tensor", lambda e, bT=bT, sc=sc: e.transpose(
                                      out=bT[:, 0:128], in_=probsn[:, sc * 128:(sc + 1) * 128], identity=eyeb[:]))
                                  g_probsn = ttr
                                  p.wait("vector", ttr)
                                  tevT = p.op("vector", lambda e, bT=bT, qc=qc, sc=sc: e.tensor_copy(
                                      out=pT[:, qc + sc, :], in_=bT[:, 0:128]))
                                  bankT_free[sc % 2] = tevT
                          tpT = p.token("vector")
                          cbi = 2 + hh % 2
                          cb = psf[cbi]
                          p.waits("tensor", [tpT, bank_free[cbi]])
                          p.op("tensor", lambda e, cb=cb, b2=b2, hh=hh:
                               e.matmul(out=cb[0:64, 0:128],
                                        lhsT=vh[:, b2 * 2, hh * 64:hh * 64 + 64],
                                        rhs=pT[:, 0, :], start=True, stop=True), inc=0)
                          p.op("tensor", lambda e, cb=cb, b2=b2, hh=hh:
                               e.matmul(out=cb[0:64, 128:256],
                                        lhsT=vh[:, b2 * 2, hh * 64:hh * 64 + 64],
                                        rhs=pT[:, 1, :], start=True, stop=False), inc=0)
                          tcx = p.op("tensor", lambda e, cb=cb, b2=b2, hh=hh:
                                     e.matmul(out=cb[0:64, 128:256],
                                              lhsT=vh[:, b2 * 2 + 1, hh * 64:hh * 64 + 64],
                                              rhs=pT[:, 2, :], start=False, stop=True), inc=1)
                          g_pT = tcx
                          g_vh = tcx
                          p.waits("scalar", [tcx, g_ctx])
                          tevc = p.op("scalar", lambda e, cb=cb, hp=hp, eo=eo: e.activation(
                              out=ctxf[64 * eo:64 * eo + 64, hp, :], in_=cb[0:64, 0:256],
                              func=AF.Identity, bias=bv2[64 * eo:64 * eo + 64, hp:hp + 1]))
                          bank_free[cbi] = tevc
                      t_ctx = p.token("scalar")
                      for oc in range(4):
                          p.waits("tensor", [t_ctx, bank_free[4]])
                          for kc in range(4):
                              p.op("tensor", lambda e, oc=oc, kc=kc:
                                   e.matmul(out=psf[4][:, 0:256], lhsT=wob[:, kc, oc * 128:(oc + 1) * 128],
                                            rhs=ctxf[:, kc, :], start=(kc == 0), stop=(kc == 3)),
                                   inc=(1 if kc == 3 else 0))
                          tmm = p.token("tensor")
                          g_ctx = tmm
                          p.waits("scalar", [tmm, g_mha])
                          tev = p.op("scalar", lambda e, oc=oc: e.activation(
                              out=mha[:, oc, :], in_=psf[4][:, 0:256], func=AF.Identity, bias=bo2[:, oc:oc + 1]))
                          bank_free[4] = tev
                      t_mha = p.token("scalar")
                      for tc2 in range(2):
                          for dg in range(4):
                              bki = 4 + dg % 2
                              p.waits("tensor", [t_mha, bank_free[bki], t_pb])
                              for oc in range(4):
                                  p.op("tensor", lambda e, bki=bki, tc2=tc2, dg=dg, oc=oc:
                                       e.matmul(out=psf[bki][:], lhsT=mha[:, oc, tc2 * 128:(tc2 + 1) * 128],
                                                rhs=pwb[:, oc, dg * 512:(dg + 1) * 512],
                                                start=(oc == 0), stop=(oc == 3)),
                                       inc=(1 if oc == 3 else 0))
                              tmm = p.token("tensor")
                              g_mha = tmm
                              p.waits("vector", [tmm, g_sstg])
                              tev = p.op("vector", lambda e, bki=bki, dg=dg: e.tensor_tensor(
                                  out=sstg[:, dg, :], in0=psf[bki][:], in1=pbb[:, dg, :], op=ALU.add))
                              bank_free[bki] = tev
                          tevl = p.token("vector")
                          row0 = blk * 256 + tc2 * 128
                          p.wait("gpsimd", tevl)
                          g_sstg = p.op("gpsimd", lambda e, r=row0: e.dma_start(
                              out=s_scr[r:r + 128, :], in_=sstg[:].rearrange("p g n -> p (g n)")), dma="sst")

        p.barrier()

        with ExitStack() as e3:
          if PHASE_LIMIT >= 5:
              sb3 = lambda n, s, d: e3.enter_context(nc.sbuf_tensor(n, s, d))
              psB = [e3.enter_context(nc.psum_tensor(f"psB{i}", [128, 512], F32)) for i in range(8)]
              t_buf = sb3("t_buf", [128, NJC, 512], BF16)
              wslab = [sb3(f"wslab{i}", [128, 512], BF16) for i in range(4)]
              wdslab = [sb3(f"wdslab{i}", [128, D], BF16) for i in range(2)]
              sgb = sb3("sgb", [128, 512], BF16)
              uub = sb3("uub", [128, 512], BF16)
              q8b = sb3("q8b", [128, 512], FP8)
              tmx = sb3("tmx", [128, 512], F32)
              trow = sb3("trow", [1, 512], F32)
              qrow = sb3("qrow", [1, 512], F32)
              srow = sb3("srow", [1, 512], F32)
              qsbt = sb3("qsbt", [128, 512], F32)
              stbt = sb3("stbt", [128, 512], F32)
              sslab = sb3("sslab", [128, 512], BF16)
              xslab2 = [sb3(f"xslab2{i}", [128, 512], F32) for i in range(2)]
              omb = sb3("omb", [128, 512], F32)
              gyb = sb3("gyb", [128, 512], BF16)

              g_tbuf = None
              prev_wd = [None, None]
              g_sgb = None
              g_sslab = None
              g_x2 = [None, None]
              g_omb = None
              g_gyb = None

              bank_free = {i: None for i in range(8)}
              prev_w = [None] * 4
              for tau in range(NTAU):
                  for G in range(NG):
                      for d in range(NDC):
                          for mi, scr in enumerate((wgq_scr, wuq_scr)):
                              wb = wslab[(d * 2 + mi) % 4]
                              p.wait("sync", prev_w[(d * 2 + mi) % 4])
                              tld = p.op("sync", lambda e, b=wb, s=scr, d=d, G=G: e.dma_start(
                                  out=b[:], in_=s[d * 128:(d + 1) * 128, G * 512:(G + 1) * 512]), dma=f"ws{(d * 2 + mi) % 4}")
                              p.waits("tensor", [tld] + ([bank_free[i] for i in range(8)] + [g_tbuf] if d == 0 else []))
                              for cc in range(4):
                                  bk = psB[mi * 4 + cc]
                                  p.op("tensor", lambda e, bk=bk, b=wb, cc=cc, d=d, tau=tau:
                                       e.matmul(out=bk[:], lhsT=b[:, cc * 128:(cc + 1) * 128],
                                                rhs=qx_fm[:, d, tau * 512:(tau + 1) * 512],
                                                start=(d == 0), stop=(d == NDC - 1)),
                                       inc=(1 if cc == 3 else 0))
                              prev_w[(d * 2 + mi) % 4] = p.token("tensor")
                      t_gu = p.token("tensor")
                      for cc in range(4):
                          c = G * 4 + cc
                          p.waits("scalar", [t_gu, g_sgb])
                          tg = p.op("scalar", lambda e, cc=cc, c=c: e.activation(
                              out=sgb[:], in_=psB[cc][:], func=AF.Silu, scale=wsg[:, c:c + 1]))
                          tu = p.op("scalar", lambda e, cc=cc, c=c: e.activation(
                              out=uub[:], in_=psB[4 + cc][:], func=AF.Copy, scale=wsu[:, c:c + 1]))
                          p.waits("vector", [tu, g_tbuf])
                          tt2 = p.op("vector", lambda e, c=c: e.tensor_tensor(
                              out=t_buf[:, c, :], in0=sgb[:], in1=uub[:], op=ALU.mult))
                          g_sgb = tt2
                          bank_free[cc] = tt2
                          bank_free[4 + cc] = tt2
                  g_tbuf = None

                  p.op("vector", lambda e: e.tensor_reduce(
                      out=tmx[:], in_=t_buf[:].rearrange("p c s -> p s c"),
                      axis=AX.X, op=ALU.max, apply_absolute_value=True))
                  if FENCE_MODE in ("recip", "small"): p.fence("vector")
                  t_amax = p.token("vector")
                  p.wait("gpsimd", t_amax)
                  t_crow = p.op("gpsimd", lambda e: e.tensor_reduce(
                      out=trow[:], in_=tmx[:], axis=AX.C, op=ALU.max))
                  p.wait("vector", t_crow)
                  p.op("vector", lambda e: e.reciprocal(out=qrow[:], in_=trow[:]))
                  if FENCE_MODE in ("recip", "small"): p.fence("vector")
                  p.op("vector", lambda e: e.tensor_scalar_mul(out=qrow[:], in0=qrow[:], scalar1=224.0))
                  if FENCE_MODE in ("recip", "small"): p.fence("vector")
                  p.op("vector", lambda e: e.tensor_scalar_mul(out=srow[:], in0=trow[:], scalar1=1.0 / 224.0))
                  if FENCE_MODE in ("recip", "small"): p.fence("vector")
                  t_rows = p.token("vector")
                  for (row, dst) in ((qrow, qsbt), (srow, stbt)):
                      p.waits("tensor", [t_rows, tl["onesf"], bank_free[0]])
                      tb = p.op("tensor", lambda e, r=row: e.matmul(
                          out=psB[0][:], lhsT=onesf[:], rhs=r[:], start=True, stop=True))
                      p.wait("vector", tb)
                      tcp = p.op("vector", lambda e, ds=dst: e.tensor_copy(out=ds[:], in_=psB[0][:]))
                      if FENCE_MODE in ("recip", "small"): p.fence("vector")
                      bank_free[0] = tcp
                      p.wait("tensor", tcp)
                  for c in range(NJC):
                      p.op("vector", lambda e, c=c: e.tensor_tensor(
                          out=q8b[:], in0=t_buf[:, c, :], in1=qsbt[:], op=ALU.mult))
                      p.op("vector", lambda e, c=c: e.tensor_tensor(
                          out=t_buf[:, c, :], in0=q8b[:], in1=stbt[:], op=ALU.mult))
                  t_qt = p.token("vector")

                  for tcp2 in range(2):
                      for j in range(NJC):
                          wb = wdslab[j % 2]
                          p.wait("sync", prev_wd[j % 2])
                          tld = p.op("sync", lambda e, b=wb, j=j: e.dma_start(
                              out=b[:], in_=wdq_scr[j * 128:(j + 1) * 128, :]), dma=f"wdl{j % 2}")
                          p.waits("tensor", [tld, t_qt] + ([bank_free[i] for i in range(8)] if j == 0 else []))
                          for tc2 in range(2):
                              for hg in range(4):
                                  bk = psB[tc2 * 4 + hg]
                                  p.op("tensor", lambda e, bk=bk, b=wb, j=j, tcp2=tcp2, tc2=tc2, hg=hg:
                                       e.matmul(out=bk[:],
                                                lhsT=t_buf[:, j, (tcp2 * 2 + tc2) * 128:(tcp2 * 2 + tc2 + 1) * 128],
                                                rhs=b[:, hg * 512:(hg + 1) * 512],
                                                start=(j == 0), stop=(j == NJC - 1)),
                                       inc=(1 if tc2 == 1 and hg == 3 else 0))
                          prev_wd[j % 2] = p.token("tensor")
                      t_dn = p.token("tensor")
                      g_tbuf = t_dn
                      for tc2 in range(2):
                          row0 = (tau * 4 + tcp2 * 2 + tc2) * 128
                          for hg in range(4):
                              bk = psB[tc2 * 4 + hg]
                              p.waits("scalar", [g_x2[hg % 2]])
                              tlx = p.op("scalar", lambda e, b=xslab2[hg % 2], r=row0, hg=hg: e.dma_start(
                                  out=b[:], in_=h["x"][r:r + 128, hg * 512:(hg + 1) * 512]), dma=f"x2{hg % 2}")
                              p.waits("scalar", [g_sslab])
                              tls = p.op("scalar", lambda e, r=row0, hg=hg: e.dma_start(
                                  out=sslab[:], in_=s_scr[r:r + 128, hg * 512:(hg + 1) * 512]), dma="ssl")
                              p.waits("scalar", [t_dn, g_gyb])
                              tg = p.op("scalar", lambda e, bk=bk: e.activation(
                                  out=gyb[:], in_=bk[:], func=AF.Silu))
                              bank_free[tc2 * 4 + hg] = tg
                              p.waits("vector", [tg, tls, tlx, g_omb])
                              p.op("vector", lambda e: e.tensor_tensor(out=omb[:], in0=sslab[:], in1=gyb[:], op=ALU.mult))
                              g_sslab = p.token("vector")
                              g_gyb = g_sslab
                              tcb = p.op("vector", lambda e, b=xslab2[hg % 2]: e.tensor_tensor(
                                  out=omb[:], in0=omb[:], in1=b[:], op=ALU.add))
                              g_x2[hg % 2] = tcb
                              p.wait("gpsimd", tcb)
                              g_omb = p.op("gpsimd", lambda e, r=row0, hg=hg: e.dma_start(
                                  out=h["out"][r:r + 128, hg * 512:(hg + 1) * 512], in_=omb[:]), dma="out")


        if DEBUG_DUMP:
            p.barrier()
            if PHASE_LIMIT >= 1:
                p.op("sync", lambda e: e.dma_start(out=h["dbg_wg"], in_=wgq_scr[:, :]), dma="out")
                p.op("sync", lambda e: e.dma_start(out=h["dbg_wd"], in_=wdq_scr[:, :]), dma="out")
            if PHASE_LIMIT >= 2:
                p.op("sync", lambda e: e.dma_start(out=h["dbg_qx"], in_=qx_fm[:]), dma="out")
            if PHASE_LIMIT >= 4:
                p.op("sync", lambda e: e.dma_start(out=h["dbg_s"], in_=s_scr[:, :]), dma="out")
            p.items["sync"].append(("wait", "out", p.count["out"]))
        if "out" in p.count:
            p.items["gpsimd"].append(("wait", "out", p.count["out"]))
        else:
            for cname in list(p.chans):
                p.items["gpsimd"].append(("wait", cname, p.count[cname]))
        finish()
        with nc.Block() as block:
            p.emit(block, sems)
    return nc


_NC_CACHE = None


def _get_program():
    global _NC_CACHE
    if _NC_CACHE is None:
        _NC_CACHE = build_program()
    return _NC_CACHE


def kernel(**inputs):
    f32 = np.float32
    x = np.asarray(inputs["L0_attn_x"], f32)
    cw = np.ascontiguousarray(np.asarray(inputs["L0_attn_conv_w"], f32))
    cb2 = np.ascontiguousarray(np.asarray(inputs["L0_attn_conv_b"], f32).reshape(12, 128).T)
    wq2 = np.ascontiguousarray(np.asarray(inputs["L0_attn_wq"], f32).reshape(512, 512))
    wk2 = np.ascontiguousarray(np.asarray(inputs["L0_attn_wk"], f32).reshape(512, 512))
    wv2 = np.ascontiguousarray(np.asarray(inputs["L0_attn_wv"], f32).reshape(512, 512))
    wo2 = np.ascontiguousarray(np.asarray(inputs["L0_attn_wo"], f32).reshape(512, 512))
    bq2 = np.ascontiguousarray(np.asarray(inputs["L0_attn_bq"], f32).reshape(4, 128).T)
    bk2 = np.ascontiguousarray(np.asarray(inputs["L0_attn_bk"], f32).reshape(4, 128).T)
    bv2 = np.ascontiguousarray(np.asarray(inputs["L0_attn_bv"], f32).reshape(4, 128).T)
    bo2 = np.ascontiguousarray(np.asarray(inputs["L0_attn_bo"], f32).reshape(4, 128).T)
    pw = np.ascontiguousarray(np.asarray(inputs["L0_attn_proj_w"], f32))
    pbr = np.ascontiguousarray(np.asarray(inputs["L0_attn_proj_b"], f32).reshape(1, D))
    wg = np.ascontiguousarray(np.asarray(inputs["L0_ffn_w_gate"], f32))
    wu = np.ascontiguousarray(np.asarray(inputs["L0_ffn_w_up"], f32))
    wd = np.ascontiguousarray(np.asarray(inputs["L0_ffn_w_down"], f32))
    wgT = np.ascontiguousarray(wg.T)
    wuT = np.ascontiguousarray(wu.T)
    wdT = np.ascontiguousarray(wd.T)
    didx = np.ascontiguousarray(np.asarray(inputs["L0_ffn_down_reorder_index"], np.int32))
    eyef = np.eye(128, dtype=f32)
    onesf = np.ones((1, 128), f32)
    jj = np.arange(256)[None, :]
    pp = np.arange(128)[:, None]
    cmask = np.ascontiguousarray(np.stack([
        np.where(jj <= pp, 0.0, -1e9).astype(f32),
        np.where(jj <= 128 + pp, 0.0, -1e9).astype(f32)], axis=1))

    shared = dict(cw=cw, cb2=cb2, wq2=wq2, wk2=wk2, wv2=wv2, wo2=wo2,
                  bq2=bq2, bk2=bk2, bv2=bv2, bo2=bo2, pw=pw, pbr=pbr,
                  wg=wg, wu=wu, wd=wd, wgT=wgT, wuT=wuT, wdT=wdT,
                  didx=didx, eyef=eyef, onesf=onesf, cmask=cmask)
    in_maps = []
    for c in range(NCORES):
        seq, half = c // 2, c % 2
        t0 = half * T
        m = dict(shared)
        m["x"] = np.ascontiguousarray(x[seq, t0:t0 + T])
        m["xh"] = (np.zeros((HALO, D), f32) if half == 0
                   else np.ascontiguousarray(x[seq, t0 - HALO:t0]))
        in_maps.append(m)

    nc = _get_program()
    res = run_bass_kernel_spmd(nc, in_maps, core_ids=list(range(NCORES)))
    out = np.empty((4, 4096, D), f32)
    for c in range(NCORES):
        seq, half = c // 2, c % 2
        out[seq, half * T:(half + 1) * T] = np.asarray(res.results[c]["out"])
    return out


if __name__ == "__main__":
    build_program()
    print("built ok")


# version 08
# speedup vs baseline: 1.7742x; 1.1067x over previous
import numpy as np
import concourse.bass as bass
import concourse.mybir as mybir
from concourse.bass_utils import run_bass_kernel_spmd
from contextlib import ExitStack

F32 = mybir.dt.float32
BF16 = mybir.dt.bfloat16
FP8 = mybir.dt.float8e4
I32 = mybir.dt.int32
AF = mybir.ActivationFunctionType
ALU = mybir.AluOpType
AX = mybir.AxisListType

NCORES = 8
T = 2048
D = 2048
QKV = 1536
HD = 512
FF = 5632
NTC = 16
NTAU = 4
NDC = 16
NJC = 44
NG = 11
HALO = 16
CS = 2064

ENGINES = ("sync", "scalar", "vector", "tensor", "gpsimd")

DEBUG_SIM = False
SELF_WAIT_ENGINES = None
FENCE_MODE = "small"
SKIP_HALO = False
DEBUG_DUMP = False
PHASE_LIMIT = 5


class Prog:

    def __init__(self):
        self.items = {e: [] for e in ENGINES}
        self.count = {}
        for e in ENGINES:
            self.count[e] = 0
        self.chans = []
        self.bound = {e: {} for e in ENGINES}

    def chan(self, name):
        if name not in self.count:
            self.count[name] = 0
            self.chans.append(name)
        return name

    def op(self, eng, fn, dma=None, inc=None):
        if dma is not None:
            sem = self.chan(dma)
            amt = 16
        else:
            sem = eng
            amt = 1 if inc is None else inc
            if DEBUG_SIM and FENCE_MODE == "all" and (SELF_WAIT_ENGINES is None or eng in SELF_WAIT_ENGINES):
                amt = max(amt, 1)
                if self.count[eng] > 0:
                    self.items[eng].append(("wait", eng, self.count[eng]))
        self.items[eng].append(("op", fn, sem, amt))
        self.count[sem] += amt
        return (sem, self.count[sem]) if amt else None

    def fence(self, eng):
        if self.count.get(eng, 0) > 0:
            self.items[eng].append(("wait", eng, self.count[eng]))

    def wait(self, eng, tok):
        if tok is None:
            return
        sem, val = tok
        if sem == eng or self.bound[eng].get(sem, 0) >= val:
            return
        self.items[eng].append(("wait", sem, val))
        self.bound[eng][sem] = val

    def waits(self, eng, toks):
        for t in toks:
            self.wait(eng, t)

    def barrier(self):
        snap = dict(self.count)
        for e in ENGINES:
            for s, v in snap.items():
                if s != e and v > 0:
                    self.wait(e, (s, v))

    def token(self, sem):
        return (sem, self.count[sem]) if self.count.get(sem, 0) else None

    def emit(self, block, sems):
        def mk(ename):
            def emitter(eng):
                for it in self.items[ename]:
                    if it[0] == "wait":
                        eng.wait_ge(sems[it[1]], it[2])
                    else:
                        ins = it[1](eng)
                        if it[3]:
                            ins.then_inc(sems[it[2]], it[3])
            return emitter
        block.sync(mk("sync"))
        block.scalar(mk("scalar"))
        block.vector(mk("vector"))
        block.tensor(mk("tensor"))
        block.gpsimd(mk("gpsimd"))


def build_program():
    nc = bass.Bass("TRN2", target_bir_lowering=False, debug=False,
                   detect_race_conditions=DEBUG_SIM)
    h = {}

    def inp(name, shape, dt=F32):
        h[name] = nc.dram_tensor(name, shape, dt, kind="ExternalInput").ap()

    inp("x", [T, D]); inp("xh", [HALO, D])
    inp("cw", [3, D, QKV]); inp("cb2", [128, 12])
    for nm in ("wq2", "wk2", "wv2", "wo2"):
        inp(nm, [HD, HD])
    for nm in ("bq2", "bk2", "bv2", "bo2"):
        inp(nm, [128, 4])
    inp("pw", [HD, D]); inp("pbr", [1, D])
    inp("wg", [FF, D]); inp("wu", [FF, D]); inp("wd", [D, FF])
    inp("wgT", [D, FF]); inp("wuT", [D, FF]); inp("wdT", [FF, D])
    inp("didx", [FF], I32)
    inp("eyef", [128, 128]); inp("onesf", [1, 128]); inp("cmask", [128, 2, 256])
    h["out"] = nc.dram_tensor("out", [T, D], F32, kind="ExternalOutput").ap()
    if DEBUG_DUMP:
        h["dbg_s"] = nc.dram_tensor("dbg_s", [T, D], BF16, kind="ExternalOutput").ap()
        h["dbg_wg"] = nc.dram_tensor("dbg_wg", [D, FF], BF16, kind="ExternalOutput").ap()
        h["dbg_wd"] = nc.dram_tensor("dbg_wd", [FF, D], BF16, kind="ExternalOutput").ap()
        h["dbg_qx"] = nc.dram_tensor("dbg_qx", [128, NDC, T], BF16, kind="ExternalOutput").ap()

    xn_scr = nc.dram_tensor("xn_scr", [NDC, 128, CS], BF16, kind="Internal").ap()
    s_scr = nc.dram_tensor("s_scr", [T, D], BF16, kind="Internal").ap()
    q8_scr = nc.dram_tensor("q8_scr", [T, D], FP8, kind="Internal").ap()
    wgq_scr = nc.dram_tensor("wgq_scr", [D, FF], BF16, kind="Internal").ap()
    wuq_scr = nc.dram_tensor("wuq_scr", [D, FF], BF16, kind="Internal").ap()
    wdq_scr = nc.dram_tensor("wdq_scr", [FF, D], BF16, kind="Internal").ap()

    p = Prog()
    es = ExitStack()
    with es:
        sems = {e: es.enter_context(nc.semaphore("sem_" + e)) for e in ENGINES}
        def finish():
            for cname in p.chans:
                sems[cname] = es.enter_context(nc.semaphore("sem_" + cname))
        sb = lambda name, shape, dt: es.enter_context(nc.sbuf_tensor(name, shape, dt))

        eyef = sb("c_eyef", [128, 128], F32)
        eyeb = sb("c_eyeb", [128, 128], BF16)
        onesf = sb("c_onesf", [1, 128], F32)
        cmask = sb("c_cmask", [128, 2, 256], F32)
        cb2 = sb("c_cb2", [128, 12], F32)
        bq8 = sb("c_bq8", [128, 4], F32)
        bk2 = sb("c_bk2", [128, 4], F32)
        bv2 = sb("c_bv2", [128, 4], F32)
        bo2 = sb("c_bo2", [128, 4], F32)
        wsg = sb("c_wsg", [128, NJC], F32)
        wsu = sb("c_wsu", [128, NJC], F32)
        epsc = sb("c_eps", [128, 1], F32)

        tl = {}
        for nm, dst in (("eyef", eyef), ("onesf", onesf), ("cmask", cmask),
                        ("cb2", cb2), ("bk2", bk2), ("bv2", bv2), ("bo2", bo2)):
            tl[nm] = p.op("sync", lambda e, d=dst, a=h[nm]: e.dma_start(out=d[:], in_=a[:]), dma="ld_" + nm)
        tl["bq2"] = p.op("sync", lambda e: e.dma_start(out=bq8[:], in_=h["bq2"][:]), dma="ld_bq2")
        p.wait("vector", tl["bq2"])
        p.op("vector", lambda e: e.tensor_scalar_mul(out=bq8[:], in0=bq8[:], scalar1=0.125))
        if FENCE_MODE in ("recip", "small"): p.fence("vector")
        t_bq8 = p.token("vector")
        p.wait("scalar", tl["eyef"])
        t_eyeb = p.op("scalar", lambda e: e.copy(out=eyeb[:], in_=eyef[:]))
        t_eps = p.op("vector", lambda e: e.memset(epsc[:], 1e-5))

        sx_all = sb("sx_all", [128, NTC], F32)
        _DONE = object()

        def pull(g, n):
            for _ in range(n):
                if next(g, _DONE) is _DONE:
                    return False
            return True

        t_conv = {}
        with ExitStack() as ew:
          if PHASE_LIMIT >= 1:
              sbw = lambda n, s, d: ew.enter_context(nc.sbuf_tensor(n, s, d))
              idx_s = sbw("idx_s", [128, NJC], I32)
              wnat = [sbw(f"wnat{i}", [128, 3072], F32) for i in range(2)]
              rcol = sbw("rcol", [128, 4], F32)
              q8t = [sbw(f"q8t{i}", [128, 512], FP8) for i in range(2)]
              qbf = [sbw(f"qbf{i}", [128, FF], BF16) for i in range(2)]
              qsb_g = sbw("qsb_g", [128, NG, 512], F32)
              qsb_u = sbw("qsb_u", [128, NG, 512], F32)
              qsdb = [sbw(f"qsdb{i}", [128, 512], F32) for i in range(4)]
              wsdb = [sbw(f"wsdb{i}", [128, 512], F32) for i in range(4)]
              wdqt = [sbw(f"wdqt{i}", [128, D], BF16) for i in range(2)]

              t_idx = p.op("sync", lambda e: e.dma_start(
                  out=idx_s[:], in_=h["didx"].rearrange("(c p) -> p c", p=128),
                  allow_slow_non_contiguous=True), dma="ld_idx")

              prev_wnat = [None, None]
              wtok = {}

              def w_maxes(src, n_chunks, row_len, qrow, wscol, wrow, psW):
                  prev = prev_wnat
                  t_rcol_free = None
                  for c in range(n_chunks):
                      halves = [(0, min(3072, row_len))]
                      if row_len > 3072:
                          halves.append((3072, row_len))
                      for hi, (c0, c1) in enumerate(halves):
                          buf = wnat[c % 2] if hi == 0 else wnat[(c + 1) % 2]
                          p.wait("sync", prev[c % 2] if hi == 0 else prev[(c + 1) % 2])
                          tld = p.op("sync", lambda e, b=buf, s=src, c=c, c0=c0, c1=c1: e.dma_start(
                              out=b[:, 0:c1 - c0], in_=s[c * 128:(c + 1) * 128, c0:c1]), dma=f"wn{(c + hi) % 2}")
                          p.waits("vector", [tld, t_rcol_free])
                          p.op("vector", lambda e, b=buf, n=c1 - c0, hi=hi: e.reduce_max(
                              out=rcol[:, hi:hi + 1], in_=b[:, 0:n], axis=AX.X, apply_absolute_value=True))
                          p.fence("vector")
                          if hi == 0:
                              prev[c % 2] = p.token("vector")
                          else:
                              prev[(c + 1) % 2] = p.token("vector")
                      if len(halves) > 1:
                          p.op("vector", lambda e: e.tensor_tensor(
                              out=rcol[:, 0:1], in0=rcol[:, 0:1], in1=rcol[:, 1:2], op=ALU.max))
                          p.fence("vector")
                      p.op("vector", lambda e: e.reciprocal(out=rcol[:, 1:2], in_=rcol[:, 0:1]))
                      p.fence("vector")
                      p.op("vector", lambda e: e.tensor_scalar_mul(out=rcol[:, 2:3], in0=rcol[:, 1:2], scalar1=224.0))
                      p.fence("vector")
                      if wscol is not None:
                          p.op("vector", lambda e, c=c: e.tensor_scalar_mul(
                              out=wscol[:, c:c + 1], in0=rcol[:, 0:1], scalar1=1.0 / 224.0))
                          p.fence("vector")
                      else:
                          p.op("vector", lambda e: e.tensor_scalar_mul(
                              out=rcol[:, 3:4], in0=rcol[:, 0:1], scalar1=1.0 / 224.0))
                          p.fence("vector")
                      tv = p.token("vector")
                      p.waits("tensor", [tv, tl["eyef"]])
                      tt = p.op("tensor", lambda e: e.transpose(
                          out=psW[0][0:1, 0:128], in_=rcol[:, 2:3], identity=eyef[:]))
                      t_rcol_free = tt
                      p.wait("vector", tt)
                      p.op("vector", lambda e, q=qrow, c=c: e.tensor_copy(
                          out=q[:, c * 128:(c + 1) * 128], in_=psW[0][0:1, 0:128]))
                      p.fence("vector")
                      tpe = p.token("vector")
                      p.wait("tensor", tpe)
                      if wscol is None:
                          tt2 = p.op("tensor", lambda e: e.transpose(
                              out=psW[0][0:1, 0:128], in_=rcol[:, 3:4], identity=eyef[:]))
                          t_rcol_free = tt2
                          p.wait("vector", tt2)
                          p.op("vector", lambda e, w=wrow, c=c: e.tensor_copy(
                              out=w[:, c * 128:(c + 1) * 128], in_=psW[0][0:1, 0:128]))
                          p.fence("vector")
                          tpe = p.token("vector")
                          p.wait("tensor", tpe)
                      yield
                  return p.token("vector")

              def w_bcast_groups(qrow, qdep, qsb_all, psW):
                  last = None
                  for G in range(NG):
                      p.waits("tensor", [qdep, tl["onesf"], last])
                      tb = p.op("tensor", lambda e, G=G, q=qrow: e.matmul(
                          out=psW[1][:], lhsT=onesf[:], rhs=q[:, G * 512:(G + 1) * 512],
                          start=True, stop=True))
                      p.wait("vector", tb)
                      p.op("vector", lambda e, G=G, qa=qsb_all: e.tensor_copy(out=qa[:, G, :], in_=psW[1][:]))
                      p.fence("vector")
                      last = p.token("vector")
                      p.wait("tensor", last)
                  return last

              wq_state = {"prev_st": [None, None], "q8_free": [None, None]}

              def w_quant(srcT, dst_scr, key, qsb_all):
                  prev_ld = prev_wnat
                  prev_st = wq_state["prev_st"]
                  q8_free = wq_state["q8_free"]
                  for d in range(NDC):
                      ob = qbf[d % 2]
                      tcast = None
                      for hi, (G0, G1) in enumerate(((0, 6), (6, NG))):
                          buf = wnat[(d + hi) % 2]
                          p.wait("sync", prev_ld[(d + hi) % 2])
                          tld = p.op("sync", lambda e, b=buf, s=srcT, d=d, G0=G0, G1=G1: e.dma_start(
                              out=b[:, 0:(G1 - G0) * 512], in_=s[d * 128:(d + 1) * 128, G0 * 512:G1 * 512]), dma=f"wt{(d + hi) % 2}")
                          p.waits("vector", [tld, wtok[key], prev_st[d % 2]])
                          for G in range(G0, G1):
                              qb = q8t[G % 2]
                              p.waits("vector", [q8_free[G % 2]])
                              t8 = p.op("vector", lambda e, b=buf, G=G, G0=G0, qb=qb, qa=qsb_all: e.tensor_tensor(
                                  out=qb[:], in0=b[:, (G - G0) * 512:(G - G0 + 1) * 512], in1=qa[:, G, :], op=ALU.mult))
                              p.waits("scalar", [t8, tcast, prev_st[d % 2]])
                              tcast = p.op("scalar", lambda e, o=ob, G=G, qb=qb: e.copy(
                                  out=o[:, G * 512:(G + 1) * 512], in_=qb[:]))
                              q8_free[G % 2] = tcast
                          prev_ld[(d + hi) % 2] = p.token("vector")
                      p.wait("sync", tcast)
                      ts = p.op("sync", lambda e, o=ob, ds=dst_scr, d=d: e.dma_start(
                          out=ds[d * 128:(d + 1) * 128, :], in_=o[:]), dma=f"wqs{d % 2}")
                      prev_st[d % 2] = ts
                      yield

              def w_wd():
                  prev_ld = prev_wnat
                  prev_sc = [None, None]
                  for c in range(NJC):
                      buf = wnat[c % 2]
                      p.wait("sync", prev_ld[c % 2])
                      tld = p.op("sync", lambda e, b=buf, c=c: e.dma_start(
                          out=b[:, 0:D], in_=h["wdT"][c * 128:(c + 1) * 128, :]), dma=f"wn{c % 2}")
                      ob = wdqt[c % 2]
                      p.waits("vector", [tld, wtok["b4"], prev_ld[c % 2], prev_sc[c % 2]])
                      for hg in range(4):
                          p.op("vector", lambda e, b=buf, hg=hg: e.tensor_tensor(
                              out=q8t[0][:], in0=b[:, hg * 512:(hg + 1) * 512], in1=qsdb[hg][:], op=ALU.mult))
                          p.op("vector", lambda e, o=ob, hg=hg: e.tensor_tensor(
                              out=o[:, hg * 512:(hg + 1) * 512], in0=q8t[0][:], in1=wsdb[hg][:], op=ALU.mult))
                      tf = p.token("vector")
                      prev_ld[c % 2] = tf
                      p.waits("gpsimd", [tf, t_idx, prev_sc[c % 2]])
                      tsc = p.op("gpsimd", lambda e, o=ob, c=c: e.indirect_dma_start(
                          out=wdq_scr[:, :],
                          out_offset=bass.IndirectOffsetOnAxis(ap=idx_s[:, c:c + 1], axis=0),
                          in_=o[:], in_offset=None), dma=f"sca{c % 2}")
                      prev_sc[c % 2] = tsc
                      yield

              def gw2():
                  yield from w_quant(h["wgT"], wgq_scr, "bg", qsb_g)
                  yield from w_quant(h["wuT"], wuq_scr, "bu", qsb_u)
                  yield from w_wd()

              with ExitStack() as ew2:
                  sbw2 = lambda n, s, d: ew2.enter_context(nc.sbuf_tensor(n, s, d))
                  psW = [ew2.enter_context(nc.psum_tensor(f"psW{i}", [128, 512], F32)) for i in range(2)]
                  qrow_g = sbw2("qrow_g", [1, FF], F32)
                  qrow_d = sbw2("qrow_d", [1, D], F32)
                  wrow_d = sbw2("wrow_d", [1, D], F32)

                  def gw1():
                      t_mg = yield from w_maxes(h["wg"], NJC, D, qrow_g, wsg, None, psW)
                      wtok["bg"] = w_bcast_groups(qrow_g, t_mg, qsb_g, psW)
                      yield
                      t_mu = yield from w_maxes(h["wu"], NJC, D, qrow_g, wsu, None, psW)
                      wtok["bu"] = w_bcast_groups(qrow_g, t_mu, qsb_u, psW)
                      yield
                      t_md = yield from w_maxes(h["wd"], NDC, FF, qrow_d, None, wrow_d, psW)
                      t_b4 = None
                      for hg in range(4):
                          for src_row, dstl in ((qrow_d, qsdb), (wrow_d, wsdb)):
                              p.waits("tensor", [t_md, tl["onesf"], t_b4])
                              tb = p.op("tensor", lambda e, hg=hg, r=src_row: e.matmul(
                                  out=psW[1][:], lhsT=onesf[:], rhs=r[:, hg * 512:(hg + 1) * 512],
                                  start=True, stop=True))
                              p.wait("vector", tb)
                              p.op("vector", lambda e, hg=hg, dl=dstl: e.tensor_copy(out=dl[hg][:], in_=psW[1][:]))
                              p.fence("vector")
                              t_b4 = p.token("vector")
                              p.wait("tensor", t_b4)
                      wtok["b4"] = t_b4
                      yield

                  with ExitStack() as e0:
                    if PHASE_LIMIT >= 2:
                      sb0 = lambda n, s, d: e0.enter_context(nc.sbuf_tensor(n, s, d))
                      psb = [e0.enter_context(nc.psum_tensor(f"ps0b{i}", [128, 512], BF16)) for i in range(2)]
                      xt = [sb0(f"xt{i}", [128, D], F32) for i in range(2)]
                      xnf = sb0("xnf", [128, D], F32)
                      junk = sb0("junk", [128, D], BF16)
                      xnb = sb0("xnb", [128, D], BF16)
                      q8x = sb0("q8x", [128, D], FP8)
                      cols = sb0("cols", [128, 8], F32)
                      stg = sb0("stg", [128, NDC, 128], BF16)

                      st0 = {"guard_xt": [[], []], "g_xnf": [], "g_q8x": [], "g_stg": None,
                             "bank_tok": [None, None]}

                      def p0_tile(tc):
                          halo = (tc == NTC)
                          P = HALO if halo else 128
                          buf = xt[tc % 2]
                          src = h["xh"][:] if halo else h["x"][tc * 128:(tc + 1) * 128, :]
                          p.waits("sync", st0["guard_xt"][tc % 2])
                          tld = p.op("sync", lambda e, b=buf, s=src, P=P: e.dma_start(out=b[0:P, :], in_=s), dma=f"xt{tc % 2}")
                          p.waits("vector", [tld] + st0["g_xnf"])
                          p.op("vector", lambda e, b=buf, P=P: e.reduce_sum(out=cols[0:P, 0:1], in_=b[0:P, :], axis=AX.X))
                          p.fence("vector")
                          p.wait("scalar", tld)
                          tsq = p.op("scalar", lambda e, b=buf, P=P: e.activation(
                              out=junk[0:P, :], in_=b[0:P, :], func=AF.Square, accum_out=cols[0:P, 1:2]))
                          p.wait("vector", tsq)
                          p.op("vector", lambda e, P=P: e.tensor_scalar_mul(out=cols[0:P, 2:3], in0=cols[0:P, 0:1], scalar1=1.0 / D))
                          p.fence("vector")
                          p.op("vector", lambda e, P=P: e.tensor_scalar_mul(out=cols[0:P, 3:4], in0=cols[0:P, 1:2], scalar1=1.0 / D))
                          p.fence("vector")
                          p.op("vector", lambda e, P=P: e.tensor_tensor(out=cols[0:P, 4:5], in0=cols[0:P, 2:3], in1=cols[0:P, 2:3], op=ALU.mult))
                          p.fence("vector")
                          tv = p.op("vector", lambda e, P=P: e.tensor_tensor(out=cols[0:P, 4:5], in0=cols[0:P, 3:4], in1=cols[0:P, 4:5], op=ALU.subtract))
                          p.fence("vector")
                          p.waits("scalar", [tv, t_eps])
                          tsd = p.op("scalar", lambda e, P=P: e.activation(out=cols[0:P, 5:6], in_=cols[0:P, 4:5], func=AF.Sqrt, bias=epsc[0:P, 0:1]))
                          p.wait("vector", tsd)
                          p.op("vector", lambda e, P=P: e.reciprocal(out=cols[0:P, 6:7], in_=cols[0:P, 5:6]))
                          p.fence("vector")
                          txn = p.op("vector", lambda e, b=buf, P=P: e.tensor_scalar(
                              out=xnf[0:P, :], in0=b[0:P, :], scalar1=cols[0:P, 2:3], scalar2=cols[0:P, 6:7],
                              op0=ALU.subtract, op1=ALU.mult))
                          st0["guard_xt"][tc % 2] = [txn, tsq]
                          p.wait("scalar", txn)
                          txb = p.op("scalar", lambda e, P=P: e.copy(out=xnb[0:P, :], in_=xnf[0:P, :]))
                          st0["g_xnf"] = [txb]

                          if not halo:
                              p.op("vector", lambda e: e.reduce_max(out=cols[:, 0:1], in_=xnf[:], axis=AX.X, apply_absolute_value=True))
                              p.fence("vector")
                              p.op("vector", lambda e: e.reciprocal(out=cols[:, 1:2], in_=cols[:, 0:1]))
                              p.fence("vector")
                              tqs = p.op("vector", lambda e: e.tensor_scalar_mul(out=cols[:, 2:3], in0=cols[:, 1:2], scalar1=224.0))
                              p.fence("vector")
                              p.op("vector", lambda e, tc=tc: e.tensor_scalar_mul(out=sx_all[:, tc:tc + 1], in0=cols[:, 0:1], scalar1=1.0 / 224.0))
                              p.fence("vector")
                              p.waits("scalar", [tqs] + st0["g_q8x"])
                              t8 = p.op("scalar", lambda e: e.activation(out=q8x[:], in_=xnf[:], func=AF.Copy, scale=cols[:, 2:3]))
                              st0["g_xnf"] = [t8]
                              p.wait("sync", t8)
                              t8s = p.op("sync", lambda e, tc=tc: e.dma_start(
                                  out=q8_scr[tc * 128:(tc + 1) * 128, :], in_=q8x[:]), dma="q8s")
                              st0["g_q8x"] = [t8s]

                          ev = []
                          for d in range(NDC):
                              bank = psb[d % 2]
                              p.waits("tensor", [txb, t_eyeb, st0["bank_tok"][d % 2], st0["g_stg"]])
                              tt = p.op("tensor", lambda e, d=d, b=bank, P=P: e.transpose(
                                  out=b[:, 0:P], in_=xnb[0:P, d * 128:(d + 1) * 128], identity=eyeb[0:P, 0:P]))
                              p.wait("scalar", tt)
                              te = p.op("scalar", lambda e, d=d, b=bank, P=P: e.copy(out=stg[:, d, 0:P], in_=b[:, 0:P]))
                              ev.append(te)
                              st0["bank_tok"][d % 2] = te
                          for d in range(NDC):
                              p.wait("sync", ev[d])
                              cst = 0 if halo else HALO + tc * 128
                              p.op("sync", lambda e, d=d, c=cst, P=P: e.dma_start(
                                  out=xn_scr[d, :, c:c + P], in_=stg[:, d, 0:P]), dma="stg")
                          st0["g_stg"] = p.token("stg")

                      g1 = gw1()
                      for tc in range(NTC if SKIP_HALO else NTC + 1):
                          p0_tile(tc)
                          pull(g1, 8)
                      while pull(g1, 100):
                          pass

              t_p0_stg = p.token("stg")
              qkv_fm = es.enter_context(nc.sbuf_tensor("qkv_fm", [128, 12, T], BF16, side="right"))
              with ExitStack() as e1:
                if PHASE_LIMIT >= 3:
                  sb1 = lambda n, s, d: e1.enter_context(nc.sbuf_tensor(n, s, d))
                  psA = [e1.enter_context(nc.psum_tensor(f"psA{i}", [128, 512], F32)) for i in range(8)]
                  xslab = [sb1(f"xslab{i}", [128, CS], BF16) for i in range(2)]
                  cwf = [sb1(f"cwf{i}", [128, 256], F32) for i in range(2)]
                  cwb = [sb1(f"cwb{i}", [128, 256], BF16) for i in range(2)]

                  stA = {"prev_ev": None, "prev_cw": [None, None], "prev_cwf": [None, None],
                         "prev_x": [None, None]}
                  g2 = gw2() if PHASE_LIMIT >= 1 else None

                  def a1_ccg(ccg):
                      step = 0
                      last_mm = None
                      for d in range(NDC):
                          bx = xslab[d % 2]
                          p.waits("sync", [stA["prev_x"][d % 2], t_p0_stg])
                          tlx = p.op("sync", lambda e, b=bx, d=d: e.dma_start(out=b[:], in_=xn_scr[d, :, :]), dma=f"xs{d % 2}")
                          for w in range(3):
                              bcf = cwf[step % 2]
                              bcb = cwb[step % 2]
                              p.wait("sync", stA["prev_cwf"][step % 2])
                              tlw = p.op("sync", lambda e, b=bcf, w=w, d=d, g=ccg: e.dma_start(
                                  out=b[:], in_=h["cw"][w, d * 128:(d + 1) * 128, g * 256:(g + 1) * 256]), dma=f"cw{step % 2}")
                              p.waits("scalar", [tlw, stA["prev_cw"][step % 2]])
                              tcw = p.op("scalar", lambda e, bf=bcf, bb=bcb: e.copy(out=bb[:], in_=bf[:]))
                              stA["prev_cwf"][step % 2] = tcw
                              p.waits("tensor", [tcw, tlx, stA["prev_ev"]])
                              first = (d == 0 and w == 0)
                              lastg = (d == NDC - 1 and w == 2)
                              for cc2 in range(2):
                                  for tau in range(NTAU):
                                      bk = psA[cc2 * 4 + tau]
                                      isl = (cc2 == 1 and tau == NTAU - 1)
                                      tok = p.op("tensor", lambda e, bk=bk, bb=bcb, bx=bx, cc2=cc2, tau=tau, w=w, f=first, L=lastg:
                                                 e.matmul(out=bk[:], lhsT=bb[:, cc2 * 128:(cc2 + 1) * 128],
                                                          rhs=bx[:, HALO + tau * 512 + (w - 2): HALO + tau * 512 + (w - 2) + 512],
                                                          start=f, stop=L),
                                                 inc=(1 if isl else 0))
                                      if isl:
                                          last_mm = tok
                              stA["prev_cw"][step % 2] = last_mm
                              step += 1
                          stA["prev_x"][d % 2] = last_mm
                          if g2 is not None:
                              pull(g2, 1)
                      p.waits("scalar", [last_mm, tl["cb2"]])
                      for cc2 in range(2):
                          for tau in range(NTAU):
                              bk = psA[cc2 * 4 + tau]
                              cc = ccg * 2 + cc2
                              p.op("scalar", lambda e, bk=bk, cc=cc, tau=tau: e.activation(
                                  out=qkv_fm[:, cc, tau * 512:(tau + 1) * 512], in_=bk[:],
                                  func=AF.Identity, bias=cb2[:, cc:cc + 1]))
                      stA["prev_ev"] = p.token("scalar")

                  for ccg in range(6):
                      a1_ccg(ccg)
                  if g2 is not None:
                      while pull(g2, 100):
                          pass

        p.barrier()

        qx_fm = es.enter_context(nc.sbuf_tensor("qx_fm", [128, NDC, T], BF16, side="right"))
        with ExitStack() as eq:
          if PHASE_LIMIT >= 2:
              sbq = lambda n, s, d: eq.enter_context(nc.sbuf_tensor(n, s, d))
              psq = [eq.enter_context(nc.psum_tensor(f"psq{i}", [128, 512], BF16)) for i in range(2)]
              q8i = [sbq(f"q8i{i}", [128, D], FP8) for i in range(2)]
              qxb = sbq("qxb", [128, D], BF16)
              prev_q8i = [None, None]
              g_qxb = None
              bank_tok = [None, None]
              for tc in range(NTC):
                  buf = q8i[tc % 2]
                  p.wait("sync", prev_q8i[tc % 2])
                  tld = p.op("sync", lambda e, b=buf, tc=tc: e.dma_start(
                      out=b[:], in_=q8_scr[tc * 128:(tc + 1) * 128, :]), dma=f"qx{tc % 2}")
                  p.waits("vector", [tld, g_qxb])
                  tqx = p.op("vector", lambda e, b=buf, tc=tc: e.tensor_scalar_mul(
                      out=qxb[:], in0=b[:], scalar1=sx_all[:, tc:tc + 1]))
                  prev_q8i[tc % 2] = tqx
                  ev2 = []
                  for d in range(NDC):
                      bank = psq[d % 2]
                      p.waits("tensor", [tqx, bank_tok[d % 2]])
                      tt = p.op("tensor", lambda e, d=d, b=bank: e.transpose(
                          out=b[:, 0:128], in_=qxb[:, d * 128:(d + 1) * 128], identity=eyeb[:]))
                      if d == NDC - 1:
                          g_qxb = tt
                      p.wait("vector", tt)
                      te = p.op("vector", lambda e, d=d, tc=tc, b=bank: e.tensor_copy(
                          out=qx_fm[:, d, tc * 128:(tc + 1) * 128], in_=b[:, 0:128]))
                      ev2.append(te)
                      bank_tok[d % 2] = te

        p.barrier()

        with ExitStack() as e2:
          if PHASE_LIMIT >= 4:
              sb2 = lambda n, s, d: e2.enter_context(nc.sbuf_tensor(n, s, d))
              psf = [e2.enter_context(nc.psum_tensor(f"psf{i}", [128, 512], F32)) for i in range(6)]
              psT = [e2.enter_context(nc.psum_tensor(f"psT{i}", [128, 512], BF16)) for i in range(2)]
              wqb = sb2("wqb", [128, 4, 512], BF16)
              wkb = sb2("wkb", [128, 4, 512], BF16)
              wvb = sb2("wvb", [128, 4, 512], BF16)
              wob = sb2("wob", [128, 4, 512], BF16)
              pwb = sb2("pwb", [128, 4, D], BF16)
              pbb = sb2("pbb", [128, 4, 512], F32)
              wtmp = sb2("wtmp", [128, D], F32)
              pbr_sb = sb2("pbr_sb", [1, D], F32)
              tl["pbr"] = p.op("sync", lambda e: e.dma_start(out=pbr_sb[:], in_=h["pbr"][:]), dma="ld_pbr")
              qh = sb2("qh", [128, 4, 512], BF16)
              kh = sb2("kh", [128, 4, 512], BF16)
              vh = sb2("vh", [128, 4, 512], BF16)
              probs = sb2("probs", [128, 2, 256], BF16)
              probsn = sb2("probsn", [128, 2, 256], BF16)
              pT = sb2("pT", [128, 2, 3, 128], BF16)
              ctxf = sb2("ctxf", [128, 4, 256], BF16)
              mha = sb2("mha", [128, 4, 256], BF16)
              sstg = sb2("sstg", [128, 4, 512], BF16)
              scols = sb2("scols", [128, 2, 2], F32)

              t_wcast = None
              for nm, dst in (("wq2", wqb), ("wk2", wkb), ("wv2", wvb), ("wo2", wob)):
                  for c in range(4):
                      p.wait("sync", t_wcast)
                      tld = p.op("sync", lambda e, a=h[nm], c=c: e.dma_start(
                          out=wtmp[:, 0:512], in_=a[c * 128:(c + 1) * 128, :]), dma="wtm")
                      p.waits("scalar", [tld, t_wcast])
                      t_wcast = p.op("scalar", lambda e, d=dst, c=c: e.copy(out=d[:, c, :], in_=wtmp[:, 0:512]))
              for c in range(4):
                  p.wait("sync", t_wcast)
                  tld = p.op("sync", lambda e, c=c: e.dma_start(
                      out=wtmp[:], in_=h["pw"][c * 128:(c + 1) * 128, :]), dma="wtm")
                  p.waits("scalar", [tld, t_wcast])
                  t_wcast = p.op("scalar", lambda e, c=c: e.copy(out=pwb[:, c, :], in_=wtmp[:]))

              t_pb = None
              for dg in range(4):
                  p.waits("tensor", [tl["onesf"], tl["pbr"], t_pb])
                  tb = p.op("tensor", lambda e, dg=dg: e.matmul(
                      out=psf[5][:], lhsT=onesf[:], rhs=pbr_sb[:, dg * 512:(dg + 1) * 512],
                      start=True, stop=True))
                  p.wait("vector", tb)
                  t_pb = p.op("vector", lambda e, dg=dg: e.tensor_copy(out=pbb[:, dg, :], in_=psf[5][:]))
                  if FENCE_MODE in ("recip", "small"): p.fence("vector")
                  p.wait("tensor", t_pb)

              bank_free = {i: None for i in range(6)}
              bankT_free = {i: None for i in range(2)}
              g_qhkh = None
              g_vh = None
              g_probs = [None, None]
              g_probsn = [None, None]
              g_pT = [None, None]
              g_ctx = None
              g_mha = None
              g_sstg = None

              for tau in range(NTAU):
                  for (mat, bias, outb, scl, qr) in ((wqb, bq8, qh, 0.125, 0), (wkb, bk2, kh, 1.0, 1)):
                      for c in range(4):
                          bk = psf[c % 2]
                          p.waits("tensor", [t_wcast, bank_free[c % 2]])
                          for kc in range(4):
                              p.op("tensor", lambda e, bk=bk, m=mat, c=c, kc=kc, tau=tau, qr=qr:
                                   e.matmul(out=bk[:], lhsT=m[:, kc, c * 128:(c + 1) * 128],
                                            rhs=qkv_fm[:, qr * 4 + kc, tau * 512:(tau + 1) * 512],
                                            start=(kc == 0), stop=(kc == 3)),
                                   inc=(1 if kc == 3 else 0))
                          tmm = p.token("tensor")
                          p.waits("scalar", [tmm, t_bq8, g_qhkh])
                          tev = p.op("scalar", lambda e, bk=bk, o=outb, c=c, b=bias, s=scl: e.activation(
                              out=o[:, c, :], in_=bk[:], func=AF.Identity, bias=b[:, c:c + 1], scale=s))
                          bank_free[c % 2] = tev
                  for tc2 in range(4):
                      bk = psf[2 + tc2 % 2]
                      p.waits("tensor", [t_wcast, bank_free[2 + tc2 % 2]])
                      for kc in range(4):
                          p.op("tensor", lambda e, bk=bk, tc2=tc2, kc=kc, tau=tau:
                               e.matmul(out=bk[:], lhsT=qkv_fm[:, 8 + kc, tau * 512 + tc2 * 128: tau * 512 + (tc2 + 1) * 128],
                                        rhs=wvb[:, kc, :], start=(kc == 0), stop=(kc == 3)),
                               inc=(1 if kc == 3 else 0))
                      tmm = p.token("tensor")
                      p.waits("scalar", [tmm, g_vh])
                      tev = p.op("scalar", lambda e, bk=bk, tc2=tc2: e.copy(out=vh[:, tc2, :], in_=bk[:]))
                      bank_free[2 + tc2 % 2] = tev

                  for b2 in range(2):
                      blk = tau * 2 + b2
                      for hh in range(8):
                          hp, eo = hh // 2, hh % 2
                          hd0 = 64 * eo
                          par = hh % 2
                          for qc in range(2):
                              bk = psf[qc]
                              p.waits("tensor", [bank_free[qc], tl["cmask"]])
                              tmm = p.op("tensor", lambda e, bk=bk, hp=hp, hd0=hd0, qc=qc, b2=b2:
                                         e.matmul(out=bk[:, 0:256],
                                                  lhsT=qh[hd0:hd0 + 64, hp, b2 * 256 + qc * 128: b2 * 256 + (qc + 1) * 128],
                                                  rhs=kh[hd0:hd0 + 64, hp, b2 * 256:(b2 + 1) * 256],
                                                  start=True, stop=True))
                              g_qhkh = tmm
                              p.wait("vector", tmm)
                              p.op("vector", lambda e, bk=bk, qc=qc: e.tensor_tensor(
                                  out=bk[:, 0:256], in0=bk[:, 0:256], in1=cmask[:, qc, :], op=ALU.add))
                              tmx = p.op("vector", lambda e, bk=bk, par=par: e.reduce_max(
                                  out=scols[:, par, 0:1], in_=bk[:, 0:256], axis=AX.X, negate=True))
                              p.fence("vector")
                              p.waits("scalar", [tmx] + ([g_probs[par]] if g_probs[par] else []))
                              tex = p.op("scalar", lambda e, bk=bk, par=par: e.activation(
                                  out=probs[:, par, :], in_=bk[:, 0:256], func=AF.Exp,
                                  bias=scols[:, par, 0:1], accum_out=scols[:, par, 1:2]))
                              bank_free[qc] = tex
                              p.waits("vector", [tex, g_probsn[par]])
                              p.op("vector", lambda e, par=par: e.reciprocal(out=scols[:, par, 1:2], in_=scols[:, par, 1:2]))
                              p.fence("vector")
                              tno = p.op("vector", lambda e, par=par: e.tensor_scalar_mul(
                                  out=probsn[:, par, :], in0=probs[:, par, :], scalar1=scols[:, par, 1:2]))
                              g_probs[par] = tno
                              for sc in range(qc + 1):
                                  bT = psT[sc % 2]
                                  p.waits("tensor", [tno, bankT_free[sc % 2], g_pT[par]])
                                  ttr = p.op("tensor", lambda e, bT=bT, sc=sc, par=par: e.transpose(
                                      out=bT[:, 0:128], in_=probsn[:, par, sc * 128:(sc + 1) * 128], identity=eyeb[:]))
                                  g_probsn[par] = ttr
                                  p.wait("vector", ttr)
                                  tevT = p.op("vector", lambda e, bT=bT, qc=qc, sc=sc, par=par: e.tensor_copy(
                                      out=pT[:, par, qc + sc, :], in_=bT[:, 0:128]))
                                  bankT_free[sc % 2] = tevT
                          tpT = p.token("vector")
                          cbi = 2 + hh % 2
                          cb = psf[cbi]
                          p.waits("tensor", [tpT, bank_free[cbi]])
                          p.op("tensor", lambda e, cb=cb, b2=b2, hh=hh, par=par:
                               e.matmul(out=cb[0:64, 0:128],
                                        lhsT=vh[:, b2 * 2, hh * 64:hh * 64 + 64],
                                        rhs=pT[:, par, 0, :], start=True, stop=True), inc=0)
                          p.op("tensor", lambda e, cb=cb, b2=b2, hh=hh, par=par:
                               e.matmul(out=cb[0:64, 128:256],
                                        lhsT=vh[:, b2 * 2, hh * 64:hh * 64 + 64],
                                        rhs=pT[:, par, 1, :], start=True, stop=False), inc=0)
                          tcx = p.op("tensor", lambda e, cb=cb, b2=b2, hh=hh, par=par:
                                     e.matmul(out=cb[0:64, 128:256],
                                              lhsT=vh[:, b2 * 2 + 1, hh * 64:hh * 64 + 64],
                                              rhs=pT[:, par, 2, :], start=False, stop=True), inc=1)
                          g_pT[par] = tcx
                          g_vh = tcx
                          p.waits("scalar", [tcx, g_ctx])
                          tevc = p.op("scalar", lambda e, cb=cb, hp=hp, eo=eo: e.activation(
                              out=ctxf[64 * eo:64 * eo + 64, hp, :], in_=cb[0:64, 0:256],
                              func=AF.Identity, bias=bv2[64 * eo:64 * eo + 64, hp:hp + 1]))
                          bank_free[cbi] = tevc
                      t_ctx = p.token("scalar")
                      for oc in range(4):
                          p.waits("tensor", [t_ctx, bank_free[4]])
                          for kc in range(4):
                              p.op("tensor", lambda e, oc=oc, kc=kc:
                                   e.matmul(out=psf[4][:, 0:256], lhsT=wob[:, kc, oc * 128:(oc + 1) * 128],
                                            rhs=ctxf[:, kc, :], start=(kc == 0), stop=(kc == 3)),
                                   inc=(1 if kc == 3 else 0))
                          tmm = p.token("tensor")
                          g_ctx = tmm
                          p.waits("scalar", [tmm, g_mha])
                          tev = p.op("scalar", lambda e, oc=oc: e.activation(
                              out=mha[:, oc, :], in_=psf[4][:, 0:256], func=AF.Identity, bias=bo2[:, oc:oc + 1]))
                          bank_free[4] = tev
                      t_mha = p.token("scalar")
                      for tc2 in range(2):
                          for dg in range(4):
                              bki = 4 + dg % 2
                              p.waits("tensor", [t_mha, bank_free[bki], t_pb])
                              for oc in range(4):
                                  p.op("tensor", lambda e, bki=bki, tc2=tc2, dg=dg, oc=oc:
                                       e.matmul(out=psf[bki][:], lhsT=mha[:, oc, tc2 * 128:(tc2 + 1) * 128],
                                                rhs=pwb[:, oc, dg * 512:(dg + 1) * 512],
                                                start=(oc == 0), stop=(oc == 3)),
                                       inc=(1 if oc == 3 else 0))
                              tmm = p.token("tensor")
                              g_mha = tmm
                              p.waits("vector", [tmm, g_sstg])
                              tev = p.op("vector", lambda e, bki=bki, dg=dg: e.tensor_tensor(
                                  out=sstg[:, dg, :], in0=psf[bki][:], in1=pbb[:, dg, :], op=ALU.add))
                              bank_free[bki] = tev
                          tevl = p.token("vector")
                          row0 = blk * 256 + tc2 * 128
                          p.wait("gpsimd", tevl)
                          g_sstg = p.op("gpsimd", lambda e, r=row0: e.dma_start(
                              out=s_scr[r:r + 128, :], in_=sstg[:].rearrange("p g n -> p (g n)")), dma="sst")

        p.barrier()

        with ExitStack() as e3:
          if PHASE_LIMIT >= 5:
              sb3 = lambda n, s, d: e3.enter_context(nc.sbuf_tensor(n, s, d))
              psB = [e3.enter_context(nc.psum_tensor(f"psB{i}", [128, 512], F32)) for i in range(8)]
              t_buf = sb3("t_buf", [128, NJC, 512], BF16)
              wslab = [sb3(f"wslab{i}", [128, 512], BF16) for i in range(4)]
              wdslab = [sb3(f"wdslab{i}", [128, D], BF16) for i in range(2)]
              sgb = sb3("sgb", [128, 512], BF16)
              uub = sb3("uub", [128, 512], BF16)
              q8b = sb3("q8b", [128, 512], FP8)
              tmx = sb3("tmx", [128, 512], F32)
              trow = sb3("trow", [1, 512], F32)
              qrow = sb3("qrow", [1, 512], F32)
              srow = sb3("srow", [1, 512], F32)
              qsbt = sb3("qsbt", [128, 512], F32)
              stbt = sb3("stbt", [128, 512], F32)
              sslab = sb3("sslab", [128, 512], BF16)
              xslab2 = [sb3(f"xslab2{i}", [128, 512], F32) for i in range(2)]
              omb = sb3("omb", [128, 512], F32)
              gyb = sb3("gyb", [128, 512], BF16)

              g_tbuf = None
              prev_wd = [None, None]
              g_sgb = None
              g_sslab = None
              g_x2 = [None, None]
              g_omb = None
              g_gyb = None

              bank_free = {i: None for i in range(8)}
              prev_w = [None] * 4
              for tau in range(NTAU):
                  for G in range(NG):
                      for d in range(NDC):
                          for mi, scr in enumerate((wgq_scr, wuq_scr)):
                              wb = wslab[(d * 2 + mi) % 4]
                              p.wait("sync", prev_w[(d * 2 + mi) % 4])
                              tld = p.op("sync", lambda e, b=wb, s=scr, d=d, G=G: e.dma_start(
                                  out=b[:], in_=s[d * 128:(d + 1) * 128, G * 512:(G + 1) * 512]), dma=f"ws{(d * 2 + mi) % 4}")
                              p.waits("tensor", [tld] + ([bank_free[i] for i in range(8)] + [g_tbuf] if d == 0 else []))
                              for cc in range(4):
                                  bk = psB[mi * 4 + cc]
                                  p.op("tensor", lambda e, bk=bk, b=wb, cc=cc, d=d, tau=tau:
                                       e.matmul(out=bk[:], lhsT=b[:, cc * 128:(cc + 1) * 128],
                                                rhs=qx_fm[:, d, tau * 512:(tau + 1) * 512],
                                                start=(d == 0), stop=(d == NDC - 1)),
                                       inc=(1 if cc == 3 else 0))
                              prev_w[(d * 2 + mi) % 4] = p.token("tensor")
                      t_gu = p.token("tensor")
                      for cc in range(4):
                          c = G * 4 + cc
                          p.waits("scalar", [t_gu, g_sgb])
                          tg = p.op("scalar", lambda e, cc=cc, c=c: e.activation(
                              out=sgb[:], in_=psB[cc][:], func=AF.Silu, scale=wsg[:, c:c + 1]))
                          tu = p.op("scalar", lambda e, cc=cc, c=c: e.activation(
                              out=uub[:], in_=psB[4 + cc][:], func=AF.Copy, scale=wsu[:, c:c + 1]))
                          p.waits("vector", [tu, g_tbuf])
                          tt2 = p.op("vector", lambda e, c=c: e.tensor_tensor(
                              out=t_buf[:, c, :], in0=sgb[:], in1=uub[:], op=ALU.mult))
                          g_sgb = tt2
                          bank_free[cc] = tt2
                          bank_free[4 + cc] = tt2
                  g_tbuf = None

                  p.op("vector", lambda e: e.tensor_reduce(
                      out=tmx[:], in_=t_buf[:].rearrange("p c s -> p s c"),
                      axis=AX.X, op=ALU.max, apply_absolute_value=True))
                  if FENCE_MODE in ("recip", "small"): p.fence("vector")
                  t_amax = p.token("vector")
                  p.wait("gpsimd", t_amax)
                  t_crow = p.op("gpsimd", lambda e: e.tensor_reduce(
                      out=trow[:], in_=tmx[:], axis=AX.C, op=ALU.max))
                  p.wait("vector", t_crow)
                  p.op("vector", lambda e: e.reciprocal(out=qrow[:], in_=trow[:]))
                  if FENCE_MODE in ("recip", "small"): p.fence("vector")
                  p.op("vector", lambda e: e.tensor_scalar_mul(out=qrow[:], in0=qrow[:], scalar1=224.0))
                  if FENCE_MODE in ("recip", "small"): p.fence("vector")
                  p.op("vector", lambda e: e.tensor_scalar_mul(out=srow[:], in0=trow[:], scalar1=1.0 / 224.0))
                  if FENCE_MODE in ("recip", "small"): p.fence("vector")
                  t_rows = p.token("vector")
                  for (row, dst) in ((qrow, qsbt), (srow, stbt)):
                      p.waits("tensor", [t_rows, tl["onesf"], bank_free[0]])
                      tb = p.op("tensor", lambda e, r=row: e.matmul(
                          out=psB[0][:], lhsT=onesf[:], rhs=r[:], start=True, stop=True))
                      p.wait("vector", tb)
                      tcp = p.op("vector", lambda e, ds=dst: e.tensor_copy(out=ds[:], in_=psB[0][:]))
                      if FENCE_MODE in ("recip", "small"): p.fence("vector")
                      bank_free[0] = tcp
                      p.wait("tensor", tcp)
                  for c in range(NJC):
                      p.op("vector", lambda e, c=c: e.tensor_tensor(
                          out=q8b[:], in0=t_buf[:, c, :], in1=qsbt[:], op=ALU.mult))
                      p.op("vector", lambda e, c=c: e.tensor_tensor(
                          out=t_buf[:, c, :], in0=q8b[:], in1=stbt[:], op=ALU.mult))
                  t_qt = p.token("vector")

                  for hgq in range(4):
                      for j in range(NJC):
                          wb = wdslab[j % 2]
                          p.wait("sync", prev_wd[j % 2])
                          tld = p.op("sync", lambda e, b=wb, j=j, hgq=hgq: e.dma_start(
                              out=b[:, 0:512], in_=wdq_scr[j * 128:(j + 1) * 128, hgq * 512:(hgq + 1) * 512]), dma=f"wdl{j % 2}")
                          p.waits("tensor", [tld, t_qt] + ([bank_free[i] for i in range(8)] if j == 0 else []))
                          for tc2 in range(4):
                              bk = psB[tc2]
                              p.op("tensor", lambda e, bk=bk, b=wb, j=j, tc2=tc2:
                                   e.matmul(out=bk[:],
                                            lhsT=t_buf[:, j, tc2 * 128:(tc2 + 1) * 128],
                                            rhs=b[:, 0:512],
                                            start=(j == 0), stop=(j == NJC - 1)),
                                   inc=(1 if tc2 == 3 else 0))
                          prev_wd[j % 2] = p.token("tensor")
                      t_dn = p.token("tensor")
                      g_tbuf = t_dn
                      for tc2 in range(4):
                          row0 = (tau * 4 + tc2) * 128
                          bk = psB[tc2]
                          p.waits("scalar", [g_x2[hgq % 2]])
                          tlx = p.op("scalar", lambda e, b=xslab2[hgq % 2], r=row0, hgq=hgq: e.dma_start(
                              out=b[:], in_=h["x"][r:r + 128, hgq * 512:(hgq + 1) * 512]), dma=f"x2{hgq % 2}")
                          p.waits("scalar", [g_sslab])
                          tls = p.op("scalar", lambda e, r=row0, hgq=hgq: e.dma_start(
                              out=sslab[:], in_=s_scr[r:r + 128, hgq * 512:(hgq + 1) * 512]), dma="ssl")
                          p.waits("scalar", [t_dn, g_gyb])
                          tg = p.op("scalar", lambda e, bk=bk: e.activation(
                              out=gyb[:], in_=bk[:], func=AF.Silu))
                          bank_free[tc2] = tg
                          p.waits("vector", [tg, tls, tlx, g_omb])
                          p.op("vector", lambda e: e.tensor_tensor(out=omb[:], in0=sslab[:], in1=gyb[:], op=ALU.mult))
                          g_sslab = p.token("vector")
                          g_gyb = g_sslab
                          tcb = p.op("vector", lambda e, b=xslab2[hgq % 2]: e.tensor_tensor(
                              out=omb[:], in0=omb[:], in1=b[:], op=ALU.add))
                          g_x2[hgq % 2] = tcb
                          p.wait("gpsimd", tcb)
                          g_omb = p.op("gpsimd", lambda e, r=row0, hgq=hgq: e.dma_start(
                              out=h["out"][r:r + 128, hgq * 512:(hgq + 1) * 512], in_=omb[:]), dma="out")


        if DEBUG_DUMP:
            p.barrier()
            if PHASE_LIMIT >= 1:
                p.op("sync", lambda e: e.dma_start(out=h["dbg_wg"], in_=wgq_scr[:, :]), dma="out")
                p.op("sync", lambda e: e.dma_start(out=h["dbg_wd"], in_=wdq_scr[:, :]), dma="out")
            if PHASE_LIMIT >= 2:
                p.op("sync", lambda e: e.dma_start(out=h["dbg_qx"], in_=qx_fm[:]), dma="out")
            if PHASE_LIMIT >= 4:
                p.op("sync", lambda e: e.dma_start(out=h["dbg_s"], in_=s_scr[:, :]), dma="out")
            p.items["sync"].append(("wait", "out", p.count["out"]))
        if "out" in p.count:
            p.items["gpsimd"].append(("wait", "out", p.count["out"]))
        else:
            for cname in list(p.chans):
                p.items["gpsimd"].append(("wait", cname, p.count[cname]))
        finish()
        with nc.Block() as block:
            p.emit(block, sems)
    return nc


_NC_CACHE = None


def _get_program():
    global _NC_CACHE
    if _NC_CACHE is None:
        _NC_CACHE = build_program()
    return _NC_CACHE


def kernel(**inputs):
    f32 = np.float32
    x = np.asarray(inputs["L0_attn_x"], f32)
    cw = np.ascontiguousarray(np.asarray(inputs["L0_attn_conv_w"], f32))
    cb2 = np.ascontiguousarray(np.asarray(inputs["L0_attn_conv_b"], f32).reshape(12, 128).T)
    wq2 = np.ascontiguousarray(np.asarray(inputs["L0_attn_wq"], f32).reshape(512, 512))
    wk2 = np.ascontiguousarray(np.asarray(inputs["L0_attn_wk"], f32).reshape(512, 512))
    wv2 = np.ascontiguousarray(np.asarray(inputs["L0_attn_wv"], f32).reshape(512, 512))
    wo2 = np.ascontiguousarray(np.asarray(inputs["L0_attn_wo"], f32).reshape(512, 512))
    bq2 = np.ascontiguousarray(np.asarray(inputs["L0_attn_bq"], f32).reshape(4, 128).T)
    bk2 = np.ascontiguousarray(np.asarray(inputs["L0_attn_bk"], f32).reshape(4, 128).T)
    bv2 = np.ascontiguousarray(np.asarray(inputs["L0_attn_bv"], f32).reshape(4, 128).T)
    bo2 = np.ascontiguousarray(np.asarray(inputs["L0_attn_bo"], f32).reshape(4, 128).T)
    pw = np.ascontiguousarray(np.asarray(inputs["L0_attn_proj_w"], f32))
    pbr = np.ascontiguousarray(np.asarray(inputs["L0_attn_proj_b"], f32).reshape(1, D))
    wg = np.ascontiguousarray(np.asarray(inputs["L0_ffn_w_gate"], f32))
    wu = np.ascontiguousarray(np.asarray(inputs["L0_ffn_w_up"], f32))
    wd = np.ascontiguousarray(np.asarray(inputs["L0_ffn_w_down"], f32))
    wgT = np.ascontiguousarray(wg.T)
    wuT = np.ascontiguousarray(wu.T)
    wdT = np.ascontiguousarray(wd.T)
    didx = np.ascontiguousarray(np.asarray(inputs["L0_ffn_down_reorder_index"], np.int32))
    eyef = np.eye(128, dtype=f32)
    onesf = np.ones((1, 128), f32)
    jj = np.arange(256)[None, :]
    pp = np.arange(128)[:, None]
    cmask = np.ascontiguousarray(np.stack([
        np.where(jj <= pp, 0.0, -1e9).astype(f32),
        np.where(jj <= 128 + pp, 0.0, -1e9).astype(f32)], axis=1))

    shared = dict(cw=cw, cb2=cb2, wq2=wq2, wk2=wk2, wv2=wv2, wo2=wo2,
                  bq2=bq2, bk2=bk2, bv2=bv2, bo2=bo2, pw=pw, pbr=pbr,
                  wg=wg, wu=wu, wd=wd, wgT=wgT, wuT=wuT, wdT=wdT,
                  didx=didx, eyef=eyef, onesf=onesf, cmask=cmask)
    in_maps = []
    for c in range(NCORES):
        seq, half = c // 2, c % 2
        t0 = half * T
        m = dict(shared)
        m["x"] = np.ascontiguousarray(x[seq, t0:t0 + T])
        m["xh"] = (np.zeros((HALO, D), f32) if half == 0
                   else np.ascontiguousarray(x[seq, t0 - HALO:t0]))
        in_maps.append(m)

    nc = _get_program()
    res = run_bass_kernel_spmd(nc, in_maps, core_ids=list(range(NCORES)))
    out = np.empty((4, 4096, D), f32)
    for c in range(NCORES):
        seq, half = c // 2, c % 2
        out[seq, half * T:(half + 1) * T] = np.asarray(res.results[c]["out"])
    return out


if __name__ == "__main__":
    build_program()
    print("built ok")


# version 09
# speedup vs baseline: 1.8452x; 1.0400x over previous
import numpy as np
import concourse.bass as bass
import concourse.mybir as mybir
from concourse.bass_utils import run_bass_kernel_spmd
from contextlib import ExitStack

F32 = mybir.dt.float32
BF16 = mybir.dt.bfloat16
FP8 = mybir.dt.float8e4
I32 = mybir.dt.int32
AF = mybir.ActivationFunctionType
ALU = mybir.AluOpType
AX = mybir.AxisListType

NCORES = 8
T = 2048
D = 2048
QKV = 1536
HD = 512
FF = 5632
NTC = 16
NTAU = 4
NDC = 16
NJC = 44
NG = 11
HALO = 16
CS = 2064

ENGINES = ("sync", "scalar", "vector", "tensor", "gpsimd")

DEBUG_SIM = False
SELF_WAIT_ENGINES = None
FENCE_MODE = "small"
SKIP_HALO = False
DEBUG_DUMP = False
PHASE_LIMIT = 5


class Prog:

    def __init__(self):
        self.items = {e: [] for e in ENGINES}
        self.count = {}
        for e in ENGINES:
            self.count[e] = 0
        self.chans = []
        self.bound = {e: {} for e in ENGINES}

    def chan(self, name):
        if name not in self.count:
            self.count[name] = 0
            self.chans.append(name)
        return name

    def op(self, eng, fn, dma=None, inc=None):
        if dma is not None:
            sem = self.chan(dma)
            amt = 16
        else:
            sem = eng
            amt = 1 if inc is None else inc
            if DEBUG_SIM and FENCE_MODE == "all" and (SELF_WAIT_ENGINES is None or eng in SELF_WAIT_ENGINES):
                amt = max(amt, 1)
                if self.count[eng] > 0:
                    self.items[eng].append(("wait", eng, self.count[eng]))
        self.items[eng].append(("op", fn, sem, amt))
        self.count[sem] += amt
        return (sem, self.count[sem]) if amt else None

    def fence(self, eng):
        if self.count.get(eng, 0) > 0:
            self.items[eng].append(("wait", eng, self.count[eng]))

    def wait(self, eng, tok):
        if tok is None:
            return
        sem, val = tok
        if sem == eng or self.bound[eng].get(sem, 0) >= val:
            return
        self.items[eng].append(("wait", sem, val))
        self.bound[eng][sem] = val

    def waits(self, eng, toks):
        for t in toks:
            self.wait(eng, t)

    def barrier(self):
        snap = dict(self.count)
        for e in ENGINES:
            for s, v in snap.items():
                if s != e and v > 0:
                    self.wait(e, (s, v))

    def token(self, sem):
        return (sem, self.count[sem]) if self.count.get(sem, 0) else None

    def emit(self, block, sems):
        def mk(ename):
            def emitter(eng):
                for it in self.items[ename]:
                    if it[0] == "wait":
                        eng.wait_ge(sems[it[1]], it[2])
                    else:
                        ins = it[1](eng)
                        if it[3]:
                            ins.then_inc(sems[it[2]], it[3])
            return emitter
        block.sync(mk("sync"))
        block.scalar(mk("scalar"))
        block.vector(mk("vector"))
        block.tensor(mk("tensor"))
        block.gpsimd(mk("gpsimd"))


def build_program():
    nc = bass.Bass("TRN2", target_bir_lowering=False, debug=False,
                   detect_race_conditions=DEBUG_SIM)
    h = {}

    def inp(name, shape, dt=F32):
        h[name] = nc.dram_tensor(name, shape, dt, kind="ExternalInput").ap()

    inp("x", [T, D]); inp("xh", [HALO, D])
    inp("cw", [3, D, QKV]); inp("cb2", [128, 12])
    for nm in ("wq2", "wk2", "wv2", "wo2"):
        inp(nm, [HD, HD])
    for nm in ("bq2", "bk2", "bv2", "bo2"):
        inp(nm, [128, 4])
    inp("pw", [HD, D]); inp("pbr", [1, D])
    inp("wg", [FF, D]); inp("wu", [FF, D]); inp("wd", [D, FF])
    inp("wgT", [D, FF]); inp("wuT", [D, FF]); inp("wdT", [FF, D])
    inp("didx", [FF], I32)
    inp("eyef", [128, 128]); inp("onesf", [1, 128]); inp("cmask", [128, 2, 256])
    h["out"] = nc.dram_tensor("out", [T, D], F32, kind="ExternalOutput").ap()
    if DEBUG_DUMP:
        h["dbg_s"] = nc.dram_tensor("dbg_s", [T, D], BF16, kind="ExternalOutput").ap()
        h["dbg_wg"] = nc.dram_tensor("dbg_wg", [D, FF], BF16, kind="ExternalOutput").ap()
        h["dbg_wd"] = nc.dram_tensor("dbg_wd", [FF, D], BF16, kind="ExternalOutput").ap()
        h["dbg_qx"] = nc.dram_tensor("dbg_qx", [128, NDC, T], BF16, kind="ExternalOutput").ap()

    xn_scr = nc.dram_tensor("xn_scr", [NDC, 128, CS], BF16, kind="Internal").ap()
    s_scr = nc.dram_tensor("s_scr", [T, D], BF16, kind="Internal").ap()
    q8_scr = nc.dram_tensor("q8_scr", [T, D], FP8, kind="Internal").ap()
    wgq_scr = nc.dram_tensor("wgq_scr", [D, FF], BF16, kind="Internal").ap()
    wuq_scr = nc.dram_tensor("wuq_scr", [D, FF], BF16, kind="Internal").ap()
    wdq_scr = nc.dram_tensor("wdq_scr", [FF, D], BF16, kind="Internal").ap()

    p = Prog()
    es = ExitStack()
    with es:
        sems = {e: es.enter_context(nc.semaphore("sem_" + e)) for e in ENGINES}
        def finish():
            for cname in p.chans:
                sems[cname] = es.enter_context(nc.semaphore("sem_" + cname))
        sb = lambda name, shape, dt: es.enter_context(nc.sbuf_tensor(name, shape, dt))

        eyef = sb("c_eyef", [128, 128], F32)
        eyeb = sb("c_eyeb", [128, 128], BF16)
        onesf = sb("c_onesf", [1, 128], F32)
        cmask = sb("c_cmask", [128, 2, 256], F32)
        cb2 = sb("c_cb2", [128, 12], F32)
        bq8 = sb("c_bq8", [128, 4], F32)
        bk2 = sb("c_bk2", [128, 4], F32)
        bv2 = sb("c_bv2", [128, 4], F32)
        bo2 = sb("c_bo2", [128, 4], F32)
        wsg = sb("c_wsg", [128, NJC], F32)
        wsu = sb("c_wsu", [128, NJC], F32)
        epsc = sb("c_eps", [128, 1], F32)

        tl = {}
        for nm, dst in (("eyef", eyef), ("onesf", onesf), ("cmask", cmask),
                        ("cb2", cb2), ("bk2", bk2), ("bv2", bv2), ("bo2", bo2)):
            tl[nm] = p.op("sync", lambda e, d=dst, a=h[nm]: e.dma_start(out=d[:], in_=a[:]), dma="ld_" + nm)
        tl["bq2"] = p.op("sync", lambda e: e.dma_start(out=bq8[:], in_=h["bq2"][:]), dma="ld_bq2")
        p.wait("vector", tl["bq2"])
        p.op("vector", lambda e: e.tensor_scalar_mul(out=bq8[:], in0=bq8[:], scalar1=0.125))
        if FENCE_MODE in ("recip", "small"): p.fence("vector")
        t_bq8 = p.token("vector")
        p.wait("scalar", tl["eyef"])
        t_eyeb = p.op("scalar", lambda e: e.copy(out=eyeb[:], in_=eyef[:]))
        t_eps = p.op("vector", lambda e: e.memset(epsc[:], 1e-5))

        sx_all = sb("sx_all", [128, NTC], F32)
        _DONE = object()

        def pull(g, n):
            for _ in range(n):
                if next(g, _DONE) is _DONE:
                    return False
            return True

        t_conv = {}
        with ExitStack() as ew:
          if PHASE_LIMIT >= 1:
              sbw = lambda n, s, d: ew.enter_context(nc.sbuf_tensor(n, s, d))
              idx_s = sbw("idx_s", [128, NJC], I32)
              wnat = [sbw(f"wnat{i}", [128, 3072], F32) for i in range(2)]
              rcol = sbw("rcol", [128, 4], F32)
              q8t = [sbw(f"q8t{i}", [128, 512], FP8) for i in range(2)]
              qbf = [sbw(f"qbf{i}", [128, FF], BF16) for i in range(2)]
              qsb_g = sbw("qsb_g", [128, NG, 512], F32)
              qsb_u = sbw("qsb_u", [128, NG, 512], F32)
              qsdb = [sbw(f"qsdb{i}", [128, 512], F32) for i in range(4)]
              wsdb = [sbw(f"wsdb{i}", [128, 512], F32) for i in range(4)]
              wdqt = [sbw(f"wdqt{i}", [128, D], BF16) for i in range(2)]

              t_idx = p.op("sync", lambda e: e.dma_start(
                  out=idx_s[:], in_=h["didx"].rearrange("(c p) -> p c", p=128),
                  allow_slow_non_contiguous=True), dma="ld_idx")

              prev_wnat = [None, None]
              wtok = {}

              rmax_all = sbw("rmax_all", [128, NJC], F32)
              qscol_all = sbw("qscol_all", [128, NJC], F32)
              wscol_d = sbw("wscol_d", [128, NDC], F32)

              def w_maxes(src, n_chunks, row_len, qrow, wscol, wrow, psW):
                  prev = prev_wnat
                  for c in range(n_chunks):
                      halves = [(0, min(3072, row_len))]
                      if row_len > 3072:
                          halves.append((3072, row_len))
                      for hi, (c0, c1) in enumerate(halves):
                          bi = (c + hi) % 2
                          buf = wnat[bi]
                          p.wait("sync", prev[bi])
                          tld = p.op("sync", lambda e, b=buf, s=src, c=c, c0=c0, c1=c1: e.dma_start(
                              out=b[:, 0:c1 - c0], in_=s[c * 128:(c + 1) * 128, c0:c1]), dma=f"wn{bi}")
                          p.waits("vector", [tld])
                          if hi == 0:
                              p.op("vector", lambda e, b=buf, n=c1 - c0, c=c: e.reduce_max(
                                  out=rmax_all[:, c:c + 1], in_=b[:, 0:n], axis=AX.X, apply_absolute_value=True))
                          else:
                              p.op("vector", lambda e, b=buf, n=c1 - c0: e.reduce_max(
                                  out=rcol[:, 0:1], in_=b[:, 0:n], axis=AX.X, apply_absolute_value=True))
                              p.fence("vector")
                              p.op("vector", lambda e, c=c: e.tensor_tensor(
                                  out=rmax_all[:, c:c + 1], in0=rmax_all[:, c:c + 1], in1=rcol[:, 0:1], op=ALU.max))
                              p.fence("vector")
                          prev[bi] = p.token("vector")
                      yield
                  p.fence("vector")
                  p.op("vector", lambda e, n=n_chunks: e.reciprocal(out=qscol_all[:, 0:n], in_=rmax_all[:, 0:n]))
                  p.fence("vector")
                  p.op("vector", lambda e, n=n_chunks: e.tensor_scalar_mul(out=qscol_all[:, 0:n], in0=qscol_all[:, 0:n], scalar1=224.0))
                  p.fence("vector")
                  if wscol is not None:
                      p.op("vector", lambda e, n=n_chunks: e.tensor_scalar_mul(out=wscol[:, 0:n], in0=rmax_all[:, 0:n], scalar1=1.0 / 224.0))
                      p.fence("vector")
                  else:
                      p.op("vector", lambda e, n=n_chunks: e.tensor_scalar_mul(out=wscol_d[:, 0:n], in0=rmax_all[:, 0:n], scalar1=1.0 / 224.0))
                      p.fence("vector")
                  tv = p.token("vector")
                  def cols_to_row(colsrc, rowdst, n):
                      bank_cp = [None, None]
                      p.waits("tensor", [p.token("vector"), tl["eyef"]])
                      for c in range(n):
                          bi = c % 2
                          p.wait("tensor", bank_cp[bi])
                          tt = p.op("tensor", lambda e, c=c, bi=bi, cs=colsrc: e.transpose(
                              out=psW[bi][0:1, 0:128], in_=cs[:, c:c + 1], identity=eyef[:]))
                          p.wait("vector", tt)
                          p.op("vector", lambda e, q=rowdst, c=c, bi=bi: e.tensor_copy(
                              out=q[:, c * 128:(c + 1) * 128], in_=psW[bi][0:1, 0:128]))
                          bank_cp[bi] = p.token("vector")
                      p.fence("vector")
                      tq = p.token("vector")
                      p.wait("tensor", tq)
                      return tq

                  cols_to_row(qscol_all, qrow, n_chunks)
                  if wscol is None:
                      cols_to_row(wscol_d, wrow, n_chunks)
                  return p.token("vector")

              def w_bcast_groups(qrow, qdep, qsb_all, psW):
                  last = None
                  for G in range(NG):
                      p.waits("tensor", [qdep, tl["onesf"], last])
                      tb = p.op("tensor", lambda e, G=G, q=qrow: e.matmul(
                          out=psW[1][:], lhsT=onesf[:], rhs=q[:, G * 512:(G + 1) * 512],
                          start=True, stop=True))
                      p.wait("vector", tb)
                      p.op("vector", lambda e, G=G, qa=qsb_all: e.tensor_copy(out=qa[:, G, :], in_=psW[1][:]))
                      p.fence("vector")
                      last = p.token("vector")
                      p.wait("tensor", last)
                  return last

              wq_state = {"prev_st": [None, None], "q8_free": [None, None]}

              def w_quant(srcT, dst_scr, key, qsb_all):
                  prev_ld = prev_wnat
                  prev_st = wq_state["prev_st"]
                  q8_free = wq_state["q8_free"]
                  for d in range(NDC):
                      ob = qbf[d % 2]
                      tcast = None
                      for hi, (G0, G1) in enumerate(((0, 6), (6, NG))):
                          buf = wnat[(d + hi) % 2]
                          p.wait("sync", prev_ld[(d + hi) % 2])
                          tld = p.op("sync", lambda e, b=buf, s=srcT, d=d, G0=G0, G1=G1: e.dma_start(
                              out=b[:, 0:(G1 - G0) * 512], in_=s[d * 128:(d + 1) * 128, G0 * 512:G1 * 512]), dma=f"wt{(d + hi) % 2}")
                          p.waits("vector", [tld, wtok[key], prev_st[d % 2]])
                          for G in range(G0, G1):
                              qb = q8t[G % 2]
                              p.waits("vector", [q8_free[G % 2]])
                              t8 = p.op("vector", lambda e, b=buf, G=G, G0=G0, qb=qb, qa=qsb_all: e.tensor_tensor(
                                  out=qb[:], in0=b[:, (G - G0) * 512:(G - G0 + 1) * 512], in1=qa[:, G, :], op=ALU.mult))
                              p.waits("scalar", [t8, tcast, prev_st[d % 2]])
                              tcast = p.op("scalar", lambda e, o=ob, G=G, qb=qb: e.copy(
                                  out=o[:, G * 512:(G + 1) * 512], in_=qb[:]))
                              q8_free[G % 2] = tcast
                          prev_ld[(d + hi) % 2] = p.token("vector")
                      p.wait("sync", tcast)
                      ts = p.op("sync", lambda e, o=ob, ds=dst_scr, d=d: e.dma_start(
                          out=ds[d * 128:(d + 1) * 128, :], in_=o[:]), dma=f"wqs{d % 2}")
                      prev_st[d % 2] = ts
                      yield

              def w_wd():
                  prev_ld = prev_wnat
                  prev_sc = [None, None]
                  for c in range(NJC):
                      buf = wnat[c % 2]
                      p.wait("sync", prev_ld[c % 2])
                      tld = p.op("sync", lambda e, b=buf, c=c: e.dma_start(
                          out=b[:, 0:D], in_=h["wdT"][c * 128:(c + 1) * 128, :]), dma=f"wn{c % 2}")
                      ob = wdqt[c % 2]
                      p.waits("vector", [tld, wtok["b4"], prev_ld[c % 2], prev_sc[c % 2]])
                      for hg in range(4):
                          p.op("vector", lambda e, b=buf, hg=hg: e.tensor_tensor(
                              out=q8t[0][:], in0=b[:, hg * 512:(hg + 1) * 512], in1=qsdb[hg][:], op=ALU.mult))
                          p.op("vector", lambda e, o=ob, hg=hg: e.tensor_tensor(
                              out=o[:, hg * 512:(hg + 1) * 512], in0=q8t[0][:], in1=wsdb[hg][:], op=ALU.mult))
                      tf = p.token("vector")
                      prev_ld[c % 2] = tf
                      p.waits("gpsimd", [tf, t_idx, prev_sc[c % 2]])
                      tsc = p.op("gpsimd", lambda e, o=ob, c=c: e.indirect_dma_start(
                          out=wdq_scr[:, :],
                          out_offset=bass.IndirectOffsetOnAxis(ap=idx_s[:, c:c + 1], axis=0),
                          in_=o[:], in_offset=None), dma=f"sca{c % 2}")
                      prev_sc[c % 2] = tsc
                      yield

              def gw2():
                  yield from w_quant(h["wgT"], wgq_scr, "bg", qsb_g)
                  yield from w_quant(h["wuT"], wuq_scr, "bu", qsb_u)
                  yield from w_wd()

              with ExitStack() as ew2:
                  sbw2 = lambda n, s, d: ew2.enter_context(nc.sbuf_tensor(n, s, d))
                  psW = [ew2.enter_context(nc.psum_tensor(f"psW{i}", [128, 512], F32)) for i in range(2)]
                  qrow_g = sbw2("qrow_g", [1, FF], F32)
                  qrow_d = sbw2("qrow_d", [1, D], F32)
                  wrow_d = sbw2("wrow_d", [1, D], F32)

                  def gw1():
                      t_mg = yield from w_maxes(h["wg"], NJC, D, qrow_g, wsg, None, psW)
                      wtok["bg"] = w_bcast_groups(qrow_g, t_mg, qsb_g, psW)
                      yield
                      t_mu = yield from w_maxes(h["wu"], NJC, D, qrow_g, wsu, None, psW)
                      wtok["bu"] = w_bcast_groups(qrow_g, t_mu, qsb_u, psW)
                      yield
                      t_md = yield from w_maxes(h["wd"], NDC, FF, qrow_d, None, wrow_d, psW)
                      t_b4 = None
                      for hg in range(4):
                          for src_row, dstl in ((qrow_d, qsdb), (wrow_d, wsdb)):
                              p.waits("tensor", [t_md, tl["onesf"], t_b4])
                              tb = p.op("tensor", lambda e, hg=hg, r=src_row: e.matmul(
                                  out=psW[1][:], lhsT=onesf[:], rhs=r[:, hg * 512:(hg + 1) * 512],
                                  start=True, stop=True))
                              p.wait("vector", tb)
                              p.op("vector", lambda e, hg=hg, dl=dstl: e.tensor_copy(out=dl[hg][:], in_=psW[1][:]))
                              p.fence("vector")
                              t_b4 = p.token("vector")
                              p.wait("tensor", t_b4)
                      wtok["b4"] = t_b4
                      yield

                  with ExitStack() as e0:
                    if PHASE_LIMIT >= 2:
                      sb0 = lambda n, s, d: e0.enter_context(nc.sbuf_tensor(n, s, d))
                      psb = [e0.enter_context(nc.psum_tensor(f"ps0b{i}", [128, 512], BF16)) for i in range(2)]
                      xt = [sb0(f"xt{i}", [128, D], F32) for i in range(2)]
                      xnf = sb0("xnf", [128, D], F32)
                      junk = sb0("junk", [128, D], BF16)
                      xnb = sb0("xnb", [128, D], BF16)
                      q8x = sb0("q8x", [128, D], FP8)
                      cols = sb0("cols", [128, 8], F32)
                      stg = sb0("stg", [128, NDC, 128], BF16)

                      st0 = {"guard_xt": [[], []], "g_xnf": [], "g_q8x": [], "g_stg": None,
                             "bank_tok": [None, None]}

                      def p0_tile(tc):
                          halo = (tc == NTC)
                          P = HALO if halo else 128
                          buf = xt[tc % 2]
                          src = h["xh"][:] if halo else h["x"][tc * 128:(tc + 1) * 128, :]
                          p.waits("sync", st0["guard_xt"][tc % 2])
                          tld = p.op("sync", lambda e, b=buf, s=src, P=P: e.dma_start(out=b[0:P, :], in_=s), dma=f"xt{tc % 2}")
                          p.waits("vector", [tld] + st0["g_xnf"])
                          p.op("vector", lambda e, b=buf, P=P: e.reduce_sum(out=cols[0:P, 0:1], in_=b[0:P, :], axis=AX.X))
                          p.fence("vector")
                          p.wait("scalar", tld)
                          tsq = p.op("scalar", lambda e, b=buf, P=P: e.activation(
                              out=junk[0:P, :], in_=b[0:P, :], func=AF.Square, accum_out=cols[0:P, 1:2]))
                          p.wait("vector", tsq)
                          p.op("vector", lambda e, P=P: e.tensor_scalar_mul(out=cols[0:P, 2:3], in0=cols[0:P, 0:1], scalar1=1.0 / D))
                          p.fence("vector")
                          p.op("vector", lambda e, P=P: e.tensor_scalar_mul(out=cols[0:P, 3:4], in0=cols[0:P, 1:2], scalar1=1.0 / D))
                          p.fence("vector")
                          p.op("vector", lambda e, P=P: e.tensor_tensor(out=cols[0:P, 4:5], in0=cols[0:P, 2:3], in1=cols[0:P, 2:3], op=ALU.mult))
                          p.fence("vector")
                          tv = p.op("vector", lambda e, P=P: e.tensor_tensor(out=cols[0:P, 4:5], in0=cols[0:P, 3:4], in1=cols[0:P, 4:5], op=ALU.subtract))
                          p.fence("vector")
                          p.waits("scalar", [tv, t_eps])
                          tsd = p.op("scalar", lambda e, P=P: e.activation(out=cols[0:P, 5:6], in_=cols[0:P, 4:5], func=AF.Sqrt, bias=epsc[0:P, 0:1]))
                          p.wait("vector", tsd)
                          p.op("vector", lambda e, P=P: e.reciprocal(out=cols[0:P, 6:7], in_=cols[0:P, 5:6]))
                          p.fence("vector")
                          txn = p.op("vector", lambda e, b=buf, P=P: e.tensor_scalar(
                              out=xnf[0:P, :], in0=b[0:P, :], scalar1=cols[0:P, 2:3], scalar2=cols[0:P, 6:7],
                              op0=ALU.subtract, op1=ALU.mult))
                          st0["guard_xt"][tc % 2] = [txn, tsq]
                          p.wait("scalar", txn)
                          txb = p.op("scalar", lambda e, P=P: e.copy(out=xnb[0:P, :], in_=xnf[0:P, :]))
                          st0["g_xnf"] = [txb]

                          if not halo:
                              p.op("vector", lambda e: e.reduce_max(out=cols[:, 0:1], in_=xnf[:], axis=AX.X, apply_absolute_value=True))
                              p.fence("vector")
                              p.op("vector", lambda e: e.reciprocal(out=cols[:, 1:2], in_=cols[:, 0:1]))
                              p.fence("vector")
                              tqs = p.op("vector", lambda e: e.tensor_scalar_mul(out=cols[:, 2:3], in0=cols[:, 1:2], scalar1=224.0))
                              p.fence("vector")
                              p.op("vector", lambda e, tc=tc: e.tensor_scalar_mul(out=sx_all[:, tc:tc + 1], in0=cols[:, 0:1], scalar1=1.0 / 224.0))
                              p.fence("vector")
                              p.waits("scalar", [tqs] + st0["g_q8x"])
                              t8 = p.op("scalar", lambda e: e.activation(out=q8x[:], in_=xnf[:], func=AF.Copy, scale=cols[:, 2:3]))
                              st0["g_xnf"] = [t8]
                              p.wait("sync", t8)
                              t8s = p.op("sync", lambda e, tc=tc: e.dma_start(
                                  out=q8_scr[tc * 128:(tc + 1) * 128, :], in_=q8x[:]), dma="q8s")
                              st0["g_q8x"] = [t8s]

                          ev = []
                          for d in range(NDC):
                              bank = psb[d % 2]
                              p.waits("tensor", [txb, t_eyeb, st0["bank_tok"][d % 2], st0["g_stg"]])
                              tt = p.op("tensor", lambda e, d=d, b=bank, P=P: e.transpose(
                                  out=b[:, 0:P], in_=xnb[0:P, d * 128:(d + 1) * 128], identity=eyeb[0:P, 0:P]))
                              p.wait("scalar", tt)
                              te = p.op("scalar", lambda e, d=d, b=bank, P=P: e.copy(out=stg[:, d, 0:P], in_=b[:, 0:P]))
                              ev.append(te)
                              st0["bank_tok"][d % 2] = te
                          for d in range(NDC):
                              p.wait("sync", ev[d])
                              cst = 0 if halo else HALO + tc * 128
                              p.op("sync", lambda e, d=d, c=cst, P=P: e.dma_start(
                                  out=xn_scr[d, :, c:c + P], in_=stg[:, d, 0:P]), dma="stg")
                          st0["g_stg"] = p.token("stg")

                      g1 = gw1()
                      for tc in range(NTC if SKIP_HALO else NTC + 1):
                          p0_tile(tc)
                          pull(g1, 8)
                      while pull(g1, 100):
                          pass

              t_p0_stg = p.token("stg")
              qkv_fm = es.enter_context(nc.sbuf_tensor("qkv_fm", [128, 12, T], BF16, side="right"))
              with ExitStack() as e1:
                if PHASE_LIMIT >= 3:
                  sb1 = lambda n, s, d: e1.enter_context(nc.sbuf_tensor(n, s, d))
                  psA = [e1.enter_context(nc.psum_tensor(f"psA{i}", [128, 512], F32)) for i in range(8)]
                  xslab = [sb1(f"xslab{i}", [128, CS], BF16) for i in range(2)]
                  cwf = [sb1(f"cwf{i}", [128, 256], F32) for i in range(2)]
                  cwb = [sb1(f"cwb{i}", [128, 256], BF16) for i in range(2)]

                  stA = {"prev_ev": None, "prev_cw": [None, None], "prev_cwf": [None, None],
                         "prev_x": [None, None]}
                  g2 = gw2() if PHASE_LIMIT >= 1 else None

                  def a1_ccg(ccg):
                      step = 0
                      last_mm = None
                      for d in range(NDC):
                          bx = xslab[d % 2]
                          p.waits("sync", [stA["prev_x"][d % 2], t_p0_stg])
                          tlx = p.op("sync", lambda e, b=bx, d=d: e.dma_start(out=b[:], in_=xn_scr[d, :, :]), dma=f"xs{d % 2}")
                          for w in range(3):
                              bcf = cwf[step % 2]
                              bcb = cwb[step % 2]
                              p.wait("sync", stA["prev_cwf"][step % 2])
                              tlw = p.op("sync", lambda e, b=bcf, w=w, d=d, g=ccg: e.dma_start(
                                  out=b[:], in_=h["cw"][w, d * 128:(d + 1) * 128, g * 256:(g + 1) * 256]), dma=f"cw{step % 2}")
                              p.waits("scalar", [tlw, stA["prev_cw"][step % 2]])
                              tcw = p.op("scalar", lambda e, bf=bcf, bb=bcb: e.copy(out=bb[:], in_=bf[:]))
                              stA["prev_cwf"][step % 2] = tcw
                              p.waits("tensor", [tcw, tlx, stA["prev_ev"]])
                              first = (d == 0 and w == 0)
                              lastg = (d == NDC - 1 and w == 2)
                              for cc2 in range(2):
                                  for tau in range(NTAU):
                                      bk = psA[cc2 * 4 + tau]
                                      isl = (cc2 == 1 and tau == NTAU - 1)
                                      tok = p.op("tensor", lambda e, bk=bk, bb=bcb, bx=bx, cc2=cc2, tau=tau, w=w, f=first, L=lastg:
                                                 e.matmul(out=bk[:], lhsT=bb[:, cc2 * 128:(cc2 + 1) * 128],
                                                          rhs=bx[:, HALO + tau * 512 + (w - 2): HALO + tau * 512 + (w - 2) + 512],
                                                          start=f, stop=L),
                                                 inc=(1 if isl else 0))
                                      if isl:
                                          last_mm = tok
                              stA["prev_cw"][step % 2] = last_mm
                              step += 1
                          stA["prev_x"][d % 2] = last_mm
                          if g2 is not None:
                              pull(g2, 1)
                      p.waits("scalar", [last_mm, tl["cb2"]])
                      for cc2 in range(2):
                          for tau in range(NTAU):
                              bk = psA[cc2 * 4 + tau]
                              cc = ccg * 2 + cc2
                              p.op("scalar", lambda e, bk=bk, cc=cc, tau=tau: e.activation(
                                  out=qkv_fm[:, cc, tau * 512:(tau + 1) * 512], in_=bk[:],
                                  func=AF.Identity, bias=cb2[:, cc:cc + 1]))
                      stA["prev_ev"] = p.token("scalar")

                  for ccg in range(6):
                      a1_ccg(ccg)
                  if g2 is not None:
                      while pull(g2, 100):
                          pass

        p.barrier()

        qx_fm = es.enter_context(nc.sbuf_tensor("qx_fm", [128, NDC, T], BF16, side="right"))
        with ExitStack() as eq:
          if PHASE_LIMIT >= 2:
              sbq = lambda n, s, d: eq.enter_context(nc.sbuf_tensor(n, s, d))
              psq = [eq.enter_context(nc.psum_tensor(f"psq{i}", [128, 512], BF16)) for i in range(2)]
              q8i = [sbq(f"q8i{i}", [128, D], FP8) for i in range(2)]
              qxb = sbq("qxb", [128, D], BF16)
              prev_q8i = [None, None]
              g_qxb = None
              bank_tok = [None, None]
              for tc in range(NTC):
                  buf = q8i[tc % 2]
                  p.wait("sync", prev_q8i[tc % 2])
                  tld = p.op("sync", lambda e, b=buf, tc=tc: e.dma_start(
                      out=b[:], in_=q8_scr[tc * 128:(tc + 1) * 128, :]), dma=f"qx{tc % 2}")
                  p.waits("vector", [tld, g_qxb])
                  tqx = p.op("vector", lambda e, b=buf, tc=tc: e.tensor_scalar_mul(
                      out=qxb[:], in0=b[:], scalar1=sx_all[:, tc:tc + 1]))
                  prev_q8i[tc % 2] = tqx
                  ev2 = []
                  for d in range(NDC):
                      bank = psq[d % 2]
                      p.waits("tensor", [tqx, bank_tok[d % 2]])
                      tt = p.op("tensor", lambda e, d=d, b=bank: e.transpose(
                          out=b[:, 0:128], in_=qxb[:, d * 128:(d + 1) * 128], identity=eyeb[:]))
                      if d == NDC - 1:
                          g_qxb = tt
                      p.wait("vector", tt)
                      te = p.op("vector", lambda e, d=d, tc=tc, b=bank: e.tensor_copy(
                          out=qx_fm[:, d, tc * 128:(tc + 1) * 128], in_=b[:, 0:128]))
                      ev2.append(te)
                      bank_tok[d % 2] = te

        p.barrier()

        with ExitStack() as e2:
          if PHASE_LIMIT >= 4:
              sb2 = lambda n, s, d: e2.enter_context(nc.sbuf_tensor(n, s, d))
              psf = [e2.enter_context(nc.psum_tensor(f"psf{i}", [128, 512], F32)) for i in range(6)]
              psT = [e2.enter_context(nc.psum_tensor(f"psT{i}", [128, 512], BF16)) for i in range(2)]
              wqb = sb2("wqb", [128, 4, 512], BF16)
              wkb = sb2("wkb", [128, 4, 512], BF16)
              wvb = sb2("wvb", [128, 4, 512], BF16)
              wob = sb2("wob", [128, 4, 512], BF16)
              pwb = sb2("pwb", [128, 4, D], BF16)
              pbb = sb2("pbb", [128, 4, 512], F32)
              wtmp = sb2("wtmp", [128, D], F32)
              pbr_sb = sb2("pbr_sb", [1, D], F32)
              tl["pbr"] = p.op("sync", lambda e: e.dma_start(out=pbr_sb[:], in_=h["pbr"][:]), dma="ld_pbr")
              qh = sb2("qh", [128, 4, 512], BF16)
              kh = sb2("kh", [128, 4, 512], BF16)
              vh = sb2("vh", [128, 4, 512], BF16)
              probs = sb2("probs", [128, 2, 256], BF16)
              probsn = sb2("probsn", [128, 2, 256], BF16)
              pT = sb2("pT", [128, 2, 3, 128], BF16)
              ctxf = sb2("ctxf", [128, 4, 256], BF16)
              mha = sb2("mha", [128, 4, 256], BF16)
              sstg = sb2("sstg", [128, 4, 512], BF16)
              scols = sb2("scols", [128, 2, 2], F32)

              t_wcast = None
              for nm, dst in (("wq2", wqb), ("wk2", wkb), ("wv2", wvb), ("wo2", wob)):
                  for c in range(4):
                      p.wait("sync", t_wcast)
                      tld = p.op("sync", lambda e, a=h[nm], c=c: e.dma_start(
                          out=wtmp[:, 0:512], in_=a[c * 128:(c + 1) * 128, :]), dma="wtm")
                      p.waits("scalar", [tld, t_wcast])
                      t_wcast = p.op("scalar", lambda e, d=dst, c=c: e.copy(out=d[:, c, :], in_=wtmp[:, 0:512]))
              for c in range(4):
                  p.wait("sync", t_wcast)
                  tld = p.op("sync", lambda e, c=c: e.dma_start(
                      out=wtmp[:], in_=h["pw"][c * 128:(c + 1) * 128, :]), dma="wtm")
                  p.waits("scalar", [tld, t_wcast])
                  t_wcast = p.op("scalar", lambda e, c=c: e.copy(out=pwb[:, c, :], in_=wtmp[:]))

              t_pb = None
              for dg in range(4):
                  p.waits("tensor", [tl["onesf"], tl["pbr"], t_pb])
                  tb = p.op("tensor", lambda e, dg=dg: e.matmul(
                      out=psf[5][:], lhsT=onesf[:], rhs=pbr_sb[:, dg * 512:(dg + 1) * 512],
                      start=True, stop=True))
                  p.wait("vector", tb)
                  t_pb = p.op("vector", lambda e, dg=dg: e.tensor_copy(out=pbb[:, dg, :], in_=psf[5][:]))
                  if FENCE_MODE in ("recip", "small"): p.fence("vector")
                  p.wait("tensor", t_pb)

              bank_free = {i: None for i in range(6)}
              bankT_free = {i: None for i in range(2)}
              g_qhkh = None
              g_vh = None
              g_probs = [None, None]
              g_probsn = [None, None]
              g_pT = [None, None]
              g_ctx = None
              g_mha = None
              g_sstg = None

              for tau in range(NTAU):
                  for (mat, bias, outb, scl, qr) in ((wqb, bq8, qh, 0.125, 0), (wkb, bk2, kh, 1.0, 1)):
                      for c in range(4):
                          bk = psf[c % 2]
                          p.waits("tensor", [t_wcast, bank_free[c % 2]])
                          for kc in range(4):
                              p.op("tensor", lambda e, bk=bk, m=mat, c=c, kc=kc, tau=tau, qr=qr:
                                   e.matmul(out=bk[:], lhsT=m[:, kc, c * 128:(c + 1) * 128],
                                            rhs=qkv_fm[:, qr * 4 + kc, tau * 512:(tau + 1) * 512],
                                            start=(kc == 0), stop=(kc == 3)),
                                   inc=(1 if kc == 3 else 0))
                          tmm = p.token("tensor")
                          p.waits("scalar", [tmm, t_bq8, g_qhkh])
                          tev = p.op("scalar", lambda e, bk=bk, o=outb, c=c, b=bias, s=scl: e.activation(
                              out=o[:, c, :], in_=bk[:], func=AF.Identity, bias=b[:, c:c + 1], scale=s))
                          bank_free[c % 2] = tev
                  for tc2 in range(4):
                      bk = psf[2 + tc2 % 2]
                      p.waits("tensor", [t_wcast, bank_free[2 + tc2 % 2]])
                      for kc in range(4):
                          p.op("tensor", lambda e, bk=bk, tc2=tc2, kc=kc, tau=tau:
                               e.matmul(out=bk[:], lhsT=qkv_fm[:, 8 + kc, tau * 512 + tc2 * 128: tau * 512 + (tc2 + 1) * 128],
                                        rhs=wvb[:, kc, :], start=(kc == 0), stop=(kc == 3)),
                               inc=(1 if kc == 3 else 0))
                      tmm = p.token("tensor")
                      p.waits("scalar", [tmm, g_vh])
                      tev = p.op("scalar", lambda e, bk=bk, tc2=tc2: e.copy(out=vh[:, tc2, :], in_=bk[:]))
                      bank_free[2 + tc2 % 2] = tev

                  for b2 in range(2):
                      blk = tau * 2 + b2
                      for hh in range(8):
                          hp, eo = hh // 2, hh % 2
                          hd0 = 64 * eo
                          par = hh % 2
                          for qc in range(2):
                              bk = psf[qc]
                              p.waits("tensor", [bank_free[qc], tl["cmask"]])
                              tmm = p.op("tensor", lambda e, bk=bk, hp=hp, hd0=hd0, qc=qc, b2=b2:
                                         e.matmul(out=bk[:, 0:256],
                                                  lhsT=qh[hd0:hd0 + 64, hp, b2 * 256 + qc * 128: b2 * 256 + (qc + 1) * 128],
                                                  rhs=kh[hd0:hd0 + 64, hp, b2 * 256:(b2 + 1) * 256],
                                                  start=True, stop=True))
                              g_qhkh = tmm
                              p.wait("vector", tmm)
                              p.op("vector", lambda e, bk=bk, qc=qc: e.tensor_tensor(
                                  out=bk[:, 0:256], in0=bk[:, 0:256], in1=cmask[:, qc, :], op=ALU.add))
                              tmx = p.op("vector", lambda e, bk=bk, par=par: e.reduce_max(
                                  out=scols[:, par, 0:1], in_=bk[:, 0:256], axis=AX.X, negate=True))
                              p.fence("vector")
                              p.waits("scalar", [tmx] + ([g_probs[par]] if g_probs[par] else []))
                              tex = p.op("scalar", lambda e, bk=bk, par=par: e.activation(
                                  out=probs[:, par, :], in_=bk[:, 0:256], func=AF.Exp,
                                  bias=scols[:, par, 0:1], accum_out=scols[:, par, 1:2]))
                              bank_free[qc] = tex
                              p.waits("vector", [tex, g_probsn[par]])
                              p.op("vector", lambda e, par=par: e.reciprocal(out=scols[:, par, 1:2], in_=scols[:, par, 1:2]))
                              p.fence("vector")
                              tno = p.op("vector", lambda e, par=par: e.tensor_scalar_mul(
                                  out=probsn[:, par, :], in0=probs[:, par, :], scalar1=scols[:, par, 1:2]))
                              g_probs[par] = tno
                              for sc in range(qc + 1):
                                  bT = psT[sc % 2]
                                  p.waits("tensor", [tno, bankT_free[sc % 2], g_pT[par]])
                                  ttr = p.op("tensor", lambda e, bT=bT, sc=sc, par=par: e.transpose(
                                      out=bT[:, 0:128], in_=probsn[:, par, sc * 128:(sc + 1) * 128], identity=eyeb[:]))
                                  g_probsn[par] = ttr
                                  p.wait("vector", ttr)
                                  tevT = p.op("vector", lambda e, bT=bT, qc=qc, sc=sc, par=par: e.tensor_copy(
                                      out=pT[:, par, qc + sc, :], in_=bT[:, 0:128]))
                                  bankT_free[sc % 2] = tevT
                          tpT = p.token("vector")
                          cbi = 2 + hh % 2
                          cb = psf[cbi]
                          p.waits("tensor", [tpT, bank_free[cbi]])
                          p.op("tensor", lambda e, cb=cb, b2=b2, hh=hh, par=par:
                               e.matmul(out=cb[0:64, 0:128],
                                        lhsT=vh[:, b2 * 2, hh * 64:hh * 64 + 64],
                                        rhs=pT[:, par, 0, :], start=True, stop=True), inc=0)
                          p.op("tensor", lambda e, cb=cb, b2=b2, hh=hh, par=par:
                               e.matmul(out=cb[0:64, 128:256],
                                        lhsT=vh[:, b2 * 2, hh * 64:hh * 64 + 64],
                                        rhs=pT[:, par, 1, :], start=True, stop=False), inc=0)
                          tcx = p.op("tensor", lambda e, cb=cb, b2=b2, hh=hh, par=par:
                                     e.matmul(out=cb[0:64, 128:256],
                                              lhsT=vh[:, b2 * 2 + 1, hh * 64:hh * 64 + 64],
                                              rhs=pT[:, par, 2, :], start=False, stop=True), inc=1)
                          g_pT[par] = tcx
                          g_vh = tcx
                          p.waits("scalar", [tcx, g_ctx])
                          tevc = p.op("scalar", lambda e, cb=cb, hp=hp, eo=eo: e.activation(
                              out=ctxf[64 * eo:64 * eo + 64, hp, :], in_=cb[0:64, 0:256],
                              func=AF.Identity, bias=bv2[64 * eo:64 * eo + 64, hp:hp + 1]))
                          bank_free[cbi] = tevc
                      t_ctx = p.token("scalar")
                      for oc in range(4):
                          p.waits("tensor", [t_ctx, bank_free[4]])
                          for kc in range(4):
                              p.op("tensor", lambda e, oc=oc, kc=kc:
                                   e.matmul(out=psf[4][:, 0:256], lhsT=wob[:, kc, oc * 128:(oc + 1) * 128],
                                            rhs=ctxf[:, kc, :], start=(kc == 0), stop=(kc == 3)),
                                   inc=(1 if kc == 3 else 0))
                          tmm = p.token("tensor")
                          g_ctx = tmm
                          p.waits("scalar", [tmm, g_mha])
                          tev = p.op("scalar", lambda e, oc=oc: e.activation(
                              out=mha[:, oc, :], in_=psf[4][:, 0:256], func=AF.Identity, bias=bo2[:, oc:oc + 1]))
                          bank_free[4] = tev
                      t_mha = p.token("scalar")
                      for tc2 in range(2):
                          for dg in range(4):
                              bki = 4 + dg % 2
                              p.waits("tensor", [t_mha, bank_free[bki], t_pb])
                              for oc in range(4):
                                  p.op("tensor", lambda e, bki=bki, tc2=tc2, dg=dg, oc=oc:
                                       e.matmul(out=psf[bki][:], lhsT=mha[:, oc, tc2 * 128:(tc2 + 1) * 128],
                                                rhs=pwb[:, oc, dg * 512:(dg + 1) * 512],
                                                start=(oc == 0), stop=(oc == 3)),
                                       inc=(1 if oc == 3 else 0))
                              tmm = p.token("tensor")
                              g_mha = tmm
                              p.waits("vector", [tmm, g_sstg])
                              tev = p.op("vector", lambda e, bki=bki, dg=dg: e.tensor_tensor(
                                  out=sstg[:, dg, :], in0=psf[bki][:], in1=pbb[:, dg, :], op=ALU.add))
                              bank_free[bki] = tev
                          tevl = p.token("vector")
                          row0 = blk * 256 + tc2 * 128
                          p.wait("gpsimd", tevl)
                          g_sstg = p.op("gpsimd", lambda e, r=row0: e.dma_start(
                              out=s_scr[r:r + 128, :], in_=sstg[:].rearrange("p g n -> p (g n)")), dma="sst")

        p.barrier()

        with ExitStack() as e3:
          if PHASE_LIMIT >= 5:
              sb3 = lambda n, s, d: e3.enter_context(nc.sbuf_tensor(n, s, d))
              psB = [e3.enter_context(nc.psum_tensor(f"psB{i}", [128, 512], F32)) for i in range(8)]
              t_buf = sb3("t_buf", [128, NJC, 512], BF16)
              wslab = [sb3(f"wslab{i}", [128, 512], BF16) for i in range(4)]
              wdslab = [sb3(f"wdslab{i}", [128, D], BF16) for i in range(2)]
              sgb = sb3("sgb", [128, 512], BF16)
              uub = sb3("uub", [128, 512], BF16)
              q8b = sb3("q8b", [128, 512], FP8)
              tmx = sb3("tmx", [128, 512], F32)
              tcol = sb3("tcol", [128, 8], F32)
              qrow = sb3("qrow", [1, 512], F32)
              srow = sb3("srow", [1, 512], F32)
              qsbt = sb3("qsbt", [128, 512], F32)
              stbt = sb3("stbt", [128, 512], F32)
              sslab = sb3("sslab", [128, 512], BF16)
              xslab2 = [sb3(f"xslab2{i}", [128, 512], F32) for i in range(2)]
              omb = sb3("omb", [128, 512], F32)
              gyb = sb3("gyb", [128, 512], BF16)

              g_tbuf = None
              prev_wd = [None, None]
              g_sgb = None
              g_sslab = None
              g_x2 = [None, None]
              g_omb = None
              g_gyb = None

              bank_free = {i: None for i in range(8)}
              prev_w = [None] * 4
              for tau in range(NTAU):
                  for G in range(NG):
                      for d in range(NDC):
                          for mi, scr in enumerate((wgq_scr, wuq_scr)):
                              wb = wslab[(d * 2 + mi) % 4]
                              p.wait("sync", prev_w[(d * 2 + mi) % 4])
                              tld = p.op("sync", lambda e, b=wb, s=scr, d=d, G=G: e.dma_start(
                                  out=b[:], in_=s[d * 128:(d + 1) * 128, G * 512:(G + 1) * 512]), dma=f"ws{(d * 2 + mi) % 4}")
                              p.waits("tensor", [tld] + ([bank_free[i] for i in range(8)] + [g_tbuf] if d == 0 else []))
                              for cc in range(4):
                                  bk = psB[mi * 4 + cc]
                                  p.op("tensor", lambda e, bk=bk, b=wb, cc=cc, d=d, tau=tau:
                                       e.matmul(out=bk[:], lhsT=b[:, cc * 128:(cc + 1) * 128],
                                                rhs=qx_fm[:, d, tau * 512:(tau + 1) * 512],
                                                start=(d == 0), stop=(d == NDC - 1)),
                                       inc=(1 if cc == 3 else 0))
                              prev_w[(d * 2 + mi) % 4] = p.token("tensor")
                      t_gu = p.token("tensor")
                      for cc in range(4):
                          c = G * 4 + cc
                          p.waits("scalar", [t_gu, g_sgb])
                          tg = p.op("scalar", lambda e, cc=cc, c=c: e.activation(
                              out=sgb[:], in_=psB[cc][:], func=AF.Silu, scale=wsg[:, c:c + 1]))
                          tu = p.op("scalar", lambda e, cc=cc, c=c: e.activation(
                              out=uub[:], in_=psB[4 + cc][:], func=AF.Copy, scale=wsu[:, c:c + 1]))
                          p.waits("vector", [tu, g_tbuf])
                          tt2 = p.op("vector", lambda e, c=c: e.tensor_tensor(
                              out=t_buf[:, c, :], in0=sgb[:], in1=uub[:], op=ALU.mult))
                          g_sgb = tt2
                          bank_free[cc] = tt2
                          bank_free[4 + cc] = tt2
                  g_tbuf = None

                  p.op("vector", lambda e: e.tensor_reduce(
                      out=tmx[:], in_=t_buf[:].rearrange("p c s -> p s c"),
                      axis=AX.X, op=ALU.max, apply_absolute_value=True))
                  if FENCE_MODE in ("recip", "small"): p.fence("vector")
                  t_amax = p.token("vector")
                  for q4 in range(4):
                      bi = 4 + q4 % 2
                      p.waits("tensor", [t_amax, bank_free[bi], tl["eyef"]])
                      tt = p.op("tensor", lambda e, q4=q4, bi=bi: e.transpose(
                          out=psB[bi][:, 0:128], in_=tmx[:, q4 * 128:(q4 + 1) * 128], identity=eyef[:]))
                      p.wait("vector", tt)
                      p.op("vector", lambda e, q4=q4, bi=bi: e.reduce_max(
                          out=tcol[:, q4:q4 + 1], in_=psB[bi][:, 0:128], axis=AX.X))
                      bank_free[bi] = p.token("vector")
                  p.fence("vector")
                  p.op("vector", lambda e: e.reciprocal(out=tcol[:, 4:8], in_=tcol[:, 0:4]))
                  p.fence("vector")
                  p.op("vector", lambda e: e.tensor_scalar_mul(out=tcol[:, 4:8], in0=tcol[:, 4:8], scalar1=224.0))
                  p.fence("vector")
                  p.op("vector", lambda e: e.tensor_scalar_mul(out=tcol[:, 0:4], in0=tcol[:, 0:4], scalar1=1.0 / 224.0))
                  p.fence("vector")
                  tvq = p.token("vector")
                  bank_cp = [None, None]
                  p.waits("tensor", [tvq])
                  for c in range(8):
                      bi = 4 + c % 2
                      srccol = 4 + c if c < 4 else c - 4
                      dstrow, dc = (qrow, c) if c < 4 else (srow, c - 4)
                      p.waits("tensor", [bank_cp[c % 2], bank_free[bi]])
                      tt = p.op("tensor", lambda e, sc=srccol, bi=bi: e.transpose(
                          out=psB[bi][0:1, 0:128], in_=tcol[:, sc:sc + 1], identity=eyef[:]))
                      p.wait("vector", tt)
                      p.op("vector", lambda e, q=dstrow, dc=dc, bi=bi: e.tensor_copy(
                          out=q[:, dc * 128:(dc + 1) * 128], in_=psB[bi][0:1, 0:128]))
                      bank_cp[c % 2] = p.token("vector")
                      bank_free[bi] = bank_cp[c % 2]
                  p.fence("vector")
                  t_rows = p.token("vector")
                  for (row, dst) in ((qrow, qsbt), (srow, stbt)):
                      p.waits("tensor", [t_rows, tl["onesf"], bank_free[0]])
                      tb = p.op("tensor", lambda e, r=row: e.matmul(
                          out=psB[0][:], lhsT=onesf[:], rhs=r[:], start=True, stop=True))
                      p.wait("vector", tb)
                      tcp = p.op("vector", lambda e, ds=dst: e.tensor_copy(out=ds[:], in_=psB[0][:]))
                      if FENCE_MODE in ("recip", "small"): p.fence("vector")
                      bank_free[0] = tcp
                      p.wait("tensor", tcp)
                  for c in range(NJC):
                      p.op("vector", lambda e, c=c: e.tensor_tensor(
                          out=q8b[:], in0=t_buf[:, c, :], in1=qsbt[:], op=ALU.mult))
                      p.op("vector", lambda e, c=c: e.tensor_tensor(
                          out=t_buf[:, c, :], in0=q8b[:], in1=stbt[:], op=ALU.mult))
                  t_qt = p.token("vector")

                  for hgq in range(4):
                      for j in range(NJC):
                          wb = wdslab[j % 2]
                          p.wait("sync", prev_wd[j % 2])
                          tld = p.op("sync", lambda e, b=wb, j=j, hgq=hgq: e.dma_start(
                              out=b[:, 0:512], in_=wdq_scr[j * 128:(j + 1) * 128, hgq * 512:(hgq + 1) * 512]), dma=f"wdl{j % 2}")
                          p.waits("tensor", [tld, t_qt] + ([bank_free[i] for i in range(8)] if j == 0 else []))
                          for tc2 in range(4):
                              bk = psB[tc2]
                              p.op("tensor", lambda e, bk=bk, b=wb, j=j, tc2=tc2:
                                   e.matmul(out=bk[:],
                                            lhsT=t_buf[:, j, tc2 * 128:(tc2 + 1) * 128],
                                            rhs=b[:, 0:512],
                                            start=(j == 0), stop=(j == NJC - 1)),
                                   inc=(1 if tc2 == 3 else 0))
                          prev_wd[j % 2] = p.token("tensor")
                      t_dn = p.token("tensor")
                      g_tbuf = t_dn
                      for tc2 in range(4):
                          row0 = (tau * 4 + tc2) * 128
                          bk = psB[tc2]
                          p.waits("scalar", [g_x2[hgq % 2]])
                          tlx = p.op("scalar", lambda e, b=xslab2[hgq % 2], r=row0, hgq=hgq: e.dma_start(
                              out=b[:], in_=h["x"][r:r + 128, hgq * 512:(hgq + 1) * 512]), dma=f"x2{hgq % 2}")
                          p.waits("scalar", [g_sslab])
                          tls = p.op("scalar", lambda e, r=row0, hgq=hgq: e.dma_start(
                              out=sslab[:], in_=s_scr[r:r + 128, hgq * 512:(hgq + 1) * 512]), dma="ssl")
                          p.waits("scalar", [t_dn, g_gyb])
                          tg = p.op("scalar", lambda e, bk=bk: e.activation(
                              out=gyb[:], in_=bk[:], func=AF.Silu))
                          bank_free[tc2] = tg
                          p.waits("vector", [tg, tls, tlx, g_omb])
                          p.op("vector", lambda e: e.tensor_tensor(out=omb[:], in0=sslab[:], in1=gyb[:], op=ALU.mult))
                          g_sslab = p.token("vector")
                          g_gyb = g_sslab
                          tcb = p.op("vector", lambda e, b=xslab2[hgq % 2]: e.tensor_tensor(
                              out=omb[:], in0=omb[:], in1=b[:], op=ALU.add))
                          g_x2[hgq % 2] = tcb
                          p.wait("gpsimd", tcb)
                          g_omb = p.op("gpsimd", lambda e, r=row0, hgq=hgq: e.dma_start(
                              out=h["out"][r:r + 128, hgq * 512:(hgq + 1) * 512], in_=omb[:]), dma="out")


        if DEBUG_DUMP:
            p.barrier()
            if PHASE_LIMIT >= 1:
                p.op("sync", lambda e: e.dma_start(out=h["dbg_wg"], in_=wgq_scr[:, :]), dma="out")
                p.op("sync", lambda e: e.dma_start(out=h["dbg_wd"], in_=wdq_scr[:, :]), dma="out")
            if PHASE_LIMIT >= 2:
                p.op("sync", lambda e: e.dma_start(out=h["dbg_qx"], in_=qx_fm[:]), dma="out")
            if PHASE_LIMIT >= 4:
                p.op("sync", lambda e: e.dma_start(out=h["dbg_s"], in_=s_scr[:, :]), dma="out")
            p.items["sync"].append(("wait", "out", p.count["out"]))
        if "out" in p.count:
            p.items["gpsimd"].append(("wait", "out", p.count["out"]))
        else:
            for cname in list(p.chans):
                p.items["gpsimd"].append(("wait", cname, p.count[cname]))
        finish()
        with nc.Block() as block:
            p.emit(block, sems)
    return nc


_NC_CACHE = None


def _get_program():
    global _NC_CACHE
    if _NC_CACHE is None:
        _NC_CACHE = build_program()
    return _NC_CACHE


def kernel(**inputs):
    f32 = np.float32
    x = np.asarray(inputs["L0_attn_x"], f32)
    cw = np.ascontiguousarray(np.asarray(inputs["L0_attn_conv_w"], f32))
    cb2 = np.ascontiguousarray(np.asarray(inputs["L0_attn_conv_b"], f32).reshape(12, 128).T)
    wq2 = np.ascontiguousarray(np.asarray(inputs["L0_attn_wq"], f32).reshape(512, 512))
    wk2 = np.ascontiguousarray(np.asarray(inputs["L0_attn_wk"], f32).reshape(512, 512))
    wv2 = np.ascontiguousarray(np.asarray(inputs["L0_attn_wv"], f32).reshape(512, 512))
    wo2 = np.ascontiguousarray(np.asarray(inputs["L0_attn_wo"], f32).reshape(512, 512))
    bq2 = np.ascontiguousarray(np.asarray(inputs["L0_attn_bq"], f32).reshape(4, 128).T)
    bk2 = np.ascontiguousarray(np.asarray(inputs["L0_attn_bk"], f32).reshape(4, 128).T)
    bv2 = np.ascontiguousarray(np.asarray(inputs["L0_attn_bv"], f32).reshape(4, 128).T)
    bo2 = np.ascontiguousarray(np.asarray(inputs["L0_attn_bo"], f32).reshape(4, 128).T)
    pw = np.ascontiguousarray(np.asarray(inputs["L0_attn_proj_w"], f32))
    pbr = np.ascontiguousarray(np.asarray(inputs["L0_attn_proj_b"], f32).reshape(1, D))
    wg = np.ascontiguousarray(np.asarray(inputs["L0_ffn_w_gate"], f32))
    wu = np.ascontiguousarray(np.asarray(inputs["L0_ffn_w_up"], f32))
    wd = np.ascontiguousarray(np.asarray(inputs["L0_ffn_w_down"], f32))
    wgT = np.ascontiguousarray(wg.T)
    wuT = np.ascontiguousarray(wu.T)
    wdT = np.ascontiguousarray(wd.T)
    didx = np.ascontiguousarray(np.asarray(inputs["L0_ffn_down_reorder_index"], np.int32))
    eyef = np.eye(128, dtype=f32)
    onesf = np.ones((1, 128), f32)
    jj = np.arange(256)[None, :]
    pp = np.arange(128)[:, None]
    cmask = np.ascontiguousarray(np.stack([
        np.where(jj <= pp, 0.0, -1e9).astype(f32),
        np.where(jj <= 128 + pp, 0.0, -1e9).astype(f32)], axis=1))

    shared = dict(cw=cw, cb2=cb2, wq2=wq2, wk2=wk2, wv2=wv2, wo2=wo2,
                  bq2=bq2, bk2=bk2, bv2=bv2, bo2=bo2, pw=pw, pbr=pbr,
                  wg=wg, wu=wu, wd=wd, wgT=wgT, wuT=wuT, wdT=wdT,
                  didx=didx, eyef=eyef, onesf=onesf, cmask=cmask)
    in_maps = []
    for c in range(NCORES):
        seq, half = c // 2, c % 2
        t0 = half * T
        m = dict(shared)
        m["x"] = np.ascontiguousarray(x[seq, t0:t0 + T])
        m["xh"] = (np.zeros((HALO, D), f32) if half == 0
                   else np.ascontiguousarray(x[seq, t0 - HALO:t0]))
        in_maps.append(m)

    nc = _get_program()
    res = run_bass_kernel_spmd(nc, in_maps, core_ids=list(range(NCORES)))
    out = np.empty((4, 4096, D), f32)
    for c in range(NCORES):
        seq, half = c // 2, c % 2
        out[seq, half * T:(half + 1) * T] = np.asarray(res.results[c]["out"])
    return out


if __name__ == "__main__":
    build_program()
    print("built ok")
